# version 06
# baseline (speedup 1.0000x reference)
import numpy as np
import ml_dtypes
import bass_rust
import concourse.bass as bass
import concourse.mybir as mybir
from concourse.bass_utils import run_bass_kernel_spmd
from contextlib import ExitStack

F32 = mybir.dt.float32
BF16 = mybir.dt.bfloat16
I32 = mybir.dt.int32
AF = mybir.ActivationFunctionType
ALU = mybir.AluOpType
AX = mybir.AxisListType
bfnp = ml_dtypes.bfloat16

SEQ = 4096
DM = 4096
NDT = DM // 128
NCH = 8
CHT = SEQ // NCH
NCT = 12
TOK = 1024
ATT_SCALE = float(128.0 ** -0.5)
NEGC = -12.0
EPS = 1e-5
BIG = 1e30
ENGINES = ("tensor", "vector", "scalar", "sync", "gpsimd")

DEBUG_PHASE = None


class Sem:
    def __init__(self, name):
        self.name = name
        self.h = None
        self.n = 0


class Pool:
    def __init__(self, es, nc, name, n):
        self.sems = []
        for i in range(n):
            s = Sem(f"{name}{i}")
            s.h = es.enter_context(nc.semaphore(s.name))
            self.sems.append(s)

    def inc(self, i):
        return [(self.sems[i], 16)]

    def at(self, i):
        return (self.sems[i], self.sems[i].n)

    def totals(self):
        return [(s, s.n) for s in self.sems]


class Plan:
    def __init__(self):
        self.ops = {e: [] for e in ENGINES}

    def add(self, eng, fn=None, incs=(), waits=()):
        waits = tuple((s, v) for (s, v) in waits if v > 0)
        incs = tuple(incs)
        for s, k in incs:
            s.n += k
        self.ops[eng].append((waits, fn, incs))


def build_program(debug=None):
    nc = bass.Bass("TRN2", target_bir_lowering=False, debug=False)

    xt_d = nc.dram_tensor("xt", [DM, SEQ], F32, kind="ExternalInput").ap()
    xtc_d = nc.dram_tensor("xtc", [DM, TOK], F32, kind="ExternalInput").ap()
    wqkv_d = nc.dram_tensor("wqkv", [NCT, 128, NDT, 128], F32, kind="ExternalInput").ap()
    wo_d = nc.dram_tensor("wot", [NDT, 128, 32, 128], F32, kind="ExternalInput").ap()
    awr_d = nc.dram_tensor("awr", [128, NDT, 68], F32, kind="ExternalInput").ap()
    btil_d = nc.dram_tensor("btil", [64, DM], F32, kind="ExternalInput").ap()
    masks_d = nc.dram_tensor("masks", [128, 4, 512], BF16, kind="ExternalInput").ap()
    identb_d = nc.dram_tensor("identb", [128, 128], BF16, kind="ExternalInput").ap()
    identf_d = nc.dram_tensor("identf", [128, 128], F32, kind="ExternalInput").ap()
    pmat_d = nc.dram_tensor("pmat", [4, 64], BF16, kind="ExternalInput").ap()
    gidx_d = nc.dram_tensor("gidx", [128, 32], I32, kind="ExternalInput").ap()
    yt_d = nc.dram_tensor("yt", [DM, TOK], F32, kind="ExternalOutput").ap()

    wbf_d = nc.dram_tensor("wbf", [NCT, 128, NDT, 128], BF16, kind="Internal").ap()
    qscr_k = "ExternalOutput" if debug == "A" else "Internal"
    qscr_d = nc.dram_tensor("qscr", [8, 128, SEQ], BF16, kind=qscr_k).ap()
    ccin_k = "ExternalOutput" if debug == "B" else "Internal"
    ccin_d = nc.dram_tensor("ccin", [SEQ, 1024], BF16, kind=ccin_k).ap()
    ccout_d = nc.dram_tensor("ccout", [4 * SEQ, 1024], BF16, kind="Internal").ap()
    ccwin_k = "ExternalOutput" if debug == "G" else "Internal"
    ccwin_d = nc.dram_tensor("ccwin", [4 * TOK, 1024], BF16, kind=ccwin_k).ap()
    dbg = {}
    if debug == "A":
        dbg["dkv"] = nc.dram_tensor("dkv", [128, 4 * SEQ], BF16, kind="ExternalOutput").ap()
    if debug in ("A", "C"):
        dbg["drow"] = nc.dram_tensor("drow", [4, 1024], F32, kind="ExternalOutput").ap()
    if debug in ("C", "G"):
        dbg["dy"] = nc.dram_tensor("dy", [128, 32 * 1024], BF16, kind="ExternalOutput").ap()
    if debug == "D":
        dbg["dml"] = nc.dram_tensor("dml", [68, 1024], F32, kind="ExternalOutput").ap()
        dbg["dgt"] = nc.dram_tensor("dgt", [4, 1024], BF16, kind="ExternalOutput").ap()
        dbg["dgr"] = nc.dram_tensor("dgr", [64, 1024], BF16, kind="ExternalOutput").ap()
        dbg["dac"] = nc.dram_tensor("dac", [68, 1], F32, kind="ExternalOutput").ap()
        dbg["dr2"] = nc.dram_tensor("dr2", [68, 2048], BF16, kind="ExternalOutput").ap()
        dbg["dgw"] = nc.dram_tensor("dgw", [128, 8], F32, kind="ExternalOutput").ap()
        dbg["dlg"] = nc.dram_tensor("dlg", [128, 4], F32, kind="ExternalOutput").ap()
        dbg["dg0"] = nc.dram_tensor("dg0", [128, 4], F32, kind="ExternalOutput").ap()
        dbg["dg1"] = nc.dram_tensor("dg1", [128, 4], F32, kind="ExternalOutput").ap()

    plan = Plan()
    es = ExitStack()

    def sb(name, shape, dt):
        return es.enter_context(nc.sbuf_tensor(name, shape, dt))

    sems = [Sem(n) for n in ("te", "ve", "ac", "ccs", "cst", "cstm", "mks", "vpm",
                             "cdw", "wbs", "cwp", "otl", "dbs")]
    te, ve, ac, ccs, cst, cstm, mks, vpm, cdw, wbs, cwp, otl, dbs = sems
    for s in sems:
        s.h = es.enter_context(nc.semaphore(s.name))
    xbp = Pool(es, nc, "xbp", 16)
    wlp = Pool(es, nc, "wlp", 2)
    qsp = Pool(es, nc, "qsp", 2)
    qtp = Pool(es, nc, "qtp", 2)
    oqp = Pool(es, nc, "oqp", 4)
    stp = Pool(es, nc, "stp", 2)
    wop = Pool(es, nc, "wop", 2)
    xcp = Pool(es, nc, "xcp", 1)
    ofp = Pool(es, nc, "ofp", 2)
    cws = Pool(es, nc, "cws", 2)

    ones_col = sb("ones_col", [128, 1], BF16)
    ones_row = sb("ones_row", [1, 128], BF16)
    negc_sb = sb("negc", [128, 1], F32)
    eps1 = sb("eps1", [1, 1], F32)
    r_sum = sb("r_sum", [1, 1024], F32)
    r_sq = sb("r_sq", [1, 1024], F32)
    r_ln = sb("r_ln", [1, 1024], F32)
    r_mr = r_ln
    rstd_bf = sb("rstd_bf", [1, 1024], BF16)
    mr_bf = sb("mr_bf", [1, 1024], BF16)
    wcs_sb = sb("wcs", [128, 12], F32)
    identb = sb("identb_sb", [128, 128], BF16)
    identf = sb("identf_sb", [128, 128], F32)
    pmat_sb = sb("pmat_sb", [4, 64], BF16)
    gidx_sb = sb("gidx_sb", [128, 32], I32)
    wosl = [sb(f"wosl{i}", [128, 32 * 128], BF16) for i in range(2)]

    plan.add("gpsimd", lambda g: g.memset(ones_col[:], 1.0))
    plan.add("gpsimd", lambda g: g.memset(ones_row[:], 1.0))
    plan.add("gpsimd", lambda g: g.memset(negc_sb[:], NEGC))
    plan.add("gpsimd", lambda g: g.memset(eps1[:], EPS), incs=[(cstm, 1)])
    plan.add("gpsimd", lambda g: g.dma_start(out=identb[:], in_=identb_d), incs=[(cst, 16)])
    plan.add("gpsimd", lambda g: g.dma_start(out=identf[:], in_=identf_d), incs=[(cst, 16)])
    plan.add("gpsimd", lambda g: g.dma_start(out=pmat_sb[:], in_=pmat_d), incs=[(cst, 16)])
    plan.add("gpsimd", lambda g: g.dma_start(out=gidx_sb[:], in_=gidx_d), incs=[(cst, 16)])
    const_ld = cst.n
    memset_done = cstm.n

    esAB = ExitStack()
    kt_sb = esAB.enter_context(nc.sbuf_tensor("kt_sb", [128, 2 * SEQ], BF16))
    vt_sb = esAB.enter_context(nc.sbuf_tensor("vt_sb", [128, 2 * SEQ], BF16))

    esA = ExitStack()
    xbf = [esA.enter_context(nc.sbuf_tensor(f"xbf{i}", [128, NDT * CHT], BF16)) for i in range(2)]
    wsl = [esA.enter_context(nc.sbuf_tensor(f"wsl{i}", [128, NDT * 128], BF16)) for i in range(2)]
    xsq = [esA.enter_context(nc.sbuf_tensor(f"xsq{i}", [128, CHT], BF16)) for i in range(4)]
    qstage = [esA.enter_context(nc.sbuf_tensor(f"qstage{i}", [128, CHT], BF16)) for i in range(2)]
    tmpe = [esA.enter_context(nc.sbuf_tensor(f"tmpe{i}", [128, CHT], BF16)) for i in range(2)]
    rrep = esA.enter_context(nc.sbuf_tensor("rrep", [128, CHT], BF16))
    mrrep = esA.enter_context(nc.sbuf_tensor("mrrep", [128, CHT], BF16))

    psA = ExitStack()
    ps_ssum = psA.enter_context(nc.psum_tensor("ps_ssum", [128, 512], F32))
    ps_ssq = psA.enter_context(nc.psum_tensor("ps_ssq", [128, 512], F32))
    ps_bcr = psA.enter_context(nc.psum_tensor("ps_bcr", [128, 512], F32))
    ps_bcm = psA.enter_context(nc.psum_tensor("ps_bcm", [128, 512], F32))
    ps_qkv = [psA.enter_context(nc.psum_tensor(f"ps_qkv{i}", [128, 512], F32)) for i in range(2)]
    ps_wcs = [psA.enter_context(nc.psum_tensor(f"ps_wcs{i}", [128, 1], F32)) for i in range(2)]

    st = dict(xb_ld=[None] * 8, xsq_free=[0, 0, 0, 0], slab_free=[0, 0], wsl_free=[0, 0],
              qkv_evac_ve=[0, 0], stat_rows_ve=0, bc_evac_ve=0,
              wcs_ve=[0] * 12, statmm_te=0)

    def unit_tasks(ch):
        for dt in range(NDT):
            if dt % 4 == 0:
                yield ("load", ch, dt)
            yield ("proc", ch, dt)

    def do_task(task):
        kind, ch, dt = task
        s = ch % 2
        if kind == "load":
            dg = dt // 4
            slot = (ch % 2) * 8 + dg
            src = xt_d.rearrange("(a p) t -> p a t", p=128)[
                :, dg * 4:(dg + 1) * 4, ch * CHT:(ch + 1) * CHT]
            w = []
            if dg == 0:
                w = [(te, st["slab_free"][s])]
            plan.add("gpsimd",
                     lambda g, s=s, dg=dg, src=src: g.dma_start(
                         out=xbf[s][:, dg * 4 * CHT:(dg + 1) * 4 * CHT]
                         .rearrange("p (a t) -> p a t", a=4), in_=src),
                     incs=xbp.inc(slot), waits=w)
            st["xb_ld"][dg] = xbp.at(slot)
            return
        j = dt % 4
        plan.add("vector",
                 lambda e, s=s, dt=dt, j=j: e.tensor_tensor(
                     out=xsq[j][:], in0=xbf[s][:, dt * CHT:(dt + 1) * CHT],
                     in1=xbf[s][:, dt * CHT:(dt + 1) * CHT], op=ALU.mult),
                 incs=[(ve, 1)], waits=[st["xb_ld"][dt // 4], (te, st["xsq_free"][j])])
        sq_ve = ve.n
        w = [(ve, sq_ve)]
        if dt == 0:
            w += [(cst, const_ld), (cstm, memset_done), (ve, st["stat_rows_ve"])]
        plan.add("tensor",
                 lambda e, s=s, dt=dt: e.matmul(
                     ps_ssum[0:1, :], lhsT=ones_col[:],
                     rhs=xbf[s][:, dt * CHT:(dt + 1) * CHT],
                     start=(dt == 0), stop=(dt == NDT - 1)),
                 waits=w)
        mark = (dt % 2 == 1) or (dt == NDT - 1)
        plan.add("tensor",
                 lambda e, j=j, dt=dt: e.matmul(
                     ps_ssq[0:1, :], lhsT=ones_col[:], rhs=xsq[j][:],
                     start=(dt == 0), stop=(dt == NDT - 1)),
                 incs=[(te, 1)] if mark else [])
        if mark:
            st["xsq_free"][(j - 1) % 4] = te.n
            st["xsq_free"][j] = te.n
        if dt == NDT - 1:
            st["statmm_te"] = te.n

    def stats_math(W):
        inv = 1.0 / DM
        plan.add("vector", lambda e, W=W: e.tensor_scalar(out=r_sum[0:1, 0:W], in0=r_sum[0:1, 0:W], scalar1=inv, scalar2=None, op0=ALU.mult))
        plan.add("vector", lambda e, W=W: e.tensor_scalar(out=r_sq[0:1, 0:W], in0=r_sq[0:1, 0:W], scalar1=inv, scalar2=None, op0=ALU.mult))
        plan.add("vector", lambda e, W=W: e.tensor_tensor(out=r_ln[0:1, 0:W], in0=r_sum[0:1, 0:W], in1=r_sum[0:1, 0:W], op=ALU.mult))
        plan.add("vector", lambda e, W=W: e.tensor_tensor(out=r_sq[0:1, 0:W], in0=r_sq[0:1, 0:W], in1=r_ln[0:1, 0:W], op=ALU.subtract),
                 incs=[(ve, 1)])
        var_ve = ve.n
        plan.add("scalar", lambda e, W=W: e.activation(out=r_ln[0:1, 0:W], in_=r_sq[0:1, 0:W], func=AF.Ln, bias=eps1[:], scale=1.0),
                 incs=[(ac, 1)], waits=[(ve, var_ve), (cstm, memset_done)])
        plan.add("scalar", lambda e, W=W: e.activation(out=rstd_bf[0:1, 0:W], in_=r_ln[0:1, 0:W], func=AF.Exp, scale=-0.5),
                 incs=[(ac, 1)], waits=[(ac, ac.n)])
        rstd_ac = ac.n
        plan.add("vector", lambda e, W=W: e.tensor_tensor(out=r_mr[0:1, 0:W], in0=r_sum[0:1, 0:W], in1=rstd_bf[0:1, 0:W], op=ALU.mult),
                 waits=[(ac, rstd_ac)])
        plan.add("vector", lambda e, W=W: e.tensor_copy(mr_bf[0:1, 0:W], r_mr[0:1, 0:W]), incs=[(ve, 1)])

    pending = list(unit_tasks(0))
    while pending:
        do_task(pending.pop(0))

    for ch in range(NCH):
        s = ch % 2
        plan.add("vector", lambda e: e.tensor_copy(r_sum[0:1, 0:512], ps_ssum[0:1, :]),
                 waits=[(te, st["statmm_te"])])
        plan.add("vector", lambda e: e.tensor_copy(r_sq[0:1, 0:512], ps_ssq[0:1, :]),
                 incs=[(ve, 1)])
        st["stat_rows_ve"] = ve.n
        stats_math(512)
        rows_ve = ve.n
        plan.add("tensor", lambda e: e.matmul(ps_bcr[:], lhsT=ones_row[:], rhs=rstd_bf[0:1, 0:512], start=True, stop=True),
                 waits=[(ve, rows_ve), (ve, st["bc_evac_ve"])])
        plan.add("tensor", lambda e: e.matmul(ps_bcm[:], lhsT=ones_row[:], rhs=mr_bf[0:1, 0:512], start=True, stop=True),
                 incs=[(te, 1)])
        bc_te = te.n
        plan.add("vector", lambda e: e.tensor_copy(rrep[:], ps_bcr[:]), waits=[(te, bc_te)])
        plan.add("vector", lambda e: e.tensor_copy(mrrep[:], ps_bcm[:]), incs=[(ve, 1)])
        st["bc_evac_ve"] = ve.n

        xb_qkv = list(st["xb_ld"])
        if ch + 1 < NCH:
            pending = list(unit_tasks(ch + 1))

        for ct in range(NCT):
            wi = ct % 2
            if ch == 0:
                plan.add("gpsimd",
                         lambda g, ct=ct, wi=wi: g.dma_start(
                             out=wsl[wi][:], in_=wqkv_d[ct].rearrange("p t c -> p (t c)")),
                         incs=wlp.inc(wi), waits=[(te, st["wsl_free"][wi])])
                lded = wlp.at(wi)
                plan.add("gpsimd",
                         lambda g, ct=ct, wi=wi: g.dma_start(
                             out=wbf_d[ct].rearrange("p t c -> p (t c)"), in_=wsl[wi][:]),
                         incs=[(wbs, 16)], waits=[lded])
            else:
                w = [(te, st["wsl_free"][wi])]
                if ch == 1 and ct in (0, 1):
                    w.append((wbs, 16 * NCT))
                plan.add("gpsimd",
                         lambda g, ct=ct, wi=wi: g.dma_start(
                             out=wsl[wi][:], in_=wbf_d[ct].rearrange("p t c -> p (t c)")),
                         incs=wlp.inc(wi), waits=w)
            slab_ld = wlp.at(wi)
            pi = ct % 2
            for dt in range(NDT):
                w = [xb_qkv[dt // 4]] if dt % 4 == 0 else []
                if dt == 0:
                    w = [slab_ld, (ve, st["qkv_evac_ve"][pi]), xb_qkv[0]]
                plan.add("tensor",
                         lambda e, s=s, wi=wi, dt=dt, pi=pi: e.matmul(
                             ps_qkv[pi][:], lhsT=wsl[wi][:, dt * 128:(dt + 1) * 128],
                             rhs=xbf[s][:, dt * CHT:(dt + 1) * CHT],
                             start=(dt == 0), stop=(dt == NDT - 1)),
                         waits=w, incs=[(te, 1)] if dt == NDT - 1 else [])
            mm_te = te.n
            if ch == 0:
                wci = ct % 2
                for dt in range(NDT):
                    plan.add("tensor",
                             lambda e, wi=wi, dt=dt, wci=wci: e.matmul(
                                 ps_wcs[wci][:], lhsT=wsl[wi][:, dt * 128:(dt + 1) * 128],
                                 rhs=ones_col[:], start=(dt == 0), stop=(dt == NDT - 1)),
                             waits=[(ve, st["wcs_ve"][ct - 2])] if (dt == 0 and ct >= 2) else [],
                             incs=[(te, 1)] if dt == NDT - 1 else [])
                myte = te.n
                plan.add("vector",
                         lambda e, ct=ct, wci=wci: e.tensor_copy(wcs_sb[:, ct:ct + 1], ps_wcs[wci][:]),
                         incs=[(ve, 1)], waits=[(te, myte)])
                st["wcs_ve"][ct] = ve.n
            st["wsl_free"][wi] = te.n
            ti = ct % 2
            w_ev = [(te, mm_te), (ve, st["wcs_ve"][ct])]
            plan.add("vector",
                     lambda e, pi=pi, ti=ti: e.tensor_tensor(
                         out=tmpe[ti][:], in0=ps_qkv[pi][:], in1=rrep[:], op=ALU.mult),
                     waits=w_ev)
            plan.add("vector",
                     lambda e, ct=ct, ti=ti: e.tensor_scalar(
                         out=tmpe[1 - ti][:], in0=mrrep[:], scalar1=wcs_sb[:, ct:ct + 1],
                         scalar2=None, op0=ALU.mult))
            if ct <= 7:
                qi = ct % 2
                plan.add("vector",
                         lambda e, qi=qi, ti=ti: e.tensor_tensor(
                             out=qstage[qi][:], in0=tmpe[ti][:], in1=tmpe[1 - ti][:], op=ALU.subtract),
                         incs=[(ve, 1)], waits=[qsp.at(qi)])
                st["qkv_evac_ve"][pi] = ve.n
                myve = ve.n
                plan.add("sync",
                         lambda e, ct=ct, ch=ch, qi=qi: e.dma_start(
                             out=qscr_d[ct, :, ch * CHT:(ch + 1) * CHT], in_=qstage[qi][:]),
                         incs=qsp.inc(qi), waits=[(ve, myve)])
            else:
                dsttn = kt_sb if ct <= 9 else vt_sb
                off = (ct - 8 if ct <= 9 else ct - 10) * SEQ + ch * CHT
                plan.add("vector",
                         lambda e, dsttn=dsttn, off=off, ti=ti: e.tensor_tensor(
                             out=dsttn[:, off:off + CHT], in0=tmpe[ti][:], in1=tmpe[1 - ti][:],
                             op=ALU.subtract),
                         incs=[(ve, 1)])
                st["qkv_evac_ve"][pi] = ve.n
            for _ in range(6):
                if pending:
                    do_task(pending.pop(0))
        st["slab_free"][s] = te.n
        while pending:
            do_task(pending.pop(0))

    teA, veA, acA = te.n, ve.n, ac.n
    qscr_done = qsp.totals()
    esA.close()
    psA.close()

    if debug == "A":
        plan.add("sync", lambda e: e.dma_start(out=dbg["dkv"][:, 0:2 * SEQ], in_=kt_sb[:]),
                 incs=[(dbs, 16)], waits=[(ve, veA)] + qscr_done)
        plan.add("sync", lambda e: e.dma_start(out=dbg["dkv"][:, 2 * SEQ:], in_=vt_sb[:]), incs=[(dbs, 16)])
        for k, row in enumerate([r_sum, r_sq, r_ln, r_sum]):
            plan.add("sync", lambda e, k=k, row=row: e.dma_start(out=dbg["drow"][k:k + 1, :], in_=row[0:1, :]),
                     incs=[(dbs, 16)])
        plan.add("sync", lambda e, v=dbs.n: e.wait_ge(dbs.h, v))
        return _finish(nc, plan, [es, esAB])

    esB = ExitStack()
    expT = [esB.enter_context(nc.sbuf_tensor(f"expT{i}", [128, NDT * 512], BF16)) for i in range(2)]
    vp_sb = esB.enter_context(nc.sbuf_tensor("vp_sb", [128, 2 * 32 * 129], BF16))
    qtb = [esB.enter_context(nc.sbuf_tensor(f"qtb{i}", [128, 512], BF16)) for i in range(2)]
    out_qc = [esB.enter_context(nc.sbuf_tensor(f"outqc{i}", [128, 4096], BF16)) for i in range(4)]
    masks_sb = esB.enter_context(nc.sbuf_tensor("masks_sb", [128, 4 * 512], BF16))
    rec_sb = esB.enter_context(nc.sbuf_tensor("rec_sb", [128, 4], F32))

    plan.add("gpsimd", lambda g: g.memset(vp_sb[:], 1.0), incs=[(vpm, 1)], waits=[(ve, veA)])
    plan.add("gpsimd", lambda g: g.dma_start(out=masks_sb[:].rearrange("p (j q) -> p j q", j=4), in_=masks_d),
             incs=[(mks, 16)])
    mks_ld = mks.n

    psT = ExitStack()
    ps_tr = [psT.enter_context(nc.psum_tensor(f"ps_tr{i}", [128, 128], BF16)) for i in range(2)]
    tr_free = [0, 0]
    for g2 in range(2):
        for kt in range(32):
            i = (g2 * 32 + kt) % 2
            w = [(ve, tr_free[i])]
            if g2 == 0 and kt == 0:
                w += [(te, teA), (ve, veA), (cst, const_ld)]
            plan.add("tensor",
                     lambda e, g2=g2, kt=kt, i=i: e.transpose(
                         ps_tr[i][:], in_=vt_sb[:, g2 * SEQ + kt * 128: g2 * SEQ + (kt + 1) * 128],
                         identity=identb[:]),
                     incs=[(te, 1)], waits=w)
            myte = te.n
            plan.add("vector",
                     lambda e, g2=g2, kt=kt, i=i: e.tensor_copy(
                         vp_sb[:, (g2 * 32 + kt) * 129:(g2 * 32 + kt) * 129 + 128], ps_tr[i][:]),
                     incs=[(ve, 1)], waits=[(te, myte), (mks, mks_ld), (vpm, 1)])
            tr_free[i] = ve.n
    psT.close()
    vbuild_ve = ve.n
    vbuild_te = te.n

    for dtp in range(2):
        plan.add("gpsimd",
                 lambda g, dtp=dtp: g.dma_start(
                     out=wosl[dtp][:], in_=wo_d[dtp].rearrange("p t c -> p (t c)")),
                 incs=wop.inc(dtp), waits=[(ve, veA)])

    psB = ExitStack()
    ps_sc = [psB.enter_context(nc.psum_tensor(f"ps_sc{i}", [128, 1536], F32)) for i in range(2)]
    ps_av = [psB.enter_context(nc.psum_tensor(f"ps_av{i}", [128, 258], F32)) for i in range(2)]

    bst = dict(sc_free=[0, 0], av_free=[0, 0], qtb_free=[0, 0], gidx=0)
    units = [(qc, lh) for qc in range(NCH) for lh in range(8)]
    exp_ac = {}
    mask_ve = {}
    av_done = {-2: 0, -1: 0}

    def gen_scores(ui):
        qc, lh = units[ui]
        g2 = lh // 4
        nkt = 4 * (qc + 1)
        ei = ui % 2
        qi = ui % 2
        w = [(te, bst["qtb_free"][qi])]
        if ui in (0, 1):
            w += qscr_done
        plan.add("sync",
                 lambda e, qc=qc, lh=lh, qi=qi: e.dma_start(
                     out=qtb[qi][:], in_=qscr_d[lh, :, qc * 512:(qc + 1) * 512]),
                 incs=qtp.inc(qi), waits=w)
        qt_ld = qtp.at(qi)
        exp_ac[ui] = [0] * nkt
        mask_ve[ui] = [0] * 4
        kt0 = 0
        while kt0 < nkt:
            gn = min(3, nkt - kt0)
            gi = bst["gidx"] % 2
            bst["gidx"] += 1
            for j in range(gn):
                kt = kt0 + j
                w = []
                if j == 0:
                    w = [(ac, bst["sc_free"][gi]), qt_ld]
                    if ui == 0 and kt0 == 0:
                        w.append((te, vbuild_te))
                plan.add("tensor",
                         lambda e, g2=g2, kt=kt, j=j, gi=gi, qi=qi: e.matmul(
                             ps_sc[gi][:, j * 512:(j + 1) * 512],
                             lhsT=kt_sb[:, g2 * SEQ + kt * 128: g2 * SEQ + (kt + 1) * 128],
                             rhs=qtb[qi][:], start=True, stop=True),
                         waits=w, incs=[(te, 1)] if j == gn - 1 else [])
            bst["qtb_free"][qi] = te.n
            grp_te = te.n
            plan.add("scalar",
                     lambda e, ei=ei, kt0=kt0, gn=gn, gi=gi: e.activation(
                         out=expT[ei][:, kt0 * 512:(kt0 + gn) * 512],
                         in_=ps_sc[gi][:, 0:gn * 512], func=AF.Exp,
                         bias=negc_sb[:], scale=ATT_SCALE),
                     incs=[(ac, 1)],
                     waits=[(te, max(grp_te, av_done[ui - 2])), (cstm, memset_done)])
            bst["sc_free"][gi] = ac.n
            for j in range(gn):
                exp_ac[ui][kt0 + j] = ac.n
            yield
            kt0 += gn
        for j in range(4):
            kt = 4 * qc + j
            plan.add("vector",
                     lambda e, ei=ei, kt=kt, j=j: e.tensor_tensor(
                         out=expT[ei][:, kt * 512:(kt + 1) * 512],
                         in0=expT[ei][:, kt * 512:(kt + 1) * 512],
                         in1=masks_sb[:, j * 512:(j + 1) * 512], op=ALU.mult),
                     incs=[(ve, 1)],
                     waits=[(ac, exp_ac[ui][kt]), (mks, mks_ld), (ve, vbuild_ve),
                            (te, av_done[ui - 2])])
            mask_ve[ui][j] = ve.n
        yield

    def gen_av(ui):
        qc, lh = units[ui]
        g2 = lh // 4
        ei = ui % 2
        oi = qc % 4
        for qs in range(4):
            pi = qs // 2
            nkt_q = 4 * qc + qs + 1
            for kt in range(nkt_q):
                w = []
                if kt == 0:
                    w = [(ac, exp_ac[ui][nkt_q - 1]), (ve, mask_ve[ui][qs]),
                         (ve, bst["av_free"][pi])]
                plan.add("tensor",
                         lambda e, ei=ei, kt=kt, qs=qs, g2=g2, pi=pi: e.matmul(
                             ps_av[pi][:, (qs % 2) * 129:(qs % 2) * 129 + 129],
                             lhsT=expT[ei][:, kt * 512 + qs * 128: kt * 512 + (qs + 1) * 128],
                             rhs=vp_sb[:, (g2 * 32 + kt) * 129:(g2 * 32 + kt + 1) * 129],
                             start=(kt == 0), stop=(kt == nkt_q - 1)),
                         waits=w, incs=[(te, 1)] if kt == nkt_q - 1 else [])
            avte = te.n
            if qs % 2 == 1:
                plan.add("vector",
                         lambda e, pi=pi: e.reciprocal(
                             out=rec_sb[:, 2 * pi:2 * pi + 2], in_=ps_av[pi][:, 128::129]),
                         incs=[(ve, 1)], waits=[(te, avte), oqp.at(oi)])
                myve = ve.n
                for sq in range(2):
                    qs2 = (qs - 1) + sq
                    plan.add("vector",
                             lambda e, pi=pi, sq=sq, qs2=qs2, lh=lh, oi=oi: e.tensor_scalar(
                                 out=out_qc[oi][:, (qs2 * 8 + lh) * 128:(qs2 * 8 + lh + 1) * 128],
                                 in0=ps_av[pi][:, sq * 129: sq * 129 + 128],
                                 scalar1=rec_sb[:, 2 * pi + sq:2 * pi + sq + 1],
                                 scalar2=None, op0=ALU.mult),
                             incs=[(ve, 1)] if sq == 1 else [],
                             waits=[(ve, myve)])
                bst["av_free"][pi] = ve.n
            yield
        av_done[ui] = te.n

    g_sc = gen_scores(0)
    for _ in g_sc:
        pass
    for ui in range(len(units)):
        g_av = gen_av(ui)
        g_next = gen_scores(ui + 1) if ui + 1 < len(units) else iter(())
        done_av = done_sc = False
        while not (done_av and done_sc):
            if not done_av:
                done_av = next(g_av, "end") == "end"
            if not done_sc:
                done_sc = next(g_next, "end") == "end"
        qc, lh = units[ui]
        if lh == 7:
            oi = qc % 4
            myve = ve.n
            plan.add("sync",
                     lambda e, qc=qc, oi=oi: e.dma_start(
                         out=ccin_d[qc * 512:(qc + 1) * 512, :].rearrange("(s p) x -> p s x", s=4),
                         in_=out_qc[oi][:].rearrange("p (s x) -> p s x", s=4)),
                     incs=oqp.inc(oi), waits=[(ve, myve)])
            if debug != "B":
                plan.add("gpsimd",
                         lambda g, qc=qc: g.collective_compute(
                             "AllGather", ALU.bypass,
                             replica_groups=[[0, 1, 2, 3], [4, 5, 6, 7]],
                             ins=[ccin_d[qc * 512:(qc + 1) * 512, :].opt()],
                             outs=[ccout_d[qc * 2048:(qc + 1) * 2048, :].opt()]),
                         incs=[(ccs, 1)], waits=[oqp.at(oi)])

    teB, veB, acB = te.n, ve.n, ac.n
    psB.close()
    esB.close()
    esAB.close()

    if debug == "B":
        plan.add("sync", lambda e, w=tuple(oqp.totals()): [e.wait_ge(s.h, v) for s, v in w] and None)
        return _finish(nc, plan, [es])

    esC = ExitStack()
    ot_sb = esC.enter_context(nc.sbuf_tensor("ot_sb", [128, 32 * 1024], BF16))
    y_sb = esC.enter_context(nc.sbuf_tensor("y_sb", [128, 32 * 1024], BF16))
    xfc = esC.enter_context(nc.sbuf_tensor("xfc0", [128, 1024], F32))
    ysq0 = esC.enter_context(nc.sbuf_tensor("ysq0", [128, 1024], BF16))
    outf = [esC.enter_context(nc.sbuf_tensor(f"outf{i}", [128, 1024], F32)) for i in range(2)]
    stage = [esC.enter_context(nc.sbuf_tensor(f"stage{i}", [128, 1024], BF16)) for i in range(2)]
    awr_sb = esC.enter_context(nc.sbuf_tensor("awr_sb", [128, NDT * 68], BF16))
    btil_sb = esC.enter_context(nc.sbuf_tensor("btil_sb", [64, DM], BF16))
    midlog = esC.enter_context(nc.sbuf_tensor("midlog", [68, 1024], F32))
    r2rep = esC.enter_context(nc.sbuf_tensor("r2rep", [68, 1024], BF16))
    mr2rep = esC.enter_context(nc.sbuf_tensor("mr2rep", [68, 1024], BF16))
    acs_sb = esC.enter_context(nc.sbuf_tensor("acs_sb", [68, 1], F32))
    grep_sb = esC.enter_context(nc.sbuf_tensor("grep_sb", [64, 1024], BF16))
    midg = grep_sb
    gatest = esC.enter_context(nc.sbuf_tensor("gatest", [4, 1024], BF16))
    lgn = [esC.enter_context(nc.sbuf_tensor(f"lgn{i}", [128, 4], F32)) for i in range(2)]
    g4bf = esC.enter_context(nc.sbuf_tensor("g4bf", [128, 4], BF16))
    gwork = esC.enter_context(nc.sbuf_tensor("gwork", [128, 8], F32))
    gtmp = [esC.enter_context(nc.sbuf_tensor(f"gtmp{i}", [128, 4], F32)) for i in range(2)]

    plan.add("gpsimd", lambda g: g.dma_start(out=awr_sb[:].rearrange("p (t j) -> p t j", t=NDT), in_=awr_d),
             incs=[(cdw, 16)], waits=[(te, teB), (ve, veB)])
    plan.add("gpsimd", lambda g: g.dma_start(out=btil_sb[:], in_=btil_d), incs=[(cdw, 16)])
    cd_w = cdw.n

    for blk in range(32):
        i = blk % 2
        w = [(ccs, 8), cws.at(i)]
        plan.add("gpsimd",
                 lambda g, blk=blk, i=i: g.indirect_dma_start(
                     out=stage[i][:], out_offset=None, in_=ccout_d,
                     in_offset=bass_rust.IndirectOffsetOnAxis(ap=gidx_sb[:, blk:blk + 1], axis=0)),
                 incs=stp.inc(i), waits=w)
        myg = stp.at(i)
        plan.add("gpsimd",
                 lambda g, blk=blk, i=i: g.dma_start(
                     out=ccwin_d[blk * 128:(blk + 1) * 128, :], in_=stage[i][:]),
                 incs=cws.inc(i), waits=[myg])
    gath_done = tuple(cws.totals())

    for ot in range(32):
        r, loc = ot // 8, ot % 8
        w = list(gath_done)
        if ot == 0:
            w += [(te, teB), (ve, veB), (ac, acB)]
        plan.add("sync",
                 lambda e, ot=ot, r=r, loc=loc: e.dma_start(
                     out=ot_sb[:, ot * 1024:(ot + 1) * 1024],
                     in_=ccwin_d[r * 1024:(r + 1) * 1024, loc * 128:(loc + 1) * 128],
                     transpose=True),
                 incs=[(otl, 16)], waits=w)
    ot_done = otl.n

    if debug == "G":
        plan.add("sync", lambda e: e.dma_start(out=dbg["dy"], in_=ot_sb[:]), incs=[(dbs, 16)],
                 waits=[(otl, ot_done)])
        plan.add("sync", lambda e, v=dbs.n: e.wait_ge(dbs.h, v))
        return _finish(nc, plan, [es, esC])

    psC = ExitStack()
    ps_y = [psC.enter_context(nc.psum_tensor(f"ps_y{i}", [128, 1024], F32)) for i in range(2)]
    ps_cs = psC.enter_context(nc.psum_tensor("ps_cs", [1, 1024], F32))
    ps_cq = psC.enter_context(nc.psum_tensor("ps_cq", [1, 1024], F32))

    cst_ = dict(wo_free=[0, 0], xfc_free=0, ysq_free=0, y_evac_ve=[0, 0])
    for dt in range(NDT):
        i = dt % 2
        if dt >= 2:
            plan.add("gpsimd",
                     lambda g, dt=dt, i=i: g.dma_start(
                         out=wosl[i][:], in_=wo_d[dt].rearrange("p t c -> p (t c)")),
                     incs=wop.inc(i), waits=[(te, cst_["wo_free"][i])])
        wo_ld = wop.at(i)
        plan.add("sync",
                 lambda e, dt=dt: e.dma_start(
                     out=xfc[:], in_=xtc_d[dt * 128:(dt + 1) * 128, :]),
                 incs=xcp.inc(0), waits=[(ve, cst_["xfc_free"])])
        xfc_ld = xcp.at(0)
        for ot in range(32):
            for h in range(2):
                w = []
                if ot == 0 and h == 0:
                    w = [wo_ld, (otl, ot_done), (ve, cst_["y_evac_ve"][i])]
                plan.add("tensor",
                         lambda e, dt=dt, i=i, ot=ot, h=h: e.matmul(
                             ps_y[i][:, h * 512:(h + 1) * 512],
                             lhsT=wosl[i][:, ot * 128:(ot + 1) * 128],
                             rhs=ot_sb[:, ot * 1024 + h * 512: ot * 1024 + h * 512 + 512],
                             start=(ot == 0), stop=(ot == 31)),
                         waits=w, incs=[(te, 1)] if (ot == 31 and h == 1) else [])
        cst_["wo_free"][i] = te.n
        ymm_te = te.n
        plan.add("vector",
                 lambda e, dt=dt, i=i: e.tensor_tensor(
                     out=y_sb[:, dt * 1024:(dt + 1) * 1024], in0=ps_y[i][:], in1=xfc[:], op=ALU.add),
                 incs=[(ve, 1)], waits=[(te, ymm_te), xfc_ld])
        cst_["y_evac_ve"][i] = ve.n
        cst_["xfc_free"] = ve.n
        plan.add("vector",
                 lambda e, dt=dt: e.tensor_tensor(
                     out=ysq0[:], in0=y_sb[:, dt * 1024:(dt + 1) * 1024],
                     in1=y_sb[:, dt * 1024:(dt + 1) * 1024], op=ALU.mult),
                 incs=[(ve, 1)], waits=[(te, cst_["ysq_free"])])
        sq_ve = ve.n
        for h in range(2):
            plan.add("tensor",
                     lambda e, dt=dt, h=h: e.matmul(
                         ps_cs[0:1, h * 512:(h + 1) * 512], lhsT=ones_col[:],
                         rhs=y_sb[:, dt * 1024 + h * 512: dt * 1024 + h * 512 + 512],
                         start=(dt == 0), stop=(dt == NDT - 1)),
                     waits=[(ve, sq_ve)] if h == 0 else [])
            plan.add("tensor",
                     lambda e, dt=dt, h=h: e.matmul(
                         ps_cq[0:1, h * 512:(h + 1) * 512], lhsT=ones_col[:],
                         rhs=ysq0[:, h * 512:(h + 1) * 512],
                         start=(dt == 0), stop=(dt == NDT - 1)),
                     incs=[(te, 1)] if h == 1 else [])
        cst_["ysq_free"] = te.n
    cstat_te = te.n

    plan.add("vector", lambda e: e.tensor_copy(r_sum[0:1, :], ps_cs[0:1, :]), waits=[(te, cstat_te)])
    plan.add("vector", lambda e: e.tensor_copy(r_sq[0:1, :], ps_cq[0:1, :]), incs=[(ve, 1)])
    stats_math(1024)
    rows2_ve = ve.n
    psC.close()

    if debug == "C":
        plan.add("sync", lambda e: e.dma_start(out=dbg["dy"], in_=y_sb[:]), incs=[(dbs, 16)],
                 waits=[(ve, rows2_ve)])
        for k, row in enumerate([r_sum, r_sq, r_ln, r_sum]):
            plan.add("sync", lambda e, k=k, row=row: e.dma_start(out=dbg["drow"][k:k + 1, :], in_=row[0:1, :]),
                     incs=[(dbs, 16)])
        plan.add("sync", lambda e, v=dbs.n: e.wait_ge(dbs.h, v))
        return _finish(nc, plan, [es, esC])

    psD1 = ExitStack()
    ps_mid = psD1.enter_context(nc.psum_tensor("ps_mid", [68, 1024], F32))
    ps_acs = psD1.enter_context(nc.psum_tensor("ps_acs", [68, 1], F32))
    ps_b68 = psD1.enter_context(nc.psum_tensor("ps_b68", [68, 1024], F32))

    for dt in range(NDT):
        plan.add("tensor",
                 lambda e, dt=dt: e.matmul(
                     ps_acs[:], lhsT=awr_sb[:, dt * 68:(dt + 1) * 68], rhs=ones_col[:],
                     start=(dt == 0), stop=(dt == NDT - 1)),
                 waits=[(cdw, cd_w), (te, cstat_te), (ve, rows2_ve)] if dt == 0 else [])
        for h in range(2):
            plan.add("tensor",
                     lambda e, dt=dt, h=h: e.matmul(
                         ps_mid[:, h * 512:(h + 1) * 512],
                         lhsT=awr_sb[:, dt * 68:(dt + 1) * 68],
                         rhs=y_sb[:, dt * 1024 + h * 512: dt * 1024 + h * 512 + 512],
                         start=(dt == 0), stop=(dt == NDT - 1)),
                     incs=[(te, 1)] if (dt == NDT - 1 and h == 1) else [])
    mid_te = te.n
    plan.add("vector", lambda e: e.tensor_copy(acs_sb[:], ps_acs[:]), incs=[(ve, 1)],
             waits=[(te, mid_te)])
    acs_ve = ve.n

    for h in range(2):
        plan.add("tensor",
                 lambda e, h=h: e.matmul(
                     ps_b68[:, h * 512:(h + 1) * 512], lhsT=ones_row[0:1, 0:68],
                     rhs=rstd_bf[0:1, h * 512:(h + 1) * 512], start=True, stop=True),
                 waits=[(ve, rows2_ve)] if h == 0 else [], incs=[(te, 1)] if h == 1 else [])
    plan.add("vector", lambda e: e.tensor_copy(r2rep[:], ps_b68[:]), incs=[(ve, 1)],
             waits=[(te, te.n)])
    r2_ve = ve.n
    for h in range(2):
        plan.add("tensor",
                 lambda e, h=h: e.matmul(
                     ps_b68[:, h * 512:(h + 1) * 512], lhsT=ones_row[0:1, 0:68],
                     rhs=mr_bf[0:1, h * 512:(h + 1) * 512], start=True, stop=True),
                 waits=[(ve, r2_ve)] if h == 0 else [], incs=[(te, 1)] if h == 1 else [])
    plan.add("vector", lambda e: e.tensor_copy(mr2rep[:], ps_b68[:]), incs=[(ve, 1)],
             waits=[(te, te.n)])

    plan.add("vector", lambda e: e.tensor_tensor(out=midlog[:], in0=ps_mid[:], in1=r2rep[:], op=ALU.mult),
             waits=[(ve, ve.n)])
    plan.add("vector", lambda e: e.tensor_scalar(out=ysq0[0:68, :], in0=mr2rep[:], scalar1=acs_sb[:],
                                                 scalar2=None, op0=ALU.mult),
             waits=[(ve, acs_ve)])
    plan.add("vector", lambda e: e.tensor_tensor(out=midlog[:], in0=midlog[:], in1=ysq0[0:68, :], op=ALU.subtract),
             incs=[(ve, 1)])
    midlog_ve = ve.n
    psD1.close()

    psD2 = ExitStack()
    ps_tr4 = psD2.enter_context(nc.psum_tensor("ps_tr4", [128, 4], F32))
    ps_gt = psD2.enter_context(nc.psum_tensor("ps_gt", [4, 128], BF16))
    ps_grep = psD2.enter_context(nc.psum_tensor("ps_grep", [64, 1024], F32))
    lgn_free = [0, 0]
    tr4_free = 0
    gt_copy_ve = 0
    prev_gt_te = 0
    for tc in range(8):
        i = tc % 2
        w = [(ve, lgn_free[i]), (ve, tr4_free)]
        if tc == 0:
            w.append((ve, midlog_ve))
        plan.add("tensor",
                 lambda e, tc=tc: e.transpose(
                     ps_tr4[:], in_=midlog[64:68, tc * 128:(tc + 1) * 128], identity=identf[64:68, 64:68]),
                 incs=[(te, 1)], waits=w)
        myte = te.n
        plan.add("vector", lambda e, i=i: e.tensor_copy(lgn[i][:], ps_tr4[:]), incs=[(ve, 1)],
                 waits=[(te, myte)])
        tr4_free = ve.n

        def V(fn, waits=()):
            plan.add("vector", fn, incs=[(ve, 1)], waits=list(waits) + [(ve, ve.n)])
        V(lambda e, i=i: e.tensor_reduce(out=gwork[:, 0:1], in_=lgn[i][:], axis=AX.X, op=ALU.max))
        m1_ve = ve.n
        V(lambda e: e.tensor_scalar(out=gwork[:, 1:2], in0=gwork[:, 0:1], scalar1=-1.0, scalar2=None, op0=ALU.mult))
        negm1_ve = ve.n
        plan.add("scalar",
                 lambda e, i=i: e.activation(out=gtmp[0][:], in_=lgn[i][:], func=AF.Exp, bias=gwork[:, 1:2], scale=1.0),
                 incs=[(ac, 1)], waits=[(ve, negm1_ve)])
        e4_ac = ac.n
        V(lambda e, i=i: e.tensor_scalar(out=gtmp[1][:], in0=lgn[i][:], scalar1=gwork[:, 0:1], scalar2=BIG,
                                         op0=ALU.is_equal, op1=ALU.mult))
        V(lambda e, i=i: e.tensor_tensor(out=gtmp[1][:], in0=lgn[i][:], in1=gtmp[1][:], op=ALU.subtract))
        V(lambda e: e.tensor_reduce(out=gwork[:, 2:3], in_=gtmp[1][:], axis=AX.X, op=ALU.max))
        V(lambda e, i=i: e.tensor_scalar(out=gtmp[1][:], in0=lgn[i][:], scalar1=gwork[:, 2:3], scalar2=None,
                                         op0=ALU.is_ge))
        lgn_free[i] = ve.n
        V(lambda e: e.tensor_tensor(out=gtmp[0][:], in0=gtmp[0][:], in1=gtmp[1][:], op=ALU.mult),
          waits=[(ac, e4_ac)])
        V(lambda e: e.tensor_reduce(out=gwork[:, 3:4], in_=gtmp[0][:], axis=AX.X, op=ALU.add))
        V(lambda e: e.reciprocal(out=gwork[:, 4:5], in_=gwork[:, 3:4]))
        V(lambda e: e.tensor_scalar(out=g4bf[:], in0=gtmp[0][:], scalar1=gwork[:, 4:5], scalar2=4.0,
                                    op0=ALU.mult, op1=ALU.mult),
          waits=[(te, prev_gt_te)])
        g4_ve = ve.n
        plan.add("tensor",
                 lambda e: e.transpose(ps_gt[:], in_=g4bf[:], identity=identb[:]),
                 incs=[(te, 1)], waits=[(ve, g4_ve), (ve, gt_copy_ve)])
        prev_gt_te = te.n
        plan.add("vector", lambda e, tc=tc: e.tensor_copy(gatest[:, tc * 128:(tc + 1) * 128], ps_gt[:]),
                 incs=[(ve, 1)], waits=[(te, prev_gt_te)])
        gt_copy_ve = ve.n
    gates_ve = ve.n

    for h in range(2):
        plan.add("tensor",
                 lambda e, h=h: e.matmul(
                     ps_grep[:, h * 512:(h + 1) * 512], lhsT=pmat_sb[:],
                     rhs=gatest[0:4, h * 512:(h + 1) * 512], start=True, stop=True),
                 waits=[(ve, gates_ve), (cst, const_ld)] if h == 0 else [],
                 incs=[(te, 1)] if h == 1 else [])
    plan.add("vector", lambda e: e.tensor_copy(grep_sb[:], ps_grep[:]), incs=[(ve, 1)],
             waits=[(te, te.n)])
    if debug == "D":
        plan.add("sync", lambda e: e.dma_start(out=dbg["dgw"], in_=gwork[:]), incs=[(dbs, 16)],
                 waits=[(ve, gates_ve)])
        plan.add("sync", lambda e: e.dma_start(out=dbg["dlg"], in_=lgn[1][:]), incs=[(dbs, 16)])
        plan.add("sync", lambda e: e.dma_start(out=dbg["dg0"], in_=gtmp[0][:]), incs=[(dbs, 16)])
        plan.add("sync", lambda e: e.dma_start(out=dbg["dg1"], in_=gtmp[1][:]), incs=[(dbs, 16)])
        plan.add("sync", lambda e: e.dma_start(out=dbg["dml"], in_=midlog[:]), incs=[(dbs, 16)],
                 waits=[(ve, midlog_ve)])
        plan.add("sync", lambda e: e.dma_start(out=dbg["dgt"], in_=gatest[:]), incs=[(dbs, 16)],
                 waits=[(ve, gates_ve)])
        plan.add("sync", lambda e: e.dma_start(out=dbg["dgr"], in_=grep_sb[:]), incs=[(dbs, 16)],
                 waits=[(ve, ve.n)])
        plan.add("sync", lambda e: e.dma_start(out=dbg["dac"], in_=acs_sb[:]), incs=[(dbs, 16)],
                 waits=[(ve, acs_ve)])
        plan.add("sync", lambda e: e.dma_start(out=dbg["dr2"][:, 0:1024], in_=r2rep[:]), incs=[(dbs, 16)])
        plan.add("sync", lambda e: e.dma_start(out=dbg["dr2"][:, 1024:2048], in_=mr2rep[:]), incs=[(dbs, 16)])
    plan.add("vector", lambda e: e.tensor_tensor(out=midg[:], in0=midlog[0:64, :], in1=grep_sb[:], op=ALU.mult),
             incs=[(ve, 1)])
    midg_ve = ve.n
    psD2.close()

    psD3 = ExitStack()
    ps_d = [psD3.enter_context(nc.psum_tensor(f"ps_d{i}", [128, 1024], F32)) for i in range(2)]
    d_free = [0, 0]
    for dt in range(NDT):
        i = dt % 2
        for h in range(2):
            w = []
            if h == 0:
                w = [(ve, midg_ve), (ve, d_free[i])]
            plan.add("tensor",
                     lambda e, dt=dt, i=i, h=h: e.matmul(
                         ps_d[i][:, h * 512:(h + 1) * 512],
                         lhsT=btil_sb[:, dt * 128:(dt + 1) * 128],
                         rhs=midg[:, h * 512:(h + 1) * 512], start=True, stop=True),
                     waits=w, incs=[(te, 1)] if h == 1 else [])
        dte = te.n
        plan.add("vector",
                 lambda e, dt=dt, i=i: e.tensor_tensor(
                     out=outf[i][:], in0=ps_d[i][:], in1=y_sb[:, dt * 1024:(dt + 1) * 1024], op=ALU.add),
                 incs=[(ve, 1)], waits=[(te, dte), ofp.at(i)])
        d_free[i] = ve.n
        myve = ve.n
        plan.add("sync",
                 lambda e, dt=dt, i=i: e.dma_start(out=yt_d[dt * 128:(dt + 1) * 128, :], in_=outf[i][:]),
                 incs=ofp.inc(i), waits=[(ve, myve)])

    fin = tuple(ofp.totals())
    plan.add("sync", lambda e, w=fin: [e.wait_ge(s.h, v) for s, v in w] and None)
    psD3.close()
    return _finish(nc, plan, [es, esC])


def _finish(nc, plan, stacks):
    block_cm = nc.Block()
    block = block_cm.__enter__()
    for eng in ENGINES:
        lst = plan.ops[eng]
        if not lst:
            continue
        dec = getattr(block, eng)

        def section(e, lst=lst):
            hwm = {}
            for waits, fn, incs in lst:
                for s, v in waits:
                    if v > hwm.get(s.name, 0):
                        e.wait_ge(s.h, v)
                        hwm[s.name] = v
                inst = fn(e) if fn is not None else None
                for s, k in incs:
                    inst.then_inc(s.h, k)
        dec(section)
    block_cm.__exit__(None, None, None)
    for stk in stacks[::-1]:
        try:
            stk.close()
        except Exception:
            pass
    return nc


_CACHE = {}


def _host_prep(L0_attn_x, L0_attn_wq, L0_attn_wk, L0_attn_wv, L0_attn_wo,
               L0_ffn_A, L0_ffn_B, L0_ffn_Wr):
    x = np.asarray(L0_attn_x, np.float32)
    wq = np.asarray(L0_attn_wq, np.float32)
    wk = np.asarray(L0_attn_wk, np.float32)
    wv = np.asarray(L0_attn_wv, np.float32)
    wo = np.asarray(L0_attn_wo, np.float32)
    A = np.asarray(L0_ffn_A, np.float32)
    B = np.asarray(L0_ffn_B, np.float32)
    Wr = np.asarray(L0_ffn_Wr, np.float32)

    xts = [np.ascontiguousarray(x[b].T) for b in range(2)]
    wo_arr = np.ascontiguousarray(wo.reshape(32, 128, 32, 128).transpose(2, 1, 0, 3))
    awr = np.concatenate([A.transpose(2, 0, 1).reshape(DM, 64), Wr.T], axis=1)
    awr_arr = np.ascontiguousarray(awr.reshape(NDT, 128, 68).transpose(1, 0, 2))
    btil_arr = np.ascontiguousarray(B.transpose(0, 2, 1).reshape(64, DM))
    masks = (np.arange(512)[None, None, :] >=
             (128 * np.arange(4)[None, :, None] + np.arange(128)[:, None, None])).astype(bfnp)
    identb = np.eye(128, dtype=bfnp)
    identf = np.eye(128, dtype=np.float32)
    pmat = (np.arange(64)[None, :] // 16 == np.arange(4)[:, None]).astype(bfnp)

    in_maps = []
    for core in range(8):
        b, g = core // 4, core % 4
        wcat = np.concatenate([wq[:, g * 1024:(g + 1) * 1024],
                               wk[:, g * 256:(g + 1) * 256],
                               wv[:, g * 256:(g + 1) * 256]], axis=1)
        wqkv_arr = np.ascontiguousarray(wcat.reshape(NDT, 128, NCT, 128).transpose(2, 1, 0, 3))
        xtc = np.ascontiguousarray(xts[b][:, g * TOK:(g + 1) * TOK])
        p = np.arange(128)[:, None]
        blk = np.arange(32)[None, :]
        s = blk % 8
        r = blk // 8
        gidx = ((2 * g + s // 4) * 2048 + r * 512 + (s % 4) * 128 + p).astype(np.int32)
        in_maps.append({
            "xt": xts[b], "xtc": xtc, "wqkv": wqkv_arr, "wot": wo_arr,
            "awr": awr_arr, "btil": btil_arr, "masks": masks,
            "identb": identb, "identf": identf, "pmat": pmat, "gidx": gidx,
        })
    return in_maps


def kernel(**inputs):
    in_maps = _host_prep(**inputs)
    if "nc" not in _CACHE:
        _CACHE["nc"] = build_program(DEBUG_PHASE)
    nc = _CACHE["nc"]
    res = run_bass_kernel_spmd(nc, in_maps, core_ids=list(range(8)))
    out = np.empty((2, SEQ, DM), np.float32)
    for core in range(8):
        b, g = core // 4, core % 4
        out[b, g * TOK:(g + 1) * TOK, :] = np.asarray(res.results[core]["yt"]).T
    return out


# version 09
# speedup vs baseline: 1.0659x; 1.0659x over previous
import numpy as np
import ml_dtypes
import bass_rust
import concourse.bass as bass
import concourse.mybir as mybir
from concourse.bass_utils import run_bass_kernel_spmd
from contextlib import ExitStack

F32 = mybir.dt.float32
BF16 = mybir.dt.bfloat16
I32 = mybir.dt.int32
AF = mybir.ActivationFunctionType
ALU = mybir.AluOpType
AX = mybir.AxisListType
bfnp = ml_dtypes.bfloat16

SEQ = 4096
DM = 4096
NDT = DM // 128
NCH = 8
CHT = SEQ // NCH
NCT = 12
TOK = 1024
ATT_SCALE = float(128.0 ** -0.5)
NEGC = -12.0
EPS = 1e-5
BIG = 1e30
ENGINES = ("tensor", "vector", "scalar", "sync", "gpsimd")

DEBUG_PHASE = None


class Sem:
    def __init__(self, name):
        self.name = name
        self.h = None
        self.n = 0


class Pool:
    def __init__(self, es, nc, name, n):
        self.sems = []
        for i in range(n):
            s = Sem(f"{name}{i}")
            s.h = es.enter_context(nc.semaphore(s.name))
            self.sems.append(s)

    def inc(self, i):
        return [(self.sems[i], 16)]

    def at(self, i):
        return (self.sems[i], self.sems[i].n)

    def totals(self):
        return [(s, s.n) for s in self.sems]


class Plan:
    def __init__(self):
        self.ops = {e: [] for e in ENGINES}

    def add(self, eng, fn=None, incs=(), waits=()):
        waits = tuple((s, v) for (s, v) in waits if v > 0)
        incs = tuple(incs)
        for s, k in incs:
            s.n += k
        self.ops[eng].append((waits, fn, incs))


def build_program(debug=None):
    nc = bass.Bass("TRN2", target_bir_lowering=False, debug=False)

    xt_d = nc.dram_tensor("xt", [DM, SEQ], F32, kind="ExternalInput").ap()
    xtc_d = nc.dram_tensor("xtc", [DM, TOK], F32, kind="ExternalInput").ap()
    wqkv_d = nc.dram_tensor("wqkv", [NCT, 128, NDT, 128], F32, kind="ExternalInput").ap()
    wo_d = nc.dram_tensor("wot", [NDT, 128, 32, 128], F32, kind="ExternalInput").ap()
    awr_d = nc.dram_tensor("awr", [128, NDT, 68], F32, kind="ExternalInput").ap()
    btil_d = nc.dram_tensor("btil", [64, DM], F32, kind="ExternalInput").ap()
    masks_d = nc.dram_tensor("masks", [128, 4, 512], BF16, kind="ExternalInput").ap()
    identb_d = nc.dram_tensor("identb", [128, 128], BF16, kind="ExternalInput").ap()
    identf_d = nc.dram_tensor("identf", [128, 128], F32, kind="ExternalInput").ap()
    pmat_d = nc.dram_tensor("pmat", [4, 64], BF16, kind="ExternalInput").ap()
    gidx_d = nc.dram_tensor("gidx", [128, 32], I32, kind="ExternalInput").ap()
    yt_d = nc.dram_tensor("yt", [DM, TOK], F32, kind="ExternalOutput").ap()

    wbf_d = nc.dram_tensor("wbf", [NCT, 128, NDT, 128], BF16, kind="Internal").ap()
    qscr_k = "ExternalOutput" if debug == "A" else "Internal"
    qscr_d = nc.dram_tensor("qscr", [8, 128, SEQ], BF16, kind=qscr_k).ap()
    ccin_k = "ExternalOutput" if debug == "B" else "Internal"
    ccin_d = nc.dram_tensor("ccin", [SEQ, 1024], BF16, kind=ccin_k).ap()
    ccout_d = nc.dram_tensor("ccout", [4 * SEQ, 1024], BF16, kind="Internal").ap()
    ccwin_k = "ExternalOutput" if debug == "G" else "Internal"
    ccwin_d = nc.dram_tensor("ccwin", [4 * TOK, 1024], BF16, kind=ccwin_k).ap()
    dbg = {}
    if debug == "A":
        dbg["dkv"] = nc.dram_tensor("dkv", [128, 4 * SEQ], BF16, kind="ExternalOutput").ap()
    if debug in ("A", "C"):
        dbg["drow"] = nc.dram_tensor("drow", [4, 1024], F32, kind="ExternalOutput").ap()
    if debug in ("C", "G"):
        dbg["dy"] = nc.dram_tensor("dy", [128, 32 * 1024], BF16, kind="ExternalOutput").ap()
    if debug == "D":
        dbg["dml"] = nc.dram_tensor("dml", [68, 1024], F32, kind="ExternalOutput").ap()
        dbg["dgt"] = nc.dram_tensor("dgt", [4, 1024], BF16, kind="ExternalOutput").ap()
        dbg["dgr"] = nc.dram_tensor("dgr", [64, 1024], BF16, kind="ExternalOutput").ap()
        dbg["dac"] = nc.dram_tensor("dac", [68, 1], F32, kind="ExternalOutput").ap()
        dbg["dr2"] = nc.dram_tensor("dr2", [68, 2048], BF16, kind="ExternalOutput").ap()
        dbg["dgw"] = nc.dram_tensor("dgw", [128, 8], F32, kind="ExternalOutput").ap()
        dbg["dlg"] = nc.dram_tensor("dlg", [128, 4], F32, kind="ExternalOutput").ap()
        dbg["dg0"] = nc.dram_tensor("dg0", [128, 4], F32, kind="ExternalOutput").ap()
        dbg["dg1"] = nc.dram_tensor("dg1", [128, 4], F32, kind="ExternalOutput").ap()

    plan = Plan()
    es = ExitStack()

    def sb(name, shape, dt):
        return es.enter_context(nc.sbuf_tensor(name, shape, dt))

    sems = [Sem(n) for n in ("te", "ve", "ac", "ccs", "cst", "cstm", "mks", "vpm",
                             "cdw", "wbs", "cwp", "otl", "dbs")]
    te, ve, ac, ccs, cst, cstm, mks, vpm, cdw, wbs, cwp, otl, dbs = sems
    for s in sems:
        s.h = es.enter_context(nc.semaphore(s.name))
    xbp = Pool(es, nc, "xbp", 16)
    wlp = Pool(es, nc, "wlp", 2)
    qsp = Pool(es, nc, "qsp", 2)
    qtp = Pool(es, nc, "qtp", 2)
    oqp = Pool(es, nc, "oqp", 4)
    stp = Pool(es, nc, "stp", 2)
    wop = Pool(es, nc, "wop", 2)
    xcp = Pool(es, nc, "xcp", 1)
    ofp = Pool(es, nc, "ofp", 2)
    cws = Pool(es, nc, "cws", 2)

    ones_col = sb("ones_col", [128, 1], BF16)
    ones_row = sb("ones_row", [1, 128], BF16)
    negc_sb = sb("negc", [128, 1], F32)
    eps1 = sb("eps1", [1, 1], F32)
    r_sum = sb("r_sum", [1, 1024], F32)
    r_sq = sb("r_sq", [1, 1024], F32)
    r_ln = sb("r_ln", [1, 1024], F32)
    r_mr = r_ln
    rstd_bf = sb("rstd_bf", [1, 1024], BF16)
    mr_bf = sb("mr_bf", [1, 1024], BF16)
    wcs_sb = sb("wcs", [128, 12], F32)
    identb = sb("identb_sb", [128, 128], BF16)
    identf = sb("identf_sb", [128, 128], F32)
    pmat_sb = sb("pmat_sb", [4, 64], BF16)
    gidx_sb = sb("gidx_sb", [128, 32], I32)
    wosl = [sb(f"wosl{i}", [128, 32 * 128], BF16) for i in range(2)]

    plan.add("gpsimd", lambda g: g.memset(ones_col[:], 1.0))
    plan.add("gpsimd", lambda g: g.memset(ones_row[:], 1.0))
    plan.add("gpsimd", lambda g: g.memset(negc_sb[:], NEGC))
    plan.add("gpsimd", lambda g: g.memset(eps1[:], EPS), incs=[(cstm, 1)])
    plan.add("gpsimd", lambda g: g.dma_start(out=identb[:], in_=identb_d), incs=[(cst, 16)])
    plan.add("gpsimd", lambda g: g.dma_start(out=identf[:], in_=identf_d), incs=[(cst, 16)])
    plan.add("gpsimd", lambda g: g.dma_start(out=pmat_sb[:], in_=pmat_d), incs=[(cst, 16)])
    plan.add("gpsimd", lambda g: g.dma_start(out=gidx_sb[:], in_=gidx_d), incs=[(cst, 16)])
    const_ld = cst.n
    memset_done = cstm.n

    esAB = ExitStack()
    kt_sb = esAB.enter_context(nc.sbuf_tensor("kt_sb", [128, 2 * SEQ], BF16))
    vt_sb = esAB.enter_context(nc.sbuf_tensor("vt_sb", [128, 2 * SEQ], BF16))

    esA = ExitStack()
    xbf = [esA.enter_context(nc.sbuf_tensor(f"xbf{i}", [128, NDT * CHT], BF16)) for i in range(2)]
    wsl = [esA.enter_context(nc.sbuf_tensor(f"wsl{i}", [128, NDT * 128], BF16)) for i in range(2)]
    xsq = [esA.enter_context(nc.sbuf_tensor(f"xsq{i}", [128, CHT], BF16)) for i in range(4)]
    qstage = [esA.enter_context(nc.sbuf_tensor(f"qstage{i}", [128, CHT], BF16)) for i in range(2)]
    tmpe = [esA.enter_context(nc.sbuf_tensor(f"tmpe{i}", [128, CHT], BF16)) for i in range(2)]
    rrep = esA.enter_context(nc.sbuf_tensor("rrep", [128, CHT], BF16))
    mrrep = esA.enter_context(nc.sbuf_tensor("mrrep", [128, CHT], BF16))

    psA = ExitStack()
    ps_ssum = psA.enter_context(nc.psum_tensor("ps_ssum", [128, 512], F32))
    ps_ssq = psA.enter_context(nc.psum_tensor("ps_ssq", [128, 512], F32))
    ps_bcr = psA.enter_context(nc.psum_tensor("ps_bcr", [128, 512], F32))
    ps_bcm = psA.enter_context(nc.psum_tensor("ps_bcm", [128, 512], F32))
    ps_qkv = [psA.enter_context(nc.psum_tensor(f"ps_qkv{i}", [128, 512], F32)) for i in range(2)]
    ps_wcs = [psA.enter_context(nc.psum_tensor(f"ps_wcs{i}", [128, 1], F32)) for i in range(2)]

    st = dict(xb_ld=[None] * 8, xsq_free=[0, 0, 0, 0], slab_free=[0, 0], wsl_free=[0, 0],
              qkv_evac_ve=[0, 0], stat_rows_ve=0, bc_evac_ve=0,
              wcs_ve=[0] * 12, statmm_te=0)

    def unit_tasks(ch):
        for dt in range(NDT):
            if dt % 4 == 0:
                yield ("load", ch, dt)
            yield ("proc", ch, dt)

    def do_task(task):
        kind, ch, dt = task
        s = ch % 2
        if kind == "load":
            dg = dt // 4
            slot = (ch % 2) * 8 + dg
            src = xt_d.rearrange("(a p) t -> p a t", p=128)[
                :, dg * 4:(dg + 1) * 4, ch * CHT:(ch + 1) * CHT]
            w = []
            if dg == 0:
                w = [(te, st["slab_free"][s])]
            plan.add("gpsimd",
                     lambda g, s=s, dg=dg, src=src: g.dma_start(
                         out=xbf[s][:, dg * 4 * CHT:(dg + 1) * 4 * CHT]
                         .rearrange("p (a t) -> p a t", a=4), in_=src),
                     incs=xbp.inc(slot), waits=w)
            st["xb_ld"][dg] = xbp.at(slot)
            return
        j = dt % 4
        plan.add("vector",
                 lambda e, s=s, dt=dt, j=j: e.tensor_tensor(
                     out=xsq[j][:], in0=xbf[s][:, dt * CHT:(dt + 1) * CHT],
                     in1=xbf[s][:, dt * CHT:(dt + 1) * CHT], op=ALU.mult),
                 incs=[(ve, 1)], waits=[st["xb_ld"][dt // 4], (te, st["xsq_free"][j])])
        sq_ve = ve.n
        w = [(ve, sq_ve)]
        if dt == 0:
            w += [(cst, const_ld), (cstm, memset_done), (ve, st["stat_rows_ve"])]
        plan.add("tensor",
                 lambda e, s=s, dt=dt: e.matmul(
                     ps_ssum[0:1, :], lhsT=ones_col[:],
                     rhs=xbf[s][:, dt * CHT:(dt + 1) * CHT],
                     start=(dt == 0), stop=(dt == NDT - 1)),
                 waits=w)
        mark = (dt % 2 == 1) or (dt == NDT - 1)
        plan.add("tensor",
                 lambda e, j=j, dt=dt: e.matmul(
                     ps_ssq[0:1, :], lhsT=ones_col[:], rhs=xsq[j][:],
                     start=(dt == 0), stop=(dt == NDT - 1)),
                 incs=[(te, 1)] if mark else [])
        if mark:
            st["xsq_free"][(j - 1) % 4] = te.n
            st["xsq_free"][j] = te.n
        if dt == NDT - 1:
            st["statmm_te"] = te.n

    def stats_math(W):
        inv = 1.0 / DM
        plan.add("vector", lambda e, W=W: e.tensor_scalar(out=r_sum[0:1, 0:W], in0=r_sum[0:1, 0:W], scalar1=inv, scalar2=None, op0=ALU.mult))
        plan.add("vector", lambda e, W=W: e.tensor_scalar(out=r_sq[0:1, 0:W], in0=r_sq[0:1, 0:W], scalar1=inv, scalar2=None, op0=ALU.mult))
        plan.add("vector", lambda e, W=W: e.tensor_tensor(out=r_ln[0:1, 0:W], in0=r_sum[0:1, 0:W], in1=r_sum[0:1, 0:W], op=ALU.mult))
        plan.add("vector", lambda e, W=W: e.tensor_tensor(out=r_sq[0:1, 0:W], in0=r_sq[0:1, 0:W], in1=r_ln[0:1, 0:W], op=ALU.subtract),
                 incs=[(ve, 1)])
        var_ve = ve.n
        plan.add("scalar", lambda e, W=W: e.activation(out=r_ln[0:1, 0:W], in_=r_sq[0:1, 0:W], func=AF.Ln, bias=eps1[:], scale=1.0),
                 incs=[(ac, 1)], waits=[(ve, var_ve), (cstm, memset_done)])
        plan.add("scalar", lambda e, W=W: e.activation(out=rstd_bf[0:1, 0:W], in_=r_ln[0:1, 0:W], func=AF.Exp, scale=-0.5),
                 incs=[(ac, 1)], waits=[(ac, ac.n)])
        rstd_ac = ac.n
        plan.add("vector", lambda e, W=W: e.tensor_tensor(out=r_mr[0:1, 0:W], in0=r_sum[0:1, 0:W], in1=rstd_bf[0:1, 0:W], op=ALU.mult),
                 waits=[(ac, rstd_ac)])
        plan.add("vector", lambda e, W=W: e.tensor_copy(mr_bf[0:1, 0:W], r_mr[0:1, 0:W]), incs=[(ve, 1)])

    pending = list(unit_tasks(0))
    while pending:
        do_task(pending.pop(0))

    for ch in range(NCH):
        s = ch % 2
        plan.add("vector", lambda e: e.tensor_copy(r_sum[0:1, 0:512], ps_ssum[0:1, :]),
                 waits=[(te, st["statmm_te"])])
        plan.add("vector", lambda e: e.tensor_copy(r_sq[0:1, 0:512], ps_ssq[0:1, :]),
                 incs=[(ve, 1)])
        st["stat_rows_ve"] = ve.n
        stats_math(512)
        rows_ve = ve.n
        plan.add("tensor", lambda e: e.matmul(ps_bcr[:], lhsT=ones_row[:], rhs=rstd_bf[0:1, 0:512], start=True, stop=True),
                 waits=[(ve, rows_ve), (ve, st["bc_evac_ve"])])
        plan.add("tensor", lambda e: e.matmul(ps_bcm[:], lhsT=ones_row[:], rhs=mr_bf[0:1, 0:512], start=True, stop=True),
                 incs=[(te, 1)])
        bc_te = te.n
        plan.add("vector", lambda e: e.tensor_copy(rrep[:], ps_bcr[:]), waits=[(te, bc_te)])
        plan.add("vector", lambda e: e.tensor_copy(mrrep[:], ps_bcm[:]), incs=[(ve, 1)])
        st["bc_evac_ve"] = ve.n

        xb_qkv = list(st["xb_ld"])
        if ch + 1 < NCH:
            pending = list(unit_tasks(ch + 1))

        for ct in range(NCT):
            wi = ct % 2
            if ch == 0:
                plan.add("gpsimd",
                         lambda g, ct=ct, wi=wi: g.dma_start(
                             out=wsl[wi][:], in_=wqkv_d[ct].rearrange("p t c -> p (t c)")),
                         incs=wlp.inc(wi), waits=[(te, st["wsl_free"][wi])])
                lded = wlp.at(wi)
                plan.add("gpsimd",
                         lambda g, ct=ct, wi=wi: g.dma_start(
                             out=wbf_d[ct].rearrange("p t c -> p (t c)"), in_=wsl[wi][:]),
                         incs=[(wbs, 16)], waits=[lded])
            else:
                w = [(te, st["wsl_free"][wi])]
                if ch == 1 and ct in (0, 1):
                    w.append((wbs, 16 * NCT))
                plan.add("gpsimd",
                         lambda g, ct=ct, wi=wi: g.dma_start(
                             out=wsl[wi][:], in_=wbf_d[ct].rearrange("p t c -> p (t c)")),
                         incs=wlp.inc(wi), waits=w)
            slab_ld = wlp.at(wi)
            pi = ct % 2
            for dt in range(NDT):
                w = [xb_qkv[dt // 4]] if dt % 4 == 0 else []
                if dt == 0:
                    w = [slab_ld, (ve, st["qkv_evac_ve"][pi]), xb_qkv[0]]
                plan.add("tensor",
                         lambda e, s=s, wi=wi, dt=dt, pi=pi: e.matmul(
                             ps_qkv[pi][:], lhsT=wsl[wi][:, dt * 128:(dt + 1) * 128],
                             rhs=xbf[s][:, dt * CHT:(dt + 1) * CHT],
                             start=(dt == 0), stop=(dt == NDT - 1)),
                         waits=w, incs=[(te, 1)] if dt == NDT - 1 else [])
            mm_te = te.n
            if ch == 0:
                wci = ct % 2
                for dt in range(NDT):
                    plan.add("tensor",
                             lambda e, wi=wi, dt=dt, wci=wci: e.matmul(
                                 ps_wcs[wci][:], lhsT=wsl[wi][:, dt * 128:(dt + 1) * 128],
                                 rhs=ones_col[:], start=(dt == 0), stop=(dt == NDT - 1)),
                             waits=[(ve, st["wcs_ve"][ct - 2])] if (dt == 0 and ct >= 2) else [],
                             incs=[(te, 1)] if dt == NDT - 1 else [])
                myte = te.n
                plan.add("vector",
                         lambda e, ct=ct, wci=wci: e.tensor_copy(wcs_sb[:, ct:ct + 1], ps_wcs[wci][:]),
                         incs=[(ve, 1)], waits=[(te, myte)])
                st["wcs_ve"][ct] = ve.n
            st["wsl_free"][wi] = te.n
            ti = ct % 2
            w_ev = [(te, mm_te), (ve, st["wcs_ve"][ct])]
            plan.add("vector",
                     lambda e, pi=pi, ti=ti: e.tensor_tensor(
                         out=tmpe[ti][:], in0=ps_qkv[pi][:], in1=rrep[:], op=ALU.mult),
                     waits=w_ev)
            plan.add("vector",
                     lambda e, ct=ct, ti=ti: e.tensor_scalar(
                         out=tmpe[1 - ti][:], in0=mrrep[:], scalar1=wcs_sb[:, ct:ct + 1],
                         scalar2=None, op0=ALU.mult))
            if ct <= 7:
                qi = ct % 2
                plan.add("vector",
                         lambda e, qi=qi, ti=ti: e.tensor_tensor(
                             out=qstage[qi][:], in0=tmpe[ti][:], in1=tmpe[1 - ti][:], op=ALU.subtract),
                         incs=[(ve, 1)], waits=[qsp.at(qi)])
                st["qkv_evac_ve"][pi] = ve.n
                myve = ve.n
                plan.add("sync",
                         lambda e, ct=ct, ch=ch, qi=qi: e.dma_start(
                             out=qscr_d[ct, :, ch * CHT:(ch + 1) * CHT], in_=qstage[qi][:]),
                         incs=qsp.inc(qi), waits=[(ve, myve)])
            else:
                dsttn = kt_sb if ct <= 9 else vt_sb
                off = (ct - 8 if ct <= 9 else ct - 10) * SEQ + ch * CHT
                plan.add("vector",
                         lambda e, dsttn=dsttn, off=off, ti=ti: e.tensor_tensor(
                             out=dsttn[:, off:off + CHT], in0=tmpe[ti][:], in1=tmpe[1 - ti][:],
                             op=ALU.subtract),
                         incs=[(ve, 1)])
                st["qkv_evac_ve"][pi] = ve.n
            for _ in range(6):
                if pending:
                    do_task(pending.pop(0))
        st["slab_free"][s] = te.n
        while pending:
            do_task(pending.pop(0))

    teA, veA, acA = te.n, ve.n, ac.n
    qscr_done = qsp.totals()
    esA.close()
    psA.close()

    if debug == "A":
        plan.add("sync", lambda e: e.dma_start(out=dbg["dkv"][:, 0:2 * SEQ], in_=kt_sb[:]),
                 incs=[(dbs, 16)], waits=[(ve, veA)] + qscr_done)
        plan.add("sync", lambda e: e.dma_start(out=dbg["dkv"][:, 2 * SEQ:], in_=vt_sb[:]), incs=[(dbs, 16)])
        for k, row in enumerate([r_sum, r_sq, r_ln, r_sum]):
            plan.add("sync", lambda e, k=k, row=row: e.dma_start(out=dbg["drow"][k:k + 1, :], in_=row[0:1, :]),
                     incs=[(dbs, 16)])
        plan.add("sync", lambda e, v=dbs.n: e.wait_ge(dbs.h, v))
        return _finish(nc, plan, [es, esAB])

    esB = ExitStack()
    expT = [esB.enter_context(nc.sbuf_tensor(f"expT{i}", [128, NDT * 512], BF16)) for i in range(2)]
    vp_sb = esB.enter_context(nc.sbuf_tensor("vp_sb", [128, 2 * 32 * 129], BF16))
    qtb = [esB.enter_context(nc.sbuf_tensor(f"qtb{i}", [128, 512], BF16)) for i in range(2)]
    out_qc = [esB.enter_context(nc.sbuf_tensor(f"outqc{i}", [128, 4096], BF16)) for i in range(4)]
    masks_sb = esB.enter_context(nc.sbuf_tensor("masks_sb", [128, 4 * 512], BF16))
    rec_sb = esB.enter_context(nc.sbuf_tensor("rec_sb", [128, 4], F32))

    plan.add("gpsimd", lambda g: g.memset(vp_sb[:], 1.0), incs=[(vpm, 1)], waits=[(ve, veA)])
    plan.add("gpsimd", lambda g: g.dma_start(out=masks_sb[:].rearrange("p (j q) -> p j q", j=4), in_=masks_d),
             incs=[(mks, 16)])
    mks_ld = mks.n

    psT = ExitStack()
    ps_tr = [psT.enter_context(nc.psum_tensor(f"ps_tr{i}", [128, 128], BF16)) for i in range(2)]
    tr_free = [0, 0]
    for g2 in range(2):
        for kt in range(32):
            i = (g2 * 32 + kt) % 2
            w = [(ve, tr_free[i])]
            if g2 == 0 and kt == 0:
                w += [(te, teA), (ve, veA), (cst, const_ld)]
            plan.add("tensor",
                     lambda e, g2=g2, kt=kt, i=i: e.transpose(
                         ps_tr[i][:], in_=vt_sb[:, g2 * SEQ + kt * 128: g2 * SEQ + (kt + 1) * 128],
                         identity=identb[:]),
                     incs=[(te, 1)], waits=w)
            myte = te.n
            plan.add("vector",
                     lambda e, g2=g2, kt=kt, i=i: e.tensor_copy(
                         vp_sb[:, (g2 * 32 + kt) * 129:(g2 * 32 + kt) * 129 + 128], ps_tr[i][:]),
                     incs=[(ve, 1)], waits=[(te, myte), (mks, mks_ld), (vpm, 1)])
            tr_free[i] = ve.n
    psT.close()
    vbuild_ve = ve.n
    vbuild_te = te.n

    for dtp in range(2):
        plan.add("gpsimd",
                 lambda g, dtp=dtp: g.dma_start(
                     out=wosl[dtp][:], in_=wo_d[dtp].rearrange("p t c -> p (t c)")),
                 incs=wop.inc(dtp), waits=[(ve, veA)])

    psB = ExitStack()
    ps_sc = [psB.enter_context(nc.psum_tensor(f"ps_sc{i}", [128, 1536], F32)) for i in range(2)]
    ps_av = [psB.enter_context(nc.psum_tensor(f"ps_av{i}", [128, 258], F32)) for i in range(2)]

    bst = dict(sc_free=[0, 0], av_free=[0, 0], qtb_free=[0, 0], gidx=0)
    units = [(qc, lh) for qc in range(NCH) for lh in range(8)]
    exp_ac = {}
    mask_ve = {}
    av_done = {-2: 0, -1: 0}

    def gen_scores(ui):
        qc, lh = units[ui]
        g2 = lh // 4
        nkt = 4 * (qc + 1)
        ei = ui % 2
        qi = ui % 2
        w = [(te, bst["qtb_free"][qi])]
        if ui in (0, 1):
            w += qscr_done
        plan.add("sync",
                 lambda e, qc=qc, lh=lh, qi=qi: e.dma_start(
                     out=qtb[qi][:], in_=qscr_d[lh, :, qc * 512:(qc + 1) * 512]),
                 incs=qtp.inc(qi), waits=w)
        qt_ld = qtp.at(qi)
        exp_ac[ui] = [0] * nkt
        mask_ve[ui] = [0] * 4
        kt0 = 0
        while kt0 < nkt:
            gn = min(3, nkt - kt0)
            gi = bst["gidx"] % 2
            bst["gidx"] += 1
            for j in range(gn):
                kt = kt0 + j
                w = []
                if j == 0:
                    w = [(ac, bst["sc_free"][gi]), qt_ld]
                    if ui == 0 and kt0 == 0:
                        w.append((te, vbuild_te))
                plan.add("tensor",
                         lambda e, g2=g2, kt=kt, j=j, gi=gi, qi=qi: e.matmul(
                             ps_sc[gi][:, j * 512:(j + 1) * 512],
                             lhsT=kt_sb[:, g2 * SEQ + kt * 128: g2 * SEQ + (kt + 1) * 128],
                             rhs=qtb[qi][:], start=True, stop=True),
                         waits=w, incs=[(te, 1)] if j == gn - 1 else [])
            bst["qtb_free"][qi] = te.n
            grp_te = te.n
            plan.add("scalar",
                     lambda e, ei=ei, kt0=kt0, gn=gn, gi=gi: e.activation(
                         out=expT[ei][:, kt0 * 512:(kt0 + gn) * 512],
                         in_=ps_sc[gi][:, 0:gn * 512], func=AF.Exp,
                         bias=negc_sb[:], scale=ATT_SCALE),
                     incs=[(ac, 1)],
                     waits=[(te, max(grp_te, av_done[ui - 2])), (cstm, memset_done)])
            bst["sc_free"][gi] = ac.n
            for j in range(gn):
                exp_ac[ui][kt0 + j] = ac.n
            yield
            kt0 += gn
        for j in range(4):
            kt = 4 * qc + j
            plan.add("gpsimd",
                     lambda g, ei=ei, kt=kt, j=j: g.tensor_tensor(
                         out=expT[ei][:, kt * 512:(kt + 1) * 512],
                         in0=expT[ei][:, kt * 512:(kt + 1) * 512],
                         in1=masks_sb[:, j * 512:(j + 1) * 512], op=ALU.mult),
                     incs=[(vpm, 1)],
                     waits=[(ac, exp_ac[ui][kt]), (mks, mks_ld), (ve, vbuild_ve),
                            (te, av_done[ui - 2])])
            mask_ve[ui][j] = vpm.n
        yield

    def gen_av(ui):
        qc, lh = units[ui]
        g2 = lh // 4
        ei = ui % 2
        oi = qc % 4
        for qs in range(4):
            pi = qs // 2
            nkt_q = 4 * qc + qs + 1
            for kt in range(nkt_q):
                w = []
                if kt == 0:
                    w = [(ac, exp_ac[ui][nkt_q - 1]), (vpm, mask_ve[ui][qs]),
                         (ve, bst["av_free"][pi])]
                plan.add("tensor",
                         lambda e, ei=ei, kt=kt, qs=qs, g2=g2, pi=pi: e.matmul(
                             ps_av[pi][:, (qs % 2) * 129:(qs % 2) * 129 + 129],
                             lhsT=expT[ei][:, kt * 512 + qs * 128: kt * 512 + (qs + 1) * 128],
                             rhs=vp_sb[:, (g2 * 32 + kt) * 129:(g2 * 32 + kt + 1) * 129],
                             start=(kt == 0), stop=(kt == nkt_q - 1)),
                         waits=w, incs=[(te, 1)] if kt == nkt_q - 1 else [])
            avte = te.n
            if qs % 2 == 1:
                plan.add("vector",
                         lambda e, pi=pi: e.reciprocal(
                             out=rec_sb[:, 2 * pi:2 * pi + 2], in_=ps_av[pi][:, 128::129]),
                         incs=[(ve, 1)], waits=[(te, avte), oqp.at(oi)])
                myve = ve.n
                for sq in range(2):
                    qs2 = (qs - 1) + sq
                    plan.add("vector",
                             lambda e, pi=pi, sq=sq, qs2=qs2, lh=lh, oi=oi: e.tensor_scalar(
                                 out=out_qc[oi][:, (qs2 * 8 + lh) * 128:(qs2 * 8 + lh + 1) * 128],
                                 in0=ps_av[pi][:, sq * 129: sq * 129 + 128],
                                 scalar1=rec_sb[:, 2 * pi + sq:2 * pi + sq + 1],
                                 scalar2=None, op0=ALU.mult),
                             incs=[(ve, 1)] if sq == 1 else [],
                             waits=[(ve, myve)])
                bst["av_free"][pi] = ve.n
            yield
        av_done[ui] = te.n

    g_sc = gen_scores(0)
    for _ in g_sc:
        pass
    for ui in range(len(units)):
        g_av = gen_av(ui)
        g_next = gen_scores(ui + 1) if ui + 1 < len(units) else iter(())
        done_av = done_sc = False
        while not (done_av and done_sc):
            if not done_av:
                done_av = next(g_av, "end") == "end"
            if not done_sc:
                done_sc = next(g_next, "end") == "end"
            if not done_sc:
                done_sc = next(g_next, "end") == "end"
        qc, lh = units[ui]
        if lh == 7:
            oi = qc % 4
            myve = ve.n
            plan.add("sync",
                     lambda e, qc=qc, oi=oi: e.dma_start(
                         out=ccin_d[qc * 512:(qc + 1) * 512, :].rearrange("(s p) x -> p s x", s=4),
                         in_=out_qc[oi][:].rearrange("p (s x) -> p s x", s=4)),
                     incs=oqp.inc(oi), waits=[(ve, myve)])
            if debug != "B":
                plan.add("gpsimd",
                         lambda g, qc=qc: g.collective_compute(
                             "AllGather", ALU.bypass,
                             replica_groups=[[0, 1, 2, 3], [4, 5, 6, 7]],
                             ins=[ccin_d[qc * 512:(qc + 1) * 512, :].opt()],
                             outs=[ccout_d[qc * 2048:(qc + 1) * 2048, :].opt()]),
                         incs=[(ccs, 1)], waits=[oqp.at(oi)])

    teB, veB, acB = te.n, ve.n, ac.n
    psB.close()
    esB.close()
    esAB.close()

    if debug == "B":
        plan.add("sync", lambda e, w=tuple(oqp.totals()): [e.wait_ge(s.h, v) for s, v in w] and None)
        return _finish(nc, plan, [es])

    esC = ExitStack()
    ot_sb = esC.enter_context(nc.sbuf_tensor("ot_sb", [128, 32 * 1024], BF16))
    y_sb = esC.enter_context(nc.sbuf_tensor("y_sb", [128, 32 * 1024], BF16))
    xfc = esC.enter_context(nc.sbuf_tensor("xfc0", [128, 1024], F32))
    ysq0 = esC.enter_context(nc.sbuf_tensor("ysq0", [128, 1024], BF16))
    outf = [esC.enter_context(nc.sbuf_tensor(f"outf{i}", [128, 1024], F32)) for i in range(2)]
    stage = esC.enter_context(nc.sbuf_tensor("stage", [128, 4 * 1024], BF16))
    awr_sb = esC.enter_context(nc.sbuf_tensor("awr_sb", [128, NDT * 68], BF16))
    btil_sb = esC.enter_context(nc.sbuf_tensor("btil_sb", [64, DM], BF16))
    midlog = esC.enter_context(nc.sbuf_tensor("midlog", [68, 1024], F32))
    r2rep = esC.enter_context(nc.sbuf_tensor("r2rep", [68, 1024], BF16))
    mr2rep = esC.enter_context(nc.sbuf_tensor("mr2rep", [68, 1024], BF16))
    acs_sb = esC.enter_context(nc.sbuf_tensor("acs_sb", [68, 1], F32))
    grep_sb = esC.enter_context(nc.sbuf_tensor("grep_sb", [64, 1024], BF16))
    midg = grep_sb
    gatest = esC.enter_context(nc.sbuf_tensor("gatest", [4, 1024], BF16))
    lgn = [esC.enter_context(nc.sbuf_tensor(f"lgn{i}", [128, 4], F32)) for i in range(2)]
    g4bf = esC.enter_context(nc.sbuf_tensor("g4bf", [128, 4], BF16))
    gwork = esC.enter_context(nc.sbuf_tensor("gwork", [128, 8], F32))
    gtmp0 = esC.enter_context(nc.sbuf_tensor("gtmp0", [128, 8], F32))
    gtmp = [gtmp0[:, 0:4], gtmp0[:, 4:8]]

    plan.add("gpsimd", lambda g: g.dma_start(out=awr_sb[:].rearrange("p (t j) -> p t j", t=NDT), in_=awr_d),
             incs=[(cdw, 16)], waits=[(te, teB), (ve, veB)])
    plan.add("gpsimd", lambda g: g.dma_start(out=btil_sb[:], in_=btil_d), incs=[(cdw, 16)])
    cd_w = cdw.n

    for blk in range(32):
        i = blk % 4
        grp = blk // 4
        w = [(ccs, 8), cws.at((grp - 1) % 2)]
        plan.add("gpsimd",
                 lambda g, blk=blk, i=i: g.indirect_dma_start(
                     out=stage[:, i * 1024:(i + 1) * 1024], out_offset=None, in_=ccout_d,
                     in_offset=bass_rust.IndirectOffsetOnAxis(ap=gidx_sb[:, blk:blk + 1], axis=0)),
                 incs=stp.inc(blk % 2), waits=w)
        if blk % 4 == 3:
            wg = [stp.at(0), stp.at(1)]
            plan.add("sync",
                     lambda e, grp=grp: e.dma_start(
                         out=ccwin_d[grp * 512:(grp + 1) * 512, :]
                         .rearrange("(s p) c -> p s c", s=4),
                         in_=stage[:].rearrange("p (s c) -> p s c", s=4)),
                     incs=cws.inc(grp % 2), waits=wg)
        if blk % 8 == 7:
            r = blk // 8
            wr = tuple(cws.totals())
            for loc in range(8):
                ot = r * 8 + loc
                w = list(wr)
                if ot == 0:
                    w += [(te, teB), (ve, veB), (ac, acB)]
                plan.add("sync",
                         lambda e, ot=ot, r=r, loc=loc: e.dma_start(
                             out=ot_sb[:, ot * 1024:(ot + 1) * 1024],
                             in_=ccwin_d[r * 1024:(r + 1) * 1024, loc * 128:(loc + 1) * 128],
                             transpose=True),
                         incs=[(otl, 16)], waits=w)
    ot_done = otl.n

    if debug == "G":
        plan.add("sync", lambda e: e.dma_start(out=dbg["dy"], in_=ot_sb[:]), incs=[(dbs, 16)],
                 waits=[(otl, ot_done)])
        plan.add("sync", lambda e, v=dbs.n: e.wait_ge(dbs.h, v))
        return _finish(nc, plan, [es, esC])

    psC = ExitStack()
    ps_y = [psC.enter_context(nc.psum_tensor(f"ps_y{i}", [128, 1024], F32)) for i in range(2)]
    ps_cs = psC.enter_context(nc.psum_tensor("ps_cs", [1, 1024], F32))
    ps_cq = psC.enter_context(nc.psum_tensor("ps_cq", [1, 1024], F32))

    cst_ = dict(wo_free=[0, 0], xfc_free=0, ysq_free=0, y_evac_ve=[0, 0])
    for dt in range(NDT):
        i = dt % 2
        if dt >= 2:
            plan.add("gpsimd",
                     lambda g, dt=dt, i=i: g.dma_start(
                         out=wosl[i][:], in_=wo_d[dt].rearrange("p t c -> p (t c)")),
                     incs=wop.inc(i), waits=[(te, cst_["wo_free"][i])])
        wo_ld = wop.at(i)
        plan.add("sync",
                 lambda e, dt=dt: e.dma_start(
                     out=xfc[:], in_=xtc_d[dt * 128:(dt + 1) * 128, :]),
                 incs=xcp.inc(0), waits=[(ve, cst_["xfc_free"])])
        xfc_ld = xcp.at(0)
        for ot in range(32):
            for h in range(2):
                w = []
                if ot == 0 and h == 0:
                    w = [wo_ld, (otl, ot_done), (ve, cst_["y_evac_ve"][i])]
                plan.add("tensor",
                         lambda e, dt=dt, i=i, ot=ot, h=h: e.matmul(
                             ps_y[i][:, h * 512:(h + 1) * 512],
                             lhsT=wosl[i][:, ot * 128:(ot + 1) * 128],
                             rhs=ot_sb[:, ot * 1024 + h * 512: ot * 1024 + h * 512 + 512],
                             start=(ot == 0), stop=(ot == 31)),
                         waits=w, incs=[(te, 1)] if (ot == 31 and h == 1) else [])
        cst_["wo_free"][i] = te.n
        ymm_te = te.n
        plan.add("vector",
                 lambda e, dt=dt, i=i: e.tensor_tensor(
                     out=y_sb[:, dt * 1024:(dt + 1) * 1024], in0=ps_y[i][:], in1=xfc[:], op=ALU.add),
                 incs=[(ve, 1)], waits=[(te, ymm_te), xfc_ld])
        cst_["y_evac_ve"][i] = ve.n
        cst_["xfc_free"] = ve.n
        plan.add("vector",
                 lambda e, dt=dt: e.tensor_tensor(
                     out=ysq0[:], in0=y_sb[:, dt * 1024:(dt + 1) * 1024],
                     in1=y_sb[:, dt * 1024:(dt + 1) * 1024], op=ALU.mult),
                 incs=[(ve, 1)], waits=[(te, cst_["ysq_free"])])
        sq_ve = ve.n
        for h in range(2):
            plan.add("tensor",
                     lambda e, dt=dt, h=h: e.matmul(
                         ps_cs[0:1, h * 512:(h + 1) * 512], lhsT=ones_col[:],
                         rhs=y_sb[:, dt * 1024 + h * 512: dt * 1024 + h * 512 + 512],
                         start=(dt == 0), stop=(dt == NDT - 1)),
                     waits=[(ve, sq_ve)] if h == 0 else [])
            plan.add("tensor",
                     lambda e, dt=dt, h=h: e.matmul(
                         ps_cq[0:1, h * 512:(h + 1) * 512], lhsT=ones_col[:],
                         rhs=ysq0[:, h * 512:(h + 1) * 512],
                         start=(dt == 0), stop=(dt == NDT - 1)),
                     incs=[(te, 1)] if h == 1 else [])
        cst_["ysq_free"] = te.n
    cstat_te = te.n

    plan.add("vector", lambda e: e.tensor_copy(r_sum[0:1, :], ps_cs[0:1, :]), waits=[(te, cstat_te)])
    plan.add("vector", lambda e: e.tensor_copy(r_sq[0:1, :], ps_cq[0:1, :]), incs=[(ve, 1)])
    stats_math(1024)
    rows2_ve = ve.n
    psC.close()

    if debug == "C":
        plan.add("sync", lambda e: e.dma_start(out=dbg["dy"], in_=y_sb[:]), incs=[(dbs, 16)],
                 waits=[(ve, rows2_ve)])
        for k, row in enumerate([r_sum, r_sq, r_ln, r_sum]):
            plan.add("sync", lambda e, k=k, row=row: e.dma_start(out=dbg["drow"][k:k + 1, :], in_=row[0:1, :]),
                     incs=[(dbs, 16)])
        plan.add("sync", lambda e, v=dbs.n: e.wait_ge(dbs.h, v))
        return _finish(nc, plan, [es, esC])

    psD1 = ExitStack()
    ps_mid = psD1.enter_context(nc.psum_tensor("ps_mid", [68, 1024], F32))
    ps_acs = psD1.enter_context(nc.psum_tensor("ps_acs", [68, 1], F32))
    ps_b68 = psD1.enter_context(nc.psum_tensor("ps_b68", [68, 1024], F32))

    for dt in range(NDT):
        plan.add("tensor",
                 lambda e, dt=dt: e.matmul(
                     ps_acs[:], lhsT=awr_sb[:, dt * 68:(dt + 1) * 68], rhs=ones_col[:],
                     start=(dt == 0), stop=(dt == NDT - 1)),
                 waits=[(cdw, cd_w), (te, cstat_te), (ve, rows2_ve)] if dt == 0 else [])
        for h in range(2):
            plan.add("tensor",
                     lambda e, dt=dt, h=h: e.matmul(
                         ps_mid[:, h * 512:(h + 1) * 512],
                         lhsT=awr_sb[:, dt * 68:(dt + 1) * 68],
                         rhs=y_sb[:, dt * 1024 + h * 512: dt * 1024 + h * 512 + 512],
                         start=(dt == 0), stop=(dt == NDT - 1)),
                     incs=[(te, 1)] if (dt == NDT - 1 and h == 1) else [])
    mid_te = te.n
    plan.add("vector", lambda e: e.tensor_copy(acs_sb[:], ps_acs[:]), incs=[(ve, 1)],
             waits=[(te, mid_te)])
    acs_ve = ve.n

    for h in range(2):
        plan.add("tensor",
                 lambda e, h=h: e.matmul(
                     ps_b68[:, h * 512:(h + 1) * 512], lhsT=ones_row[0:1, 0:68],
                     rhs=rstd_bf[0:1, h * 512:(h + 1) * 512], start=True, stop=True),
                 waits=[(ve, rows2_ve)] if h == 0 else [], incs=[(te, 1)] if h == 1 else [])
    plan.add("vector", lambda e: e.tensor_copy(r2rep[:], ps_b68[:]), incs=[(ve, 1)],
             waits=[(te, te.n)])
    r2_ve = ve.n
    for h in range(2):
        plan.add("tensor",
                 lambda e, h=h: e.matmul(
                     ps_b68[:, h * 512:(h + 1) * 512], lhsT=ones_row[0:1, 0:68],
                     rhs=mr_bf[0:1, h * 512:(h + 1) * 512], start=True, stop=True),
                 waits=[(ve, r2_ve)] if h == 0 else [], incs=[(te, 1)] if h == 1 else [])
    plan.add("vector", lambda e: e.tensor_copy(mr2rep[:], ps_b68[:]), incs=[(ve, 1)],
             waits=[(te, te.n)])

    plan.add("vector", lambda e: e.tensor_tensor(out=midlog[:], in0=ps_mid[:], in1=r2rep[:], op=ALU.mult),
             waits=[(ve, ve.n)])
    plan.add("vector", lambda e: e.tensor_scalar(out=ysq0[0:68, :], in0=mr2rep[:], scalar1=acs_sb[:],
                                                 scalar2=None, op0=ALU.mult),
             waits=[(ve, acs_ve)])
    plan.add("vector", lambda e: e.tensor_tensor(out=midlog[:], in0=midlog[:], in1=ysq0[0:68, :], op=ALU.subtract),
             incs=[(ve, 1)])
    midlog_ve = ve.n
    psD1.close()

    psD2 = ExitStack()
    ps_tr4 = psD2.enter_context(nc.psum_tensor("ps_tr4", [128, 4], F32))
    ps_gt = psD2.enter_context(nc.psum_tensor("ps_gt", [4, 128], BF16))
    ps_grep = psD2.enter_context(nc.psum_tensor("ps_grep", [64, 1024], F32))
    lgn_free = [0, 0]
    tr4_free = 0
    gt_copy_ve = 0
    prev_gt_te = 0
    for tc in range(8):
        i = tc % 2
        w = [(ve, lgn_free[i]), (ve, tr4_free)]
        if tc == 0:
            w.append((ve, midlog_ve))
        plan.add("tensor",
                 lambda e, tc=tc: e.transpose(
                     ps_tr4[:], in_=midlog[64:68, tc * 128:(tc + 1) * 128], identity=identf[64:68, 64:68]),
                 incs=[(te, 1)], waits=w)
        myte = te.n
        plan.add("vector", lambda e, i=i: e.tensor_copy(lgn[i][:], ps_tr4[:]), incs=[(ve, 1)],
                 waits=[(te, myte)])
        tr4_free = ve.n

        def V(fn, waits=()):
            plan.add("vector", fn, incs=[(ve, 1)], waits=list(waits) + [(ve, ve.n)])
        V(lambda e, i=i: e.tensor_reduce(out=gwork[:, 0:1], in_=lgn[i][:], axis=AX.X, op=ALU.max))
        m1_ve = ve.n
        V(lambda e: e.tensor_scalar(out=gwork[:, 1:2], in0=gwork[:, 0:1], scalar1=-1.0, scalar2=None, op0=ALU.mult))
        negm1_ve = ve.n
        plan.add("scalar",
                 lambda e, i=i: e.activation(out=gtmp[0], in_=lgn[i][:], func=AF.Exp, bias=gwork[:, 1:2], scale=1.0),
                 incs=[(ac, 1)], waits=[(ve, negm1_ve)])
        e4_ac = ac.n
        V(lambda e, i=i: e.tensor_scalar(out=gtmp[1], in0=lgn[i][:], scalar1=gwork[:, 0:1], scalar2=BIG,
                                         op0=ALU.is_equal, op1=ALU.mult))
        V(lambda e, i=i: e.tensor_tensor(out=gtmp[1], in0=lgn[i][:], in1=gtmp[1], op=ALU.subtract))
        V(lambda e: e.tensor_reduce(out=gwork[:, 2:3], in_=gtmp[1], axis=AX.X, op=ALU.max))
        V(lambda e, i=i: e.tensor_scalar(out=gtmp[1], in0=lgn[i][:], scalar1=gwork[:, 2:3], scalar2=None,
                                         op0=ALU.is_ge))
        lgn_free[i] = ve.n
        V(lambda e: e.tensor_tensor(out=gtmp[0], in0=gtmp[0], in1=gtmp[1], op=ALU.mult),
          waits=[(ac, e4_ac)])
        V(lambda e: e.tensor_reduce(out=gwork[:, 3:4], in_=gtmp[0], axis=AX.X, op=ALU.add))
        V(lambda e: e.reciprocal(out=gwork[:, 4:5], in_=gwork[:, 3:4]))
        V(lambda e: e.tensor_scalar(out=g4bf[:], in0=gtmp[0], scalar1=gwork[:, 4:5], scalar2=4.0,
                                    op0=ALU.mult, op1=ALU.mult),
          waits=[(te, prev_gt_te)])
        g4_ve = ve.n
        plan.add("tensor",
                 lambda e: e.transpose(ps_gt[:], in_=g4bf[:], identity=identb[:]),
                 incs=[(te, 1)], waits=[(ve, g4_ve), (ve, gt_copy_ve)])
        prev_gt_te = te.n
        plan.add("vector", lambda e, tc=tc: e.tensor_copy(gatest[:, tc * 128:(tc + 1) * 128], ps_gt[:]),
                 incs=[(ve, 1)], waits=[(te, prev_gt_te)])
        gt_copy_ve = ve.n
    gates_ve = ve.n

    for h in range(2):
        plan.add("tensor",
                 lambda e, h=h: e.matmul(
                     ps_grep[:, h * 512:(h + 1) * 512], lhsT=pmat_sb[:],
                     rhs=gatest[0:4, h * 512:(h + 1) * 512], start=True, stop=True),
                 waits=[(ve, gates_ve), (cst, const_ld)] if h == 0 else [],
                 incs=[(te, 1)] if h == 1 else [])
    plan.add("vector", lambda e: e.tensor_copy(grep_sb[:], ps_grep[:]), incs=[(ve, 1)],
             waits=[(te, te.n)])
    if debug == "D":
        plan.add("sync", lambda e: e.dma_start(out=dbg["dgw"], in_=gwork[:]), incs=[(dbs, 16)],
                 waits=[(ve, gates_ve)])
        plan.add("sync", lambda e: e.dma_start(out=dbg["dlg"], in_=lgn[1][:]), incs=[(dbs, 16)])
        plan.add("sync", lambda e: e.dma_start(out=dbg["dg0"], in_=gtmp[0]), incs=[(dbs, 16)])
        plan.add("sync", lambda e: e.dma_start(out=dbg["dg1"], in_=gtmp[1]), incs=[(dbs, 16)])
        plan.add("sync", lambda e: e.dma_start(out=dbg["dml"], in_=midlog[:]), incs=[(dbs, 16)],
                 waits=[(ve, midlog_ve)])
        plan.add("sync", lambda e: e.dma_start(out=dbg["dgt"], in_=gatest[:]), incs=[(dbs, 16)],
                 waits=[(ve, gates_ve)])
        plan.add("sync", lambda e: e.dma_start(out=dbg["dgr"], in_=grep_sb[:]), incs=[(dbs, 16)],
                 waits=[(ve, ve.n)])
        plan.add("sync", lambda e: e.dma_start(out=dbg["dac"], in_=acs_sb[:]), incs=[(dbs, 16)],
                 waits=[(ve, acs_ve)])
        plan.add("sync", lambda e: e.dma_start(out=dbg["dr2"][:, 0:1024], in_=r2rep[:]), incs=[(dbs, 16)])
        plan.add("sync", lambda e: e.dma_start(out=dbg["dr2"][:, 1024:2048], in_=mr2rep[:]), incs=[(dbs, 16)])
    plan.add("vector", lambda e: e.tensor_tensor(out=midg[:], in0=midlog[0:64, :], in1=grep_sb[:], op=ALU.mult),
             incs=[(ve, 1)])
    midg_ve = ve.n
    psD2.close()

    psD3 = ExitStack()
    ps_d = [psD3.enter_context(nc.psum_tensor(f"ps_d{i}", [128, 1024], F32)) for i in range(2)]
    d_free = [0, 0]
    for dt in range(NDT):
        i = dt % 2
        for h in range(2):
            w = []
            if h == 0:
                w = [(ve, midg_ve), (ve, d_free[i])]
            plan.add("tensor",
                     lambda e, dt=dt, i=i, h=h: e.matmul(
                         ps_d[i][:, h * 512:(h + 1) * 512],
                         lhsT=btil_sb[:, dt * 128:(dt + 1) * 128],
                         rhs=midg[:, h * 512:(h + 1) * 512], start=True, stop=True),
                     waits=w, incs=[(te, 1)] if h == 1 else [])
        dte = te.n
        plan.add("vector",
                 lambda e, dt=dt, i=i: e.tensor_tensor(
                     out=outf[i][:], in0=ps_d[i][:], in1=y_sb[:, dt * 1024:(dt + 1) * 1024], op=ALU.add),
                 incs=[(ve, 1)], waits=[(te, dte), ofp.at(i)])
        d_free[i] = ve.n
        myve = ve.n
        plan.add("sync",
                 lambda e, dt=dt, i=i: e.dma_start(out=yt_d[dt * 128:(dt + 1) * 128, :], in_=outf[i][:]),
                 incs=ofp.inc(i), waits=[(ve, myve)])

    fin = tuple(ofp.totals())
    plan.add("sync", lambda e, w=fin: [e.wait_ge(s.h, v) for s, v in w] and None)
    psD3.close()
    return _finish(nc, plan, [es, esC])


def _finish(nc, plan, stacks):
    block_cm = nc.Block()
    block = block_cm.__enter__()
    for eng in ENGINES:
        lst = plan.ops[eng]
        if not lst:
            continue
        dec = getattr(block, eng)

        def section(e, lst=lst):
            hwm = {}
            for waits, fn, incs in lst:
                for s, v in waits:
                    if v > hwm.get(s.name, 0):
                        e.wait_ge(s.h, v)
                        hwm[s.name] = v
                inst = fn(e) if fn is not None else None
                for s, k in incs:
                    inst.then_inc(s.h, k)
        dec(section)
    block_cm.__exit__(None, None, None)
    for stk in stacks[::-1]:
        try:
            stk.close()
        except Exception:
            pass
    return nc


_CACHE = {}


def _host_prep(L0_attn_x, L0_attn_wq, L0_attn_wk, L0_attn_wv, L0_attn_wo,
               L0_ffn_A, L0_ffn_B, L0_ffn_Wr):
    x = np.asarray(L0_attn_x, np.float32)
    wq = np.asarray(L0_attn_wq, np.float32)
    wk = np.asarray(L0_attn_wk, np.float32)
    wv = np.asarray(L0_attn_wv, np.float32)
    wo = np.asarray(L0_attn_wo, np.float32)
    A = np.asarray(L0_ffn_A, np.float32)
    B = np.asarray(L0_ffn_B, np.float32)
    Wr = np.asarray(L0_ffn_Wr, np.float32)

    xts = [np.ascontiguousarray(x[b].T) for b in range(2)]
    wo_arr = np.ascontiguousarray(wo.reshape(32, 128, 32, 128).transpose(2, 1, 0, 3))
    awr = np.concatenate([A.transpose(2, 0, 1).reshape(DM, 64), Wr.T], axis=1)
    awr_arr = np.ascontiguousarray(awr.reshape(NDT, 128, 68).transpose(1, 0, 2))
    btil_arr = np.ascontiguousarray(B.transpose(0, 2, 1).reshape(64, DM))
    masks = (np.arange(512)[None, None, :] >=
             (128 * np.arange(4)[None, :, None] + np.arange(128)[:, None, None])).astype(bfnp)
    identb = np.eye(128, dtype=bfnp)
    identf = np.eye(128, dtype=np.float32)
    pmat = (np.arange(64)[None, :] // 16 == np.arange(4)[:, None]).astype(bfnp)

    in_maps = []
    for core in range(8):
        b, g = core // 4, core % 4
        wcat = np.concatenate([wq[:, g * 1024:(g + 1) * 1024],
                               wk[:, g * 256:(g + 1) * 256],
                               wv[:, g * 256:(g + 1) * 256]], axis=1)
        wqkv_arr = np.ascontiguousarray(wcat.reshape(NDT, 128, NCT, 128).transpose(2, 1, 0, 3))
        xtc = np.ascontiguousarray(xts[b][:, g * TOK:(g + 1) * TOK])
        p = np.arange(128)[:, None]
        blk = np.arange(32)[None, :]
        s = blk % 8
        r = blk // 8
        gidx = ((2 * g + s // 4) * 2048 + r * 512 + (s % 4) * 128 + p).astype(np.int32)
        in_maps.append({
            "xt": xts[b], "xtc": xtc, "wqkv": wqkv_arr, "wot": wo_arr,
            "awr": awr_arr, "btil": btil_arr, "masks": masks,
            "identb": identb, "identf": identf, "pmat": pmat, "gidx": gidx,
        })
    return in_maps


def kernel(**inputs):
    in_maps = _host_prep(**inputs)
    if "nc" not in _CACHE:
        _CACHE["nc"] = build_program(DEBUG_PHASE)
    nc = _CACHE["nc"]
    res = run_bass_kernel_spmd(nc, in_maps, core_ids=list(range(8)))
    out = np.empty((2, SEQ, DM), np.float32)
    for core in range(8):
        b, g = core // 4, core % 4
        out[b, g * TOK:(g + 1) * TOK, :] = np.asarray(res.results[core]["yt"]).T
    return out


# version 10
# speedup vs baseline: 1.0692x; 1.0031x over previous
import numpy as np
import ml_dtypes
import bass_rust
import concourse.bass as bass
import concourse.mybir as mybir
from concourse.bass_utils import run_bass_kernel_spmd
from contextlib import ExitStack

F32 = mybir.dt.float32
BF16 = mybir.dt.bfloat16
I32 = mybir.dt.int32
AF = mybir.ActivationFunctionType
ALU = mybir.AluOpType
AX = mybir.AxisListType
bfnp = ml_dtypes.bfloat16

SEQ = 4096
DM = 4096
NDT = DM // 128
NCH = 8
CHT = SEQ // NCH
NCT = 12
TOK = 1024
ATT_SCALE = float(128.0 ** -0.5)
NEGC = -12.0
EPS = 1e-5
BIG = 1e30
ENGINES = ("tensor", "vector", "scalar", "sync", "gpsimd")

DEBUG_PHASE = None


class Sem:
    def __init__(self, name):
        self.name = name
        self.h = None
        self.n = 0


class Pool:
    def __init__(self, es, nc, name, n):
        self.sems = []
        for i in range(n):
            s = Sem(f"{name}{i}")
            s.h = es.enter_context(nc.semaphore(s.name))
            self.sems.append(s)

    def inc(self, i):
        return [(self.sems[i], 16)]

    def at(self, i):
        return (self.sems[i], self.sems[i].n)

    def totals(self):
        return [(s, s.n) for s in self.sems]


class Plan:
    def __init__(self):
        self.ops = {e: [] for e in ENGINES}

    def add(self, eng, fn=None, incs=(), waits=()):
        waits = tuple((s, v) for (s, v) in waits if v > 0)
        incs = tuple(incs)
        for s, k in incs:
            s.n += k
        self.ops[eng].append((waits, fn, incs))


def build_program(debug=None):
    nc = bass.Bass("TRN2", target_bir_lowering=False, debug=False)

    xt_d = nc.dram_tensor("xt", [DM, SEQ], F32, kind="ExternalInput").ap()
    xtc_d = nc.dram_tensor("xtc", [DM, TOK], F32, kind="ExternalInput").ap()
    wqkv_d = nc.dram_tensor("wqkv", [NCT, 128, NDT, 128], F32, kind="ExternalInput").ap()
    wo_d = nc.dram_tensor("wot", [NDT, 128, 32, 128], F32, kind="ExternalInput").ap()
    awr_d = nc.dram_tensor("awr", [128, NDT, 68], F32, kind="ExternalInput").ap()
    btil_d = nc.dram_tensor("btil", [64, DM], F32, kind="ExternalInput").ap()
    masks_d = nc.dram_tensor("masks", [128, 4, 512], BF16, kind="ExternalInput").ap()
    identb_d = nc.dram_tensor("identb", [128, 128], BF16, kind="ExternalInput").ap()
    identf_d = nc.dram_tensor("identf", [128, 128], F32, kind="ExternalInput").ap()
    pmat_d = nc.dram_tensor("pmat", [4, 64], BF16, kind="ExternalInput").ap()
    gidx_d = nc.dram_tensor("gidx", [128, 32], I32, kind="ExternalInput").ap()
    yt_d = nc.dram_tensor("yt", [DM, TOK], F32, kind="ExternalOutput").ap()

    wbf_d = nc.dram_tensor("wbf", [NCT, 128, NDT, 128], BF16, kind="Internal").ap()
    qscr_k = "ExternalOutput" if debug == "A" else "Internal"
    qscr_d = nc.dram_tensor("qscr", [8, 128, SEQ], BF16, kind=qscr_k).ap()
    ccin_k = "ExternalOutput" if debug == "B" else "Internal"
    ccin_d = nc.dram_tensor("ccin", [SEQ, 1024], BF16, kind=ccin_k).ap()
    ccout_d = nc.dram_tensor("ccout", [4 * SEQ, 1024], BF16, kind="Internal").ap()
    ccwin_k = "ExternalOutput" if debug == "G" else "Internal"
    ccwin_d = nc.dram_tensor("ccwin", [4 * TOK, 1024], BF16, kind=ccwin_k).ap()
    dbg = {}
    if debug == "A":
        dbg["dkv"] = nc.dram_tensor("dkv", [128, 4 * SEQ], BF16, kind="ExternalOutput").ap()
    if debug in ("A", "C"):
        dbg["drow"] = nc.dram_tensor("drow", [4, 1024], F32, kind="ExternalOutput").ap()
    if debug in ("C", "G"):
        dbg["dy"] = nc.dram_tensor("dy", [128, 32 * 1024], BF16, kind="ExternalOutput").ap()
    if debug == "D":
        dbg["dml"] = nc.dram_tensor("dml", [68, 1024], F32, kind="ExternalOutput").ap()
        dbg["dgt"] = nc.dram_tensor("dgt", [4, 1024], BF16, kind="ExternalOutput").ap()
        dbg["dgr"] = nc.dram_tensor("dgr", [64, 1024], BF16, kind="ExternalOutput").ap()
        dbg["dac"] = nc.dram_tensor("dac", [68, 1], F32, kind="ExternalOutput").ap()
        dbg["dr2"] = nc.dram_tensor("dr2", [68, 2048], BF16, kind="ExternalOutput").ap()
        dbg["dgw"] = nc.dram_tensor("dgw", [128, 8], F32, kind="ExternalOutput").ap()
        dbg["dlg"] = nc.dram_tensor("dlg", [128, 4], F32, kind="ExternalOutput").ap()
        dbg["dg0"] = nc.dram_tensor("dg0", [128, 4], F32, kind="ExternalOutput").ap()
        dbg["dg1"] = nc.dram_tensor("dg1", [128, 4], F32, kind="ExternalOutput").ap()

    plan = Plan()
    es = ExitStack()

    def sb(name, shape, dt):
        return es.enter_context(nc.sbuf_tensor(name, shape, dt))

    sems = [Sem(n) for n in ("te", "ve", "ac", "ccs", "cst", "cstm", "mks", "vpm",
                             "cdw", "wbs", "cwp", "otl", "dbs")]
    te, ve, ac, ccs, cst, cstm, mks, vpm, cdw, wbs, cwp, otl, dbs = sems
    for s in sems:
        s.h = es.enter_context(nc.semaphore(s.name))
    xbp = Pool(es, nc, "xbp", 16)
    wlp = Pool(es, nc, "wlp", 2)
    qsp = Pool(es, nc, "qsp", 2)
    qtp = Pool(es, nc, "qtp", 2)
    oqp = Pool(es, nc, "oqp", 4)
    stp = Pool(es, nc, "stp", 2)
    wop = Pool(es, nc, "wop", 2)
    xcp = Pool(es, nc, "xcp", 1)
    ofp = Pool(es, nc, "ofp", 2)
    cws = Pool(es, nc, "cws", 2)

    ones_col = sb("ones_col", [128, 1], BF16)
    ones_row = sb("ones_row", [1, 128], BF16)
    negc_sb = sb("negc", [128, 1], F32)
    eps1 = sb("eps1", [1, 1], F32)
    r_sum = sb("r_sum", [1, 1024], F32)
    r_sq = sb("r_sq", [1, 1024], F32)
    r_ln = sb("r_ln", [1, 1024], F32)
    r_mr = r_ln
    rstd_bf = sb("rstd_bf", [1, 1024], BF16)
    mr_bf = sb("mr_bf", [1, 1024], BF16)
    wcs_sb = sb("wcs", [128, 12], F32)
    identb = sb("identb_sb", [128, 128], BF16)
    identf = sb("identf_sb", [128, 128], F32)
    pmat_sb = sb("pmat_sb", [4, 64], BF16)
    gidx_sb = sb("gidx_sb", [128, 32], I32)
    wosl = [sb(f"wosl{i}", [128, 32 * 128], BF16) for i in range(2)]

    plan.add("gpsimd", lambda g: g.memset(ones_col[:], 1.0))
    plan.add("gpsimd", lambda g: g.memset(ones_row[:], 1.0))
    plan.add("gpsimd", lambda g: g.memset(negc_sb[:], NEGC))
    plan.add("gpsimd", lambda g: g.memset(eps1[:], EPS), incs=[(cstm, 1)])
    plan.add("gpsimd", lambda g: g.dma_start(out=identb[:], in_=identb_d), incs=[(cst, 16)])
    plan.add("gpsimd", lambda g: g.dma_start(out=identf[:], in_=identf_d), incs=[(cst, 16)])
    plan.add("gpsimd", lambda g: g.dma_start(out=pmat_sb[:], in_=pmat_d), incs=[(cst, 16)])
    plan.add("gpsimd", lambda g: g.dma_start(out=gidx_sb[:], in_=gidx_d), incs=[(cst, 16)])
    const_ld = cst.n
    memset_done = cstm.n

    esAB = ExitStack()
    kt_sb = esAB.enter_context(nc.sbuf_tensor("kt_sb", [128, 2 * SEQ], BF16))
    vt_sb = esAB.enter_context(nc.sbuf_tensor("vt_sb", [128, 2 * SEQ], BF16))

    esA = ExitStack()
    xbf = [esA.enter_context(nc.sbuf_tensor(f"xbf{i}", [128, NDT * CHT], BF16)) for i in range(2)]
    wsl = [esA.enter_context(nc.sbuf_tensor(f"wsl{i}", [128, NDT * 128], BF16)) for i in range(2)]
    xsq = [esA.enter_context(nc.sbuf_tensor(f"xsq{i}", [128, CHT], BF16)) for i in range(4)]
    qstage = [esA.enter_context(nc.sbuf_tensor(f"qstage{i}", [128, CHT], BF16)) for i in range(2)]
    tmpe = [esA.enter_context(nc.sbuf_tensor(f"tmpe{i}", [128, CHT], BF16)) for i in range(2)]
    rrep = esA.enter_context(nc.sbuf_tensor("rrep", [128, CHT], BF16))
    mrrep = esA.enter_context(nc.sbuf_tensor("mrrep", [128, CHT], BF16))

    psA = ExitStack()
    ps_ssum = psA.enter_context(nc.psum_tensor("ps_ssum", [128, 512], F32))
    ps_ssq = psA.enter_context(nc.psum_tensor("ps_ssq", [128, 512], F32))
    ps_bcr = psA.enter_context(nc.psum_tensor("ps_bcr", [128, 512], F32))
    ps_bcm = psA.enter_context(nc.psum_tensor("ps_bcm", [128, 512], F32))
    ps_qkv = [psA.enter_context(nc.psum_tensor(f"ps_qkv{i}", [128, 512], F32)) for i in range(2)]
    ps_wcs = [psA.enter_context(nc.psum_tensor(f"ps_wcs{i}", [128, 1], F32)) for i in range(2)]

    st = dict(xb_ld=[None] * 8, xsq_free=[0, 0, 0, 0], slab_free=[0, 0], wsl_free=[0, 0],
              qkv_evac_ve=[0, 0], stat_rows_ve=0, bc_evac_ve=0,
              wcs_ve=[0] * 12, statmm_te=0)

    def unit_tasks(ch):
        for dt in range(NDT):
            if dt % 4 == 0:
                yield ("load", ch, dt)
            yield ("proc", ch, dt)

    def do_task(task):
        kind, ch, dt = task
        s = ch % 2
        if kind == "load":
            dg = dt // 4
            slot = (ch % 2) * 8 + dg
            src = xt_d.rearrange("(a p) t -> p a t", p=128)[
                :, dg * 4:(dg + 1) * 4, ch * CHT:(ch + 1) * CHT]
            w = []
            if dg == 0:
                w = [(te, st["slab_free"][s])]
            plan.add("gpsimd",
                     lambda g, s=s, dg=dg, src=src: g.dma_start(
                         out=xbf[s][:, dg * 4 * CHT:(dg + 1) * 4 * CHT]
                         .rearrange("p (a t) -> p a t", a=4), in_=src),
                     incs=xbp.inc(slot), waits=w)
            st["xb_ld"][dg] = xbp.at(slot)
            return
        j = dt % 4
        plan.add("vector",
                 lambda e, s=s, dt=dt, j=j: e.tensor_tensor(
                     out=xsq[j][:], in0=xbf[s][:, dt * CHT:(dt + 1) * CHT],
                     in1=xbf[s][:, dt * CHT:(dt + 1) * CHT], op=ALU.mult),
                 incs=[(ve, 1)], waits=[st["xb_ld"][dt // 4], (te, st["xsq_free"][j])])
        sq_ve = ve.n
        w = [(ve, sq_ve)]
        if dt == 0:
            w += [(cst, const_ld), (cstm, memset_done), (ve, st["stat_rows_ve"])]
        plan.add("tensor",
                 lambda e, s=s, dt=dt: e.matmul(
                     ps_ssum[0:1, :], lhsT=ones_col[:],
                     rhs=xbf[s][:, dt * CHT:(dt + 1) * CHT],
                     start=(dt == 0), stop=(dt == NDT - 1)),
                 waits=w)
        mark = (dt % 2 == 1) or (dt == NDT - 1)
        plan.add("tensor",
                 lambda e, j=j, dt=dt: e.matmul(
                     ps_ssq[0:1, :], lhsT=ones_col[:], rhs=xsq[j][:],
                     start=(dt == 0), stop=(dt == NDT - 1)),
                 incs=[(te, 1)] if mark else [])
        if mark:
            st["xsq_free"][(j - 1) % 4] = te.n
            st["xsq_free"][j] = te.n
        if dt == NDT - 1:
            st["statmm_te"] = te.n

    def stats_math(W):
        inv = 1.0 / DM
        plan.add("vector", lambda e, W=W: e.tensor_scalar(out=r_sum[0:1, 0:W], in0=r_sum[0:1, 0:W], scalar1=inv, scalar2=None, op0=ALU.mult))
        plan.add("vector", lambda e, W=W: e.tensor_scalar(out=r_sq[0:1, 0:W], in0=r_sq[0:1, 0:W], scalar1=inv, scalar2=None, op0=ALU.mult))
        plan.add("vector", lambda e, W=W: e.tensor_tensor(out=r_ln[0:1, 0:W], in0=r_sum[0:1, 0:W], in1=r_sum[0:1, 0:W], op=ALU.mult))
        plan.add("vector", lambda e, W=W: e.tensor_tensor(out=r_sq[0:1, 0:W], in0=r_sq[0:1, 0:W], in1=r_ln[0:1, 0:W], op=ALU.subtract),
                 incs=[(ve, 1)])
        var_ve = ve.n
        plan.add("scalar", lambda e, W=W: e.activation(out=r_ln[0:1, 0:W], in_=r_sq[0:1, 0:W], func=AF.Ln, bias=eps1[:], scale=1.0),
                 incs=[(ac, 1)], waits=[(ve, var_ve), (cstm, memset_done)])
        plan.add("scalar", lambda e, W=W: e.activation(out=rstd_bf[0:1, 0:W], in_=r_ln[0:1, 0:W], func=AF.Exp, scale=-0.5),
                 incs=[(ac, 1)], waits=[(ac, ac.n)])
        rstd_ac = ac.n
        plan.add("vector", lambda e, W=W: e.tensor_tensor(out=r_mr[0:1, 0:W], in0=r_sum[0:1, 0:W], in1=rstd_bf[0:1, 0:W], op=ALU.mult),
                 waits=[(ac, rstd_ac)])
        plan.add("vector", lambda e, W=W: e.tensor_copy(mr_bf[0:1, 0:W], r_mr[0:1, 0:W]), incs=[(ve, 1)])

    pending = list(unit_tasks(0))
    while pending:
        do_task(pending.pop(0))

    for ch in range(NCH):
        s = ch % 2
        plan.add("vector", lambda e: e.tensor_copy(r_sum[0:1, 0:512], ps_ssum[0:1, :]),
                 waits=[(te, st["statmm_te"])])
        plan.add("vector", lambda e: e.tensor_copy(r_sq[0:1, 0:512], ps_ssq[0:1, :]),
                 incs=[(ve, 1)])
        st["stat_rows_ve"] = ve.n
        stats_math(512)
        rows_ve = ve.n
        plan.add("tensor", lambda e: e.matmul(ps_bcr[:], lhsT=ones_row[:], rhs=rstd_bf[0:1, 0:512], start=True, stop=True),
                 waits=[(ve, rows_ve), (ve, st["bc_evac_ve"])])
        plan.add("tensor", lambda e: e.matmul(ps_bcm[:], lhsT=ones_row[:], rhs=mr_bf[0:1, 0:512], start=True, stop=True),
                 incs=[(te, 1)])
        bc_te = te.n
        plan.add("vector", lambda e: e.tensor_copy(rrep[:], ps_bcr[:]), waits=[(te, bc_te)])
        plan.add("vector", lambda e: e.tensor_copy(mrrep[:], ps_bcm[:]), incs=[(ve, 1)])
        st["bc_evac_ve"] = ve.n

        xb_qkv = list(st["xb_ld"])
        if ch + 1 < NCH:
            pending = list(unit_tasks(ch + 1))

        for ct in range(NCT):
            wi = ct % 2
            if ch == 0:
                plan.add("gpsimd",
                         lambda g, ct=ct, wi=wi: g.dma_start(
                             out=wsl[wi][:], in_=wqkv_d[ct].rearrange("p t c -> p (t c)")),
                         incs=wlp.inc(wi), waits=[(te, st["wsl_free"][wi])])
                lded = wlp.at(wi)
                plan.add("gpsimd",
                         lambda g, ct=ct, wi=wi: g.dma_start(
                             out=wbf_d[ct].rearrange("p t c -> p (t c)"), in_=wsl[wi][:]),
                         incs=[(wbs, 16)], waits=[lded])
            else:
                w = [(te, st["wsl_free"][wi])]
                if ch == 1 and ct in (0, 1):
                    w.append((wbs, 16 * NCT))
                plan.add("gpsimd",
                         lambda g, ct=ct, wi=wi: g.dma_start(
                             out=wsl[wi][:], in_=wbf_d[ct].rearrange("p t c -> p (t c)")),
                         incs=wlp.inc(wi), waits=w)
            slab_ld = wlp.at(wi)
            pi = ct % 2
            for dt in range(NDT):
                w = [xb_qkv[dt // 4]] if dt % 4 == 0 else []
                if dt == 0:
                    w = [slab_ld, (ve, st["qkv_evac_ve"][pi]), xb_qkv[0]]
                plan.add("tensor",
                         lambda e, s=s, wi=wi, dt=dt, pi=pi: e.matmul(
                             ps_qkv[pi][:], lhsT=wsl[wi][:, dt * 128:(dt + 1) * 128],
                             rhs=xbf[s][:, dt * CHT:(dt + 1) * CHT],
                             start=(dt == 0), stop=(dt == NDT - 1)),
                         waits=w, incs=[(te, 1)] if dt == NDT - 1 else [])
            mm_te = te.n
            if ch == 0:
                wci = ct % 2
                for dt in range(NDT):
                    plan.add("tensor",
                             lambda e, wi=wi, dt=dt, wci=wci: e.matmul(
                                 ps_wcs[wci][:], lhsT=wsl[wi][:, dt * 128:(dt + 1) * 128],
                                 rhs=ones_col[:], start=(dt == 0), stop=(dt == NDT - 1)),
                             waits=[(ve, st["wcs_ve"][ct - 2])] if (dt == 0 and ct >= 2) else [],
                             incs=[(te, 1)] if dt == NDT - 1 else [])
                myte = te.n
                plan.add("vector",
                         lambda e, ct=ct, wci=wci: e.tensor_copy(wcs_sb[:, ct:ct + 1], ps_wcs[wci][:]),
                         incs=[(ve, 1)], waits=[(te, myte)])
                st["wcs_ve"][ct] = ve.n
            st["wsl_free"][wi] = te.n
            ti = ct % 2
            w_ev = [(te, mm_te), (ve, st["wcs_ve"][ct])]
            plan.add("vector",
                     lambda e, pi=pi, ti=ti: e.tensor_tensor(
                         out=tmpe[ti][:], in0=ps_qkv[pi][:], in1=rrep[:], op=ALU.mult),
                     waits=w_ev)
            plan.add("vector",
                     lambda e, ct=ct, ti=ti: e.tensor_scalar(
                         out=tmpe[1 - ti][:], in0=mrrep[:], scalar1=wcs_sb[:, ct:ct + 1],
                         scalar2=None, op0=ALU.mult))
            if ct <= 7:
                qi = ct % 2
                plan.add("vector",
                         lambda e, qi=qi, ti=ti: e.tensor_tensor(
                             out=qstage[qi][:], in0=tmpe[ti][:], in1=tmpe[1 - ti][:], op=ALU.subtract),
                         incs=[(ve, 1)], waits=[qsp.at(qi)])
                st["qkv_evac_ve"][pi] = ve.n
                myve = ve.n
                plan.add("sync",
                         lambda e, ct=ct, ch=ch, qi=qi: e.dma_start(
                             out=qscr_d[ct, :, ch * CHT:(ch + 1) * CHT], in_=qstage[qi][:]),
                         incs=qsp.inc(qi), waits=[(ve, myve)])
            else:
                dsttn = kt_sb if ct <= 9 else vt_sb
                off = (ct - 8 if ct <= 9 else ct - 10) * SEQ + ch * CHT
                plan.add("vector",
                         lambda e, dsttn=dsttn, off=off, ti=ti: e.tensor_tensor(
                             out=dsttn[:, off:off + CHT], in0=tmpe[ti][:], in1=tmpe[1 - ti][:],
                             op=ALU.subtract),
                         incs=[(ve, 1)])
                st["qkv_evac_ve"][pi] = ve.n
            for _ in range(6):
                if pending:
                    do_task(pending.pop(0))
        st["slab_free"][s] = te.n
        while pending:
            do_task(pending.pop(0))

    teA, veA, acA = te.n, ve.n, ac.n
    qscr_done = qsp.totals()
    esA.close()
    psA.close()

    if debug == "A":
        plan.add("sync", lambda e: e.dma_start(out=dbg["dkv"][:, 0:2 * SEQ], in_=kt_sb[:]),
                 incs=[(dbs, 16)], waits=[(ve, veA)] + qscr_done)
        plan.add("sync", lambda e: e.dma_start(out=dbg["dkv"][:, 2 * SEQ:], in_=vt_sb[:]), incs=[(dbs, 16)])
        for k, row in enumerate([r_sum, r_sq, r_ln, r_sum]):
            plan.add("sync", lambda e, k=k, row=row: e.dma_start(out=dbg["drow"][k:k + 1, :], in_=row[0:1, :]),
                     incs=[(dbs, 16)])
        plan.add("sync", lambda e, v=dbs.n: e.wait_ge(dbs.h, v))
        return _finish(nc, plan, [es, esAB])

    esB = ExitStack()
    expT = [esB.enter_context(nc.sbuf_tensor(f"expT{i}", [128, NDT * 512], BF16)) for i in range(2)]
    vp_sb = esB.enter_context(nc.sbuf_tensor("vp_sb", [128, 2 * 32 * 129], BF16))
    qtb = [esB.enter_context(nc.sbuf_tensor(f"qtb{i}", [128, 512], BF16)) for i in range(2)]
    out_qc = [esB.enter_context(nc.sbuf_tensor(f"outqc{i}", [128, 4096], BF16)) for i in range(4)]
    masks_sb = esB.enter_context(nc.sbuf_tensor("masks_sb", [128, 4 * 512], BF16))
    rec_sb = esB.enter_context(nc.sbuf_tensor("rec_sb", [128, 4], F32))

    plan.add("gpsimd", lambda g: g.memset(vp_sb[:], 1.0), incs=[(vpm, 1)], waits=[(ve, veA)])
    plan.add("gpsimd", lambda g: g.dma_start(out=masks_sb[:].rearrange("p (j q) -> p j q", j=4), in_=masks_d),
             incs=[(mks, 16)])
    mks_ld = mks.n

    psT = ExitStack()
    ps_tr = [psT.enter_context(nc.psum_tensor(f"ps_tr{i}", [128, 128], BF16)) for i in range(2)]
    tr_free = [0, 0]
    for g2 in range(2):
        for kt in range(32):
            i = (g2 * 32 + kt) % 2
            w = [(ve, tr_free[i])]
            if g2 == 0 and kt == 0:
                w += [(te, teA), (ve, veA), (cst, const_ld)]
            plan.add("tensor",
                     lambda e, g2=g2, kt=kt, i=i: e.transpose(
                         ps_tr[i][:], in_=vt_sb[:, g2 * SEQ + kt * 128: g2 * SEQ + (kt + 1) * 128],
                         identity=identb[:]),
                     incs=[(te, 1)], waits=w)
            myte = te.n
            plan.add("vector",
                     lambda e, g2=g2, kt=kt, i=i: e.tensor_copy(
                         vp_sb[:, (g2 * 32 + kt) * 129:(g2 * 32 + kt) * 129 + 128], ps_tr[i][:]),
                     incs=[(ve, 1)], waits=[(te, myte), (mks, mks_ld), (vpm, 1)])
            tr_free[i] = ve.n
    psT.close()
    vbuild_ve = ve.n
    vbuild_te = te.n

    for dtp in range(2):
        plan.add("gpsimd",
                 lambda g, dtp=dtp: g.dma_start(
                     out=wosl[dtp][:], in_=wo_d[dtp].rearrange("p t c -> p (t c)")),
                 incs=wop.inc(dtp), waits=[(ve, veA)])

    psB = ExitStack()
    ps_sc = [psB.enter_context(nc.psum_tensor(f"ps_sc{i}", [128, 1536], F32)) for i in range(2)]
    ps_av = [psB.enter_context(nc.psum_tensor(f"ps_av{i}", [128, 258], F32)) for i in range(2)]

    bst = dict(sc_free=[0, 0], av_free=[0, 0], qtb_free=[0, 0], gidx=0)
    units = [(qc, lh) for qc in (7, 2, 5, 0, 6, 3, 4, 1) for lh in range(8)]
    exp_ac = {}
    mask_ve = {}
    av_done = {-2: 0, -1: 0}

    def gen_scores(ui):
        qc, lh = units[ui]
        g2 = lh // 4
        nkt = 4 * (qc + 1)
        ei = ui % 2
        qi = ui % 2
        w = [(te, bst["qtb_free"][qi])]
        if ui in (0, 1):
            w += qscr_done
        plan.add("sync",
                 lambda e, qc=qc, lh=lh, qi=qi: e.dma_start(
                     out=qtb[qi][:], in_=qscr_d[lh, :, qc * 512:(qc + 1) * 512]),
                 incs=qtp.inc(qi), waits=w)
        qt_ld = qtp.at(qi)
        exp_ac[ui] = [0] * nkt
        mask_ve[ui] = [0] * 4
        kt0 = 0
        while kt0 < nkt:
            gn = min(3, nkt - kt0)
            gi = bst["gidx"] % 2
            bst["gidx"] += 1
            for j in range(gn):
                kt = kt0 + j
                w = []
                if j == 0:
                    w = [(ac, bst["sc_free"][gi]), qt_ld]
                    if ui == 0 and kt0 == 0:
                        w.append((te, vbuild_te))
                plan.add("tensor",
                         lambda e, g2=g2, kt=kt, j=j, gi=gi, qi=qi: e.matmul(
                             ps_sc[gi][:, j * 512:(j + 1) * 512],
                             lhsT=kt_sb[:, g2 * SEQ + kt * 128: g2 * SEQ + (kt + 1) * 128],
                             rhs=qtb[qi][:], start=True, stop=True),
                         waits=w, incs=[(te, 1)] if j == gn - 1 else [])
            bst["qtb_free"][qi] = te.n
            grp_te = te.n
            plan.add("scalar",
                     lambda e, ei=ei, kt0=kt0, gn=gn, gi=gi: e.activation(
                         out=expT[ei][:, kt0 * 512:(kt0 + gn) * 512],
                         in_=ps_sc[gi][:, 0:gn * 512], func=AF.Exp,
                         bias=negc_sb[:], scale=ATT_SCALE),
                     incs=[(ac, 1)],
                     waits=[(te, max(grp_te, av_done[ui - 2])), (cstm, memset_done)])
            bst["sc_free"][gi] = ac.n
            for j in range(gn):
                exp_ac[ui][kt0 + j] = ac.n
            yield
            kt0 += gn
        for j in range(4):
            kt = 4 * qc + j
            plan.add("gpsimd",
                     lambda g, ei=ei, kt=kt, j=j: g.tensor_tensor(
                         out=expT[ei][:, kt * 512:(kt + 1) * 512],
                         in0=expT[ei][:, kt * 512:(kt + 1) * 512],
                         in1=masks_sb[:, j * 512:(j + 1) * 512], op=ALU.mult),
                     incs=[(vpm, 1)],
                     waits=[(ac, exp_ac[ui][kt]), (mks, mks_ld), (ve, vbuild_ve),
                            (te, av_done[ui - 2])])
            mask_ve[ui][j] = vpm.n
        yield

    def gen_av(ui):
        qc, lh = units[ui]
        g2 = lh // 4
        ei = ui % 2
        oi = qc % 4
        for qs in range(4):
            pi = qs // 2
            nkt_q = 4 * qc + qs + 1
            for kt in range(nkt_q):
                w = []
                if kt == 0:
                    w = [(ac, exp_ac[ui][nkt_q - 1]), (vpm, mask_ve[ui][qs]),
                         (ve, bst["av_free"][pi])]
                plan.add("tensor",
                         lambda e, ei=ei, kt=kt, qs=qs, g2=g2, pi=pi: e.matmul(
                             ps_av[pi][:, (qs % 2) * 129:(qs % 2) * 129 + 129],
                             lhsT=expT[ei][:, kt * 512 + qs * 128: kt * 512 + (qs + 1) * 128],
                             rhs=vp_sb[:, (g2 * 32 + kt) * 129:(g2 * 32 + kt + 1) * 129],
                             start=(kt == 0), stop=(kt == nkt_q - 1)),
                         waits=w, incs=[(te, 1)] if kt == nkt_q - 1 else [])
            avte = te.n
            if qs % 2 == 1:
                plan.add("vector",
                         lambda e, pi=pi: e.reciprocal(
                             out=rec_sb[:, 2 * pi:2 * pi + 2], in_=ps_av[pi][:, 128::129]),
                         incs=[(ve, 1)], waits=[(te, avte), oqp.at(oi)])
                myve = ve.n
                for sq in range(2):
                    qs2 = (qs - 1) + sq
                    plan.add("vector",
                             lambda e, pi=pi, sq=sq, qs2=qs2, lh=lh, oi=oi: e.tensor_scalar(
                                 out=out_qc[oi][:, (qs2 * 8 + lh) * 128:(qs2 * 8 + lh + 1) * 128],
                                 in0=ps_av[pi][:, sq * 129: sq * 129 + 128],
                                 scalar1=rec_sb[:, 2 * pi + sq:2 * pi + sq + 1],
                                 scalar2=None, op0=ALU.mult),
                             incs=[(ve, 1)] if sq == 1 else [],
                             waits=[(ve, myve)])
                bst["av_free"][pi] = ve.n
            yield
        av_done[ui] = te.n

    g_sc = gen_scores(0)
    for _ in g_sc:
        pass
    for ui in range(len(units)):
        g_av = gen_av(ui)
        g_next = gen_scores(ui + 1) if ui + 1 < len(units) else iter(())
        done_av = done_sc = False
        while not (done_av and done_sc):
            if not done_av:
                done_av = next(g_av, "end") == "end"
            if not done_sc:
                done_sc = next(g_next, "end") == "end"
            if not done_sc:
                done_sc = next(g_next, "end") == "end"
        qc, lh = units[ui]
        if lh == 7:
            oi = qc % 4
            myve = ve.n
            plan.add("sync",
                     lambda e, qc=qc, oi=oi: e.dma_start(
                         out=ccin_d[qc * 512:(qc + 1) * 512, :].rearrange("(s p) x -> p s x", s=4),
                         in_=out_qc[oi][:].rearrange("p (s x) -> p s x", s=4)),
                     incs=oqp.inc(oi), waits=[(ve, myve)])
            if debug != "B":
                plan.add("gpsimd",
                         lambda g, qc=qc: g.collective_compute(
                             "AllGather", ALU.bypass,
                             replica_groups=[[0, 1, 2, 3], [4, 5, 6, 7]],
                             ins=[ccin_d[qc * 512:(qc + 1) * 512, :].opt()],
                             outs=[ccout_d[qc * 2048:(qc + 1) * 2048, :].opt()]),
                         incs=[(ccs, 1)], waits=[oqp.at(oi)])

    teB, veB, acB = te.n, ve.n, ac.n
    psB.close()
    esB.close()
    esAB.close()

    if debug == "B":
        plan.add("sync", lambda e, w=tuple(oqp.totals()): [e.wait_ge(s.h, v) for s, v in w] and None)
        return _finish(nc, plan, [es])

    esC = ExitStack()
    ot_sb = esC.enter_context(nc.sbuf_tensor("ot_sb", [128, 32 * 1024], BF16))
    y_sb = esC.enter_context(nc.sbuf_tensor("y_sb", [128, 32 * 1024], BF16))
    xfc = esC.enter_context(nc.sbuf_tensor("xfc0", [128, 1024], F32))
    ysq0 = esC.enter_context(nc.sbuf_tensor("ysq0", [128, 1024], BF16))
    outf = [esC.enter_context(nc.sbuf_tensor(f"outf{i}", [128, 1024], F32)) for i in range(2)]
    stage = esC.enter_context(nc.sbuf_tensor("stage", [128, 4 * 1024], BF16))
    awr_sb = esC.enter_context(nc.sbuf_tensor("awr_sb", [128, NDT * 68], BF16))
    btil_sb = esC.enter_context(nc.sbuf_tensor("btil_sb", [64, DM], BF16))
    midlog = esC.enter_context(nc.sbuf_tensor("midlog", [68, 1024], F32))
    r2rep = esC.enter_context(nc.sbuf_tensor("r2rep", [68, 1024], BF16))
    mr2rep = esC.enter_context(nc.sbuf_tensor("mr2rep", [68, 1024], BF16))
    acs_sb = esC.enter_context(nc.sbuf_tensor("acs_sb", [68, 1], F32))
    grep_sb = esC.enter_context(nc.sbuf_tensor("grep_sb", [64, 1024], BF16))
    midg = grep_sb
    gatest = esC.enter_context(nc.sbuf_tensor("gatest", [4, 1024], BF16))
    lgn = [esC.enter_context(nc.sbuf_tensor(f"lgn{i}", [128, 4], F32)) for i in range(2)]
    g4bf = esC.enter_context(nc.sbuf_tensor("g4bf", [128, 4], BF16))
    gwork = esC.enter_context(nc.sbuf_tensor("gwork", [128, 8], F32))
    gtmp0 = esC.enter_context(nc.sbuf_tensor("gtmp0", [128, 8], F32))
    gtmp = [gtmp0[:, 0:4], gtmp0[:, 4:8]]

    plan.add("gpsimd", lambda g: g.dma_start(out=awr_sb[:].rearrange("p (t j) -> p t j", t=NDT), in_=awr_d),
             incs=[(cdw, 16)], waits=[(te, teB), (ve, veB)])
    plan.add("gpsimd", lambda g: g.dma_start(out=btil_sb[:], in_=btil_d), incs=[(cdw, 16)])
    cd_w = cdw.n

    for blk in range(32):
        i = blk % 4
        grp = blk // 4
        w = [(ccs, 8), cws.at((grp - 1) % 2)]
        plan.add("gpsimd",
                 lambda g, blk=blk, i=i: g.indirect_dma_start(
                     out=stage[:, i * 1024:(i + 1) * 1024], out_offset=None, in_=ccout_d,
                     in_offset=bass_rust.IndirectOffsetOnAxis(ap=gidx_sb[:, blk:blk + 1], axis=0)),
                 incs=stp.inc(blk % 2), waits=w)
        if blk % 4 == 3:
            wg = [stp.at(0), stp.at(1)]
            plan.add("sync",
                     lambda e, grp=grp: e.dma_start(
                         out=ccwin_d[grp * 512:(grp + 1) * 512, :]
                         .rearrange("(s p) c -> p s c", s=4),
                         in_=stage[:].rearrange("p (s c) -> p s c", s=4)),
                     incs=cws.inc(grp % 2), waits=wg)
        if blk % 8 == 7:
            r = blk // 8
            wr = tuple(cws.totals())
            for loc in range(8):
                ot = r * 8 + loc
                w = list(wr)
                if ot == 0:
                    w += [(te, teB), (ve, veB), (ac, acB)]
                plan.add("sync",
                         lambda e, ot=ot, r=r, loc=loc: e.dma_start(
                             out=ot_sb[:, ot * 1024:(ot + 1) * 1024],
                             in_=ccwin_d[r * 1024:(r + 1) * 1024, loc * 128:(loc + 1) * 128],
                             transpose=True),
                         incs=[(otl, 16)], waits=w)
    ot_done = otl.n

    if debug == "G":
        plan.add("sync", lambda e: e.dma_start(out=dbg["dy"], in_=ot_sb[:]), incs=[(dbs, 16)],
                 waits=[(otl, ot_done)])
        plan.add("sync", lambda e, v=dbs.n: e.wait_ge(dbs.h, v))
        return _finish(nc, plan, [es, esC])

    psC = ExitStack()
    ps_y = [psC.enter_context(nc.psum_tensor(f"ps_y{i}", [128, 1024], F32)) for i in range(2)]
    ps_cs = psC.enter_context(nc.psum_tensor("ps_cs", [1, 1024], F32))
    ps_cq = psC.enter_context(nc.psum_tensor("ps_cq", [1, 1024], F32))

    cst_ = dict(wo_free=[0, 0], xfc_free=0, ysq_free=0, y_evac_ve=[0, 0])
    for dt in range(NDT):
        i = dt % 2
        if dt >= 2:
            plan.add("gpsimd",
                     lambda g, dt=dt, i=i: g.dma_start(
                         out=wosl[i][:], in_=wo_d[dt].rearrange("p t c -> p (t c)")),
                     incs=wop.inc(i), waits=[(te, cst_["wo_free"][i])])
        wo_ld = wop.at(i)
        plan.add("sync",
                 lambda e, dt=dt: e.dma_start(
                     out=xfc[:], in_=xtc_d[dt * 128:(dt + 1) * 128, :]),
                 incs=xcp.inc(0), waits=[(ve, cst_["xfc_free"])])
        xfc_ld = xcp.at(0)
        for ot in range(32):
            for h in range(2):
                w = []
                if ot == 0 and h == 0:
                    w = [wo_ld, (otl, ot_done), (ve, cst_["y_evac_ve"][i])]
                plan.add("tensor",
                         lambda e, dt=dt, i=i, ot=ot, h=h: e.matmul(
                             ps_y[i][:, h * 512:(h + 1) * 512],
                             lhsT=wosl[i][:, ot * 128:(ot + 1) * 128],
                             rhs=ot_sb[:, ot * 1024 + h * 512: ot * 1024 + h * 512 + 512],
                             start=(ot == 0), stop=(ot == 31)),
                         waits=w, incs=[(te, 1)] if (ot == 31 and h == 1) else [])
        cst_["wo_free"][i] = te.n
        ymm_te = te.n
        plan.add("vector",
                 lambda e, dt=dt, i=i: e.tensor_tensor(
                     out=y_sb[:, dt * 1024:(dt + 1) * 1024], in0=ps_y[i][:], in1=xfc[:], op=ALU.add),
                 incs=[(ve, 1)], waits=[(te, ymm_te), xfc_ld])
        cst_["y_evac_ve"][i] = ve.n
        cst_["xfc_free"] = ve.n
        plan.add("vector",
                 lambda e, dt=dt: e.tensor_tensor(
                     out=ysq0[:], in0=y_sb[:, dt * 1024:(dt + 1) * 1024],
                     in1=y_sb[:, dt * 1024:(dt + 1) * 1024], op=ALU.mult),
                 incs=[(ve, 1)], waits=[(te, cst_["ysq_free"])])
        sq_ve = ve.n
        for h in range(2):
            plan.add("tensor",
                     lambda e, dt=dt, h=h: e.matmul(
                         ps_cs[0:1, h * 512:(h + 1) * 512], lhsT=ones_col[:],
                         rhs=y_sb[:, dt * 1024 + h * 512: dt * 1024 + h * 512 + 512],
                         start=(dt == 0), stop=(dt == NDT - 1)),
                     waits=[(ve, sq_ve)] if h == 0 else [])
            plan.add("tensor",
                     lambda e, dt=dt, h=h: e.matmul(
                         ps_cq[0:1, h * 512:(h + 1) * 512], lhsT=ones_col[:],
                         rhs=ysq0[:, h * 512:(h + 1) * 512],
                         start=(dt == 0), stop=(dt == NDT - 1)),
                     incs=[(te, 1)] if h == 1 else [])
        cst_["ysq_free"] = te.n
    cstat_te = te.n

    plan.add("vector", lambda e: e.tensor_copy(r_sum[0:1, :], ps_cs[0:1, :]), waits=[(te, cstat_te)])
    plan.add("vector", lambda e: e.tensor_copy(r_sq[0:1, :], ps_cq[0:1, :]), incs=[(ve, 1)])
    stats_math(1024)
    rows2_ve = ve.n
    psC.close()

    if debug == "C":
        plan.add("sync", lambda e: e.dma_start(out=dbg["dy"], in_=y_sb[:]), incs=[(dbs, 16)],
                 waits=[(ve, rows2_ve)])
        for k, row in enumerate([r_sum, r_sq, r_ln, r_sum]):
            plan.add("sync", lambda e, k=k, row=row: e.dma_start(out=dbg["drow"][k:k + 1, :], in_=row[0:1, :]),
                     incs=[(dbs, 16)])
        plan.add("sync", lambda e, v=dbs.n: e.wait_ge(dbs.h, v))
        return _finish(nc, plan, [es, esC])

    psD1 = ExitStack()
    ps_mid = psD1.enter_context(nc.psum_tensor("ps_mid", [68, 1024], F32))
    ps_acs = psD1.enter_context(nc.psum_tensor("ps_acs", [68, 1], F32))
    ps_b68 = psD1.enter_context(nc.psum_tensor("ps_b68", [68, 1024], F32))

    for dt in range(NDT):
        plan.add("tensor",
                 lambda e, dt=dt: e.matmul(
                     ps_acs[:], lhsT=awr_sb[:, dt * 68:(dt + 1) * 68], rhs=ones_col[:],
                     start=(dt == 0), stop=(dt == NDT - 1)),
                 waits=[(cdw, cd_w), (te, cstat_te), (ve, rows2_ve)] if dt == 0 else [])
        for h in range(2):
            plan.add("tensor",
                     lambda e, dt=dt, h=h: e.matmul(
                         ps_mid[:, h * 512:(h + 1) * 512],
                         lhsT=awr_sb[:, dt * 68:(dt + 1) * 68],
                         rhs=y_sb[:, dt * 1024 + h * 512: dt * 1024 + h * 512 + 512],
                         start=(dt == 0), stop=(dt == NDT - 1)),
                     incs=[(te, 1)] if (dt == NDT - 1 and h == 1) else [])
    mid_te = te.n
    plan.add("vector", lambda e: e.tensor_copy(acs_sb[:], ps_acs[:]), incs=[(ve, 1)],
             waits=[(te, mid_te)])
    acs_ve = ve.n

    for h in range(2):
        plan.add("tensor",
                 lambda e, h=h: e.matmul(
                     ps_b68[:, h * 512:(h + 1) * 512], lhsT=ones_row[0:1, 0:68],
                     rhs=rstd_bf[0:1, h * 512:(h + 1) * 512], start=True, stop=True),
                 waits=[(ve, rows2_ve)] if h == 0 else [], incs=[(te, 1)] if h == 1 else [])
    plan.add("vector", lambda e: e.tensor_copy(r2rep[:], ps_b68[:]), incs=[(ve, 1)],
             waits=[(te, te.n)])
    r2_ve = ve.n
    for h in range(2):
        plan.add("tensor",
                 lambda e, h=h: e.matmul(
                     ps_b68[:, h * 512:(h + 1) * 512], lhsT=ones_row[0:1, 0:68],
                     rhs=mr_bf[0:1, h * 512:(h + 1) * 512], start=True, stop=True),
                 waits=[(ve, r2_ve)] if h == 0 else [], incs=[(te, 1)] if h == 1 else [])
    plan.add("vector", lambda e: e.tensor_copy(mr2rep[:], ps_b68[:]), incs=[(ve, 1)],
             waits=[(te, te.n)])

    plan.add("vector", lambda e: e.tensor_tensor(out=midlog[:], in0=ps_mid[:], in1=r2rep[:], op=ALU.mult),
             waits=[(ve, ve.n)])
    plan.add("vector", lambda e: e.tensor_scalar(out=ysq0[0:68, :], in0=mr2rep[:], scalar1=acs_sb[:],
                                                 scalar2=None, op0=ALU.mult),
             waits=[(ve, acs_ve)])
    plan.add("vector", lambda e: e.tensor_tensor(out=midlog[:], in0=midlog[:], in1=ysq0[0:68, :], op=ALU.subtract),
             incs=[(ve, 1)])
    midlog_ve = ve.n
    psD1.close()

    psD2 = ExitStack()
    ps_tr4 = psD2.enter_context(nc.psum_tensor("ps_tr4", [128, 4], F32))
    ps_gt = psD2.enter_context(nc.psum_tensor("ps_gt", [4, 128], BF16))
    ps_grep = psD2.enter_context(nc.psum_tensor("ps_grep", [64, 1024], F32))
    lgn_free = [0, 0]
    tr4_free = 0
    gt_copy_ve = 0
    prev_gt_te = 0
    for tc in range(8):
        i = tc % 2
        w = [(ve, lgn_free[i]), (ve, tr4_free)]
        if tc == 0:
            w.append((ve, midlog_ve))
        plan.add("tensor",
                 lambda e, tc=tc: e.transpose(
                     ps_tr4[:], in_=midlog[64:68, tc * 128:(tc + 1) * 128], identity=identf[64:68, 64:68]),
                 incs=[(te, 1)], waits=w)
        myte = te.n
        plan.add("vector", lambda e, i=i: e.tensor_copy(lgn[i][:], ps_tr4[:]), incs=[(ve, 1)],
                 waits=[(te, myte)])
        tr4_free = ve.n

        def V(fn, waits=()):
            plan.add("vector", fn, incs=[(ve, 1)], waits=list(waits) + [(ve, ve.n)])
        V(lambda e, i=i: e.tensor_reduce(out=gwork[:, 0:1], in_=lgn[i][:], axis=AX.X, op=ALU.max))
        m1_ve = ve.n
        V(lambda e: e.tensor_scalar(out=gwork[:, 1:2], in0=gwork[:, 0:1], scalar1=-1.0, scalar2=None, op0=ALU.mult))
        negm1_ve = ve.n
        plan.add("scalar",
                 lambda e, i=i: e.activation(out=gtmp[0], in_=lgn[i][:], func=AF.Exp, bias=gwork[:, 1:2], scale=1.0),
                 incs=[(ac, 1)], waits=[(ve, negm1_ve)])
        e4_ac = ac.n
        V(lambda e, i=i: e.tensor_scalar(out=gtmp[1], in0=lgn[i][:], scalar1=gwork[:, 0:1], scalar2=BIG,
                                         op0=ALU.is_equal, op1=ALU.mult))
        V(lambda e, i=i: e.tensor_tensor(out=gtmp[1], in0=lgn[i][:], in1=gtmp[1], op=ALU.subtract))
        V(lambda e: e.tensor_reduce(out=gwork[:, 2:3], in_=gtmp[1], axis=AX.X, op=ALU.max))
        V(lambda e, i=i: e.tensor_scalar(out=gtmp[1], in0=lgn[i][:], scalar1=gwork[:, 2:3], scalar2=None,
                                         op0=ALU.is_ge))
        lgn_free[i] = ve.n
        V(lambda e: e.tensor_tensor(out=gtmp[0], in0=gtmp[0], in1=gtmp[1], op=ALU.mult),
          waits=[(ac, e4_ac)])
        V(lambda e: e.tensor_reduce(out=gwork[:, 3:4], in_=gtmp[0], axis=AX.X, op=ALU.add))
        V(lambda e: e.reciprocal(out=gwork[:, 4:5], in_=gwork[:, 3:4]))
        V(lambda e: e.tensor_scalar(out=g4bf[:], in0=gtmp[0], scalar1=gwork[:, 4:5], scalar2=4.0,
                                    op0=ALU.mult, op1=ALU.mult),
          waits=[(te, prev_gt_te)])
        g4_ve = ve.n
        plan.add("tensor",
                 lambda e: e.transpose(ps_gt[:], in_=g4bf[:], identity=identb[:]),
                 incs=[(te, 1)], waits=[(ve, g4_ve), (ve, gt_copy_ve)])
        prev_gt_te = te.n
        plan.add("vector", lambda e, tc=tc: e.tensor_copy(gatest[:, tc * 128:(tc + 1) * 128], ps_gt[:]),
                 incs=[(ve, 1)], waits=[(te, prev_gt_te)])
        gt_copy_ve = ve.n
    gates_ve = ve.n

    for h in range(2):
        plan.add("tensor",
                 lambda e, h=h: e.matmul(
                     ps_grep[:, h * 512:(h + 1) * 512], lhsT=pmat_sb[:],
                     rhs=gatest[0:4, h * 512:(h + 1) * 512], start=True, stop=True),
                 waits=[(ve, gates_ve), (cst, const_ld)] if h == 0 else [],
                 incs=[(te, 1)] if h == 1 else [])
    plan.add("vector", lambda e: e.tensor_copy(grep_sb[:], ps_grep[:]), incs=[(ve, 1)],
             waits=[(te, te.n)])
    if debug == "D":
        plan.add("sync", lambda e: e.dma_start(out=dbg["dgw"], in_=gwork[:]), incs=[(dbs, 16)],
                 waits=[(ve, gates_ve)])
        plan.add("sync", lambda e: e.dma_start(out=dbg["dlg"], in_=lgn[1][:]), incs=[(dbs, 16)])
        plan.add("sync", lambda e: e.dma_start(out=dbg["dg0"], in_=gtmp[0]), incs=[(dbs, 16)])
        plan.add("sync", lambda e: e.dma_start(out=dbg["dg1"], in_=gtmp[1]), incs=[(dbs, 16)])
        plan.add("sync", lambda e: e.dma_start(out=dbg["dml"], in_=midlog[:]), incs=[(dbs, 16)],
                 waits=[(ve, midlog_ve)])
        plan.add("sync", lambda e: e.dma_start(out=dbg["dgt"], in_=gatest[:]), incs=[(dbs, 16)],
                 waits=[(ve, gates_ve)])
        plan.add("sync", lambda e: e.dma_start(out=dbg["dgr"], in_=grep_sb[:]), incs=[(dbs, 16)],
                 waits=[(ve, ve.n)])
        plan.add("sync", lambda e: e.dma_start(out=dbg["dac"], in_=acs_sb[:]), incs=[(dbs, 16)],
                 waits=[(ve, acs_ve)])
        plan.add("sync", lambda e: e.dma_start(out=dbg["dr2"][:, 0:1024], in_=r2rep[:]), incs=[(dbs, 16)])
        plan.add("sync", lambda e: e.dma_start(out=dbg["dr2"][:, 1024:2048], in_=mr2rep[:]), incs=[(dbs, 16)])
    plan.add("vector", lambda e: e.tensor_tensor(out=midg[:], in0=midlog[0:64, :], in1=grep_sb[:], op=ALU.mult),
             incs=[(ve, 1)])
    midg_ve = ve.n
    psD2.close()

    psD3 = ExitStack()
    ps_d = [psD3.enter_context(nc.psum_tensor(f"ps_d{i}", [128, 1024], F32)) for i in range(2)]
    d_free = [0, 0]
    for dt in range(NDT):
        i = dt % 2
        for h in range(2):
            w = []
            if h == 0:
                w = [(ve, midg_ve), (ve, d_free[i])]
            plan.add("tensor",
                     lambda e, dt=dt, i=i, h=h: e.matmul(
                         ps_d[i][:, h * 512:(h + 1) * 512],
                         lhsT=btil_sb[:, dt * 128:(dt + 1) * 128],
                         rhs=midg[:, h * 512:(h + 1) * 512], start=True, stop=True),
                     waits=w, incs=[(te, 1)] if h == 1 else [])
        dte = te.n
        plan.add("vector",
                 lambda e, dt=dt, i=i: e.tensor_tensor(
                     out=outf[i][:], in0=ps_d[i][:], in1=y_sb[:, dt * 1024:(dt + 1) * 1024], op=ALU.add),
                 incs=[(ve, 1)], waits=[(te, dte), ofp.at(i)])
        d_free[i] = ve.n
        myve = ve.n
        plan.add("sync",
                 lambda e, dt=dt, i=i: e.dma_start(out=yt_d[dt * 128:(dt + 1) * 128, :], in_=outf[i][:]),
                 incs=ofp.inc(i), waits=[(ve, myve)])

    fin = tuple(ofp.totals())
    plan.add("sync", lambda e, w=fin: [e.wait_ge(s.h, v) for s, v in w] and None)
    psD3.close()
    return _finish(nc, plan, [es, esC])


def _finish(nc, plan, stacks):
    block_cm = nc.Block()
    block = block_cm.__enter__()
    for eng in ENGINES:
        lst = plan.ops[eng]
        if not lst:
            continue
        dec = getattr(block, eng)

        def section(e, lst=lst):
            hwm = {}
            for waits, fn, incs in lst:
                for s, v in waits:
                    if v > hwm.get(s.name, 0):
                        e.wait_ge(s.h, v)
                        hwm[s.name] = v
                inst = fn(e) if fn is not None else None
                for s, k in incs:
                    inst.then_inc(s.h, k)
        dec(section)
    block_cm.__exit__(None, None, None)
    for stk in stacks[::-1]:
        try:
            stk.close()
        except Exception:
            pass
    return nc


_CACHE = {}


def _host_prep(L0_attn_x, L0_attn_wq, L0_attn_wk, L0_attn_wv, L0_attn_wo,
               L0_ffn_A, L0_ffn_B, L0_ffn_Wr):
    x = np.asarray(L0_attn_x, np.float32)
    wq = np.asarray(L0_attn_wq, np.float32)
    wk = np.asarray(L0_attn_wk, np.float32)
    wv = np.asarray(L0_attn_wv, np.float32)
    wo = np.asarray(L0_attn_wo, np.float32)
    A = np.asarray(L0_ffn_A, np.float32)
    B = np.asarray(L0_ffn_B, np.float32)
    Wr = np.asarray(L0_ffn_Wr, np.float32)

    xts = [np.ascontiguousarray(x[b].T) for b in range(2)]
    wo_arr = np.ascontiguousarray(wo.reshape(32, 128, 32, 128).transpose(2, 1, 0, 3))
    awr = np.concatenate([A.transpose(2, 0, 1).reshape(DM, 64), Wr.T], axis=1)
    awr_arr = np.ascontiguousarray(awr.reshape(NDT, 128, 68).transpose(1, 0, 2))
    btil_arr = np.ascontiguousarray(B.transpose(0, 2, 1).reshape(64, DM))
    masks = (np.arange(512)[None, None, :] >=
             (128 * np.arange(4)[None, :, None] + np.arange(128)[:, None, None])).astype(bfnp)
    identb = np.eye(128, dtype=bfnp)
    identf = np.eye(128, dtype=np.float32)
    pmat = (np.arange(64)[None, :] // 16 == np.arange(4)[:, None]).astype(bfnp)

    in_maps = []
    for core in range(8):
        b, g = core // 4, core % 4
        wcat = np.concatenate([wq[:, g * 1024:(g + 1) * 1024],
                               wk[:, g * 256:(g + 1) * 256],
                               wv[:, g * 256:(g + 1) * 256]], axis=1)
        wqkv_arr = np.ascontiguousarray(wcat.reshape(NDT, 128, NCT, 128).transpose(2, 1, 0, 3))
        xtc = np.ascontiguousarray(xts[b][:, g * TOK:(g + 1) * TOK])
        p = np.arange(128)[:, None]
        blk = np.arange(32)[None, :]
        s = blk % 8
        r = blk // 8
        gidx = ((2 * g + s // 4) * 2048 + r * 512 + (s % 4) * 128 + p).astype(np.int32)
        in_maps.append({
            "xt": xts[b], "xtc": xtc, "wqkv": wqkv_arr, "wot": wo_arr,
            "awr": awr_arr, "btil": btil_arr, "masks": masks,
            "identb": identb, "identf": identf, "pmat": pmat, "gidx": gidx,
        })
    return in_maps


def kernel(**inputs):
    in_maps = _host_prep(**inputs)
    if "nc" not in _CACHE:
        _CACHE["nc"] = build_program(DEBUG_PHASE)
    nc = _CACHE["nc"]
    res = run_bass_kernel_spmd(nc, in_maps, core_ids=list(range(8)))
    out = np.empty((2, SEQ, DM), np.float32)
    for core in range(8):
        b, g = core // 4, core % 4
        out[b, g * TOK:(g + 1) * TOK, :] = np.asarray(res.results[core]["yt"]).T
    return out


# version 11 (final)
# speedup vs baseline: 1.1423x; 1.0684x over previous
import numpy as np
import ml_dtypes
import bass_rust
import concourse.bass as bass
import concourse.mybir as mybir
from concourse.bass_utils import run_bass_kernel_spmd
from contextlib import ExitStack

F32 = mybir.dt.float32
BF16 = mybir.dt.bfloat16
I32 = mybir.dt.int32
AF = mybir.ActivationFunctionType
ALU = mybir.AluOpType
AX = mybir.AxisListType
bfnp = ml_dtypes.bfloat16

SEQ = 4096
DM = 4096
NDT = DM // 128
NCH = 8
CHT = SEQ // NCH
NCT = 12
TOK = 1024
ATT_SCALE = float(128.0 ** -0.5)
NEGC = -12.0
EPS = 1e-5
BIG = 1e30
ENGINES = ("tensor", "vector", "scalar", "sync", "gpsimd")

DEBUG_PHASE = None


class Sem:
    def __init__(self, name):
        self.name = name
        self.h = None
        self.n = 0


class Pool:
    def __init__(self, es, nc, name, n):
        self.sems = []
        for i in range(n):
            s = Sem(f"{name}{i}")
            s.h = es.enter_context(nc.semaphore(s.name))
            self.sems.append(s)

    def inc(self, i):
        return [(self.sems[i], 16)]

    def at(self, i):
        return (self.sems[i], self.sems[i].n)

    def totals(self):
        return [(s, s.n) for s in self.sems]


class Plan:
    def __init__(self):
        self.ops = {e: [] for e in ENGINES}

    def add(self, eng, fn=None, incs=(), waits=()):
        waits = tuple((s, v) for (s, v) in waits if v > 0)
        incs = tuple(incs)
        for s, k in incs:
            s.n += k
        self.ops[eng].append((waits, fn, incs))


def build_program(debug=None):
    nc = bass.Bass("TRN2", target_bir_lowering=False, debug=False)

    xt_d = nc.dram_tensor("xt", [DM, SEQ], F32, kind="ExternalInput").ap()
    xtc_d = nc.dram_tensor("xtc", [DM, TOK], F32, kind="ExternalInput").ap()
    wqkv_d = nc.dram_tensor("wqkv", [NCT, 128, NDT, 128], F32, kind="ExternalInput").ap()
    wo_d = nc.dram_tensor("wot", [NDT, 128, 32, 128], F32, kind="ExternalInput").ap()
    awr_d = nc.dram_tensor("awr", [128, NDT, 68], F32, kind="ExternalInput").ap()
    btil_d = nc.dram_tensor("btil", [64, DM], F32, kind="ExternalInput").ap()
    masks_d = nc.dram_tensor("masks", [128, 4, 512], BF16, kind="ExternalInput").ap()
    identb_d = nc.dram_tensor("identb", [128, 128], BF16, kind="ExternalInput").ap()
    identf_d = nc.dram_tensor("identf", [128, 128], F32, kind="ExternalInput").ap()
    pmat_d = nc.dram_tensor("pmat", [4, 64], BF16, kind="ExternalInput").ap()
    gidx_d = nc.dram_tensor("gidx", [128, 32], I32, kind="ExternalInput").ap()
    yt_d = nc.dram_tensor("yt", [DM, TOK], F32, kind="ExternalOutput").ap()

    wbf_d = nc.dram_tensor("wbf", [NCT, 128, NDT, 128], BF16, kind="Internal").ap()
    qscr_k = "ExternalOutput" if debug == "A" else "Internal"
    qscr_d = nc.dram_tensor("qscr", [8, 128, SEQ], BF16, kind=qscr_k).ap()
    ccin_k = "ExternalOutput" if debug == "B" else "Internal"
    ccin_d = nc.dram_tensor("ccin", [SEQ, 1024], BF16, kind=ccin_k).ap()
    ccout_d = nc.dram_tensor("ccout", [4 * SEQ, 1024], BF16, kind="Internal").ap()
    ccwin_k = "ExternalOutput" if debug == "G" else "Internal"
    ccwin_d = nc.dram_tensor("ccwin", [4 * TOK, 1024], BF16, kind=ccwin_k).ap()
    dbg = {}
    if debug == "A":
        dbg["dkv"] = nc.dram_tensor("dkv", [128, 4 * SEQ], BF16, kind="ExternalOutput").ap()
    if debug in ("A", "C"):
        dbg["drow"] = nc.dram_tensor("drow", [4, 1024], F32, kind="ExternalOutput").ap()
    if debug in ("C", "G"):
        dbg["dy"] = nc.dram_tensor("dy", [128, 32 * 1024], BF16, kind="ExternalOutput").ap()
    if debug == "D":
        dbg["dml"] = nc.dram_tensor("dml", [68, 1024], F32, kind="ExternalOutput").ap()
        dbg["dgt"] = nc.dram_tensor("dgt", [4, 1024], BF16, kind="ExternalOutput").ap()
        dbg["dgr"] = nc.dram_tensor("dgr", [64, 1024], BF16, kind="ExternalOutput").ap()
        dbg["dac"] = nc.dram_tensor("dac", [68, 1], F32, kind="ExternalOutput").ap()
        dbg["dr2"] = nc.dram_tensor("dr2", [68, 2048], BF16, kind="ExternalOutput").ap()
        dbg["dgw"] = nc.dram_tensor("dgw", [128, 8], F32, kind="ExternalOutput").ap()
        dbg["dlg"] = nc.dram_tensor("dlg", [128, 4], F32, kind="ExternalOutput").ap()
        dbg["dg0"] = nc.dram_tensor("dg0", [128, 4], F32, kind="ExternalOutput").ap()
        dbg["dg1"] = nc.dram_tensor("dg1", [128, 4], F32, kind="ExternalOutput").ap()

    plan = Plan()
    es = ExitStack()

    def sb(name, shape, dt):
        return es.enter_context(nc.sbuf_tensor(name, shape, dt))

    sems = [Sem(n) for n in ("te", "ve", "ac", "ccs", "cst", "cstm", "mks", "vpm",
                             "cdw", "wbs", "cwp", "otl", "dbs")]
    te, ve, ac, ccs, cst, cstm, mks, vpm, cdw, wbs, cwp, otl, dbs = sems
    for s in sems:
        s.h = es.enter_context(nc.semaphore(s.name))
    xbp = Pool(es, nc, "xbp", 16)
    wlp = Pool(es, nc, "wlp", 2)
    qsp = Pool(es, nc, "qsp", 2)
    qtp = Pool(es, nc, "qtp", 2)
    oqp = Pool(es, nc, "oqp", 4)
    stp = Pool(es, nc, "stp", 2)
    wop = Pool(es, nc, "wop", 2)
    xcp = Pool(es, nc, "xcp", 1)
    ofp = Pool(es, nc, "ofp", 2)
    cws = Pool(es, nc, "cws", 2)

    ones_col = sb("ones_col", [128, 1], BF16)
    ones_row = sb("ones_row", [1, 128], BF16)
    negc_sb = sb("negc", [128, 1], F32)
    eps1 = sb("eps1", [1, 1], F32)
    r_sum = sb("r_sum", [1, 1024], F32)
    r_sq = sb("r_sq", [1, 1024], F32)
    r_ln = sb("r_ln", [1, 1024], F32)
    r_mr = r_ln
    rstd_bf = sb("rstd_bf", [1, 1024], BF16)
    mr_bf = sb("mr_bf", [1, 1024], BF16)
    wcs_sb = sb("wcs", [128, 12], F32)
    identb = sb("identb_sb", [128, 128], BF16)
    identf = sb("identf_sb", [128, 128], F32)
    pmat_sb = sb("pmat_sb", [4, 64], BF16)
    gidx_sb = sb("gidx_sb", [128, 32], I32)
    wosl = [sb(f"wosl{i}", [128, 32 * 128], BF16) for i in range(2)]

    plan.add("gpsimd", lambda g: g.memset(ones_col[:], 1.0))
    plan.add("gpsimd", lambda g: g.memset(ones_row[:], 1.0))
    plan.add("gpsimd", lambda g: g.memset(negc_sb[:], NEGC))
    plan.add("gpsimd", lambda g: g.memset(eps1[:], EPS), incs=[(cstm, 1)])
    plan.add("gpsimd", lambda g: g.dma_start(out=identb[:], in_=identb_d), incs=[(cst, 16)])
    plan.add("gpsimd", lambda g: g.dma_start(out=identf[:], in_=identf_d), incs=[(cst, 16)])
    plan.add("gpsimd", lambda g: g.dma_start(out=pmat_sb[:], in_=pmat_d), incs=[(cst, 16)])
    plan.add("gpsimd", lambda g: g.dma_start(out=gidx_sb[:], in_=gidx_d), incs=[(cst, 16)])
    const_ld = cst.n
    memset_done = cstm.n

    esAB = ExitStack()
    kt_sb = esAB.enter_context(nc.sbuf_tensor("kt_sb", [128, 2 * SEQ], BF16))
    vt_sb = esAB.enter_context(nc.sbuf_tensor("vt_sb", [128, 2 * SEQ], BF16))

    esA = ExitStack()
    xbf = [esA.enter_context(nc.sbuf_tensor(f"xbf{i}", [128, NDT * CHT], BF16)) for i in range(2)]
    wsl = [esA.enter_context(nc.sbuf_tensor(f"wsl{i}", [128, NDT * 128], BF16)) for i in range(2)]
    xsq = [esA.enter_context(nc.sbuf_tensor(f"xsq{i}", [128, CHT], BF16)) for i in range(4)]
    qstage = [esA.enter_context(nc.sbuf_tensor(f"qstage{i}", [128, CHT], BF16)) for i in range(2)]
    tmpe = [esA.enter_context(nc.sbuf_tensor(f"tmpe{i}", [128, CHT], BF16)) for i in range(2)]
    rrep = esA.enter_context(nc.sbuf_tensor("rrep", [128, CHT], BF16))
    mrrep = esA.enter_context(nc.sbuf_tensor("mrrep", [128, CHT], BF16))

    psA = ExitStack()
    ps_ssum = psA.enter_context(nc.psum_tensor("ps_ssum", [128, 512], F32))
    ps_ssq = psA.enter_context(nc.psum_tensor("ps_ssq", [128, 512], F32))
    ps_bcr = psA.enter_context(nc.psum_tensor("ps_bcr", [128, 512], F32))
    ps_bcm = psA.enter_context(nc.psum_tensor("ps_bcm", [128, 512], F32))
    ps_qkv = [psA.enter_context(nc.psum_tensor(f"ps_qkv{i}", [128, 512], F32)) for i in range(2)]
    ps_wcs = [psA.enter_context(nc.psum_tensor(f"ps_wcs{i}", [128, 1], F32)) for i in range(2)]

    st = dict(xb_ld=[None] * 8, xsq_free=[0, 0, 0, 0], slab_free=[0, 0], wsl_free=[0, 0],
              qkv_evac_ve=[0, 0], stat_rows_ve=0, bc_evac_ve=0,
              wcs_ve=[0] * 12, statmm_te=0)

    def unit_tasks(ch):
        for dt in range(NDT):
            if dt % 4 == 0:
                yield ("load", ch, dt)
            yield ("proc", ch, dt)

    def do_task(task):
        kind, ch, dt = task
        s = ch % 2
        if kind == "load":
            dg = dt // 4
            slot = (ch % 2) * 8 + dg
            src = xt_d.rearrange("(a p) t -> p a t", p=128)[
                :, dg * 4:(dg + 1) * 4, ch * CHT:(ch + 1) * CHT]
            w = []
            if dg == 0:
                w = [(te, st["slab_free"][s])]
            plan.add("gpsimd",
                     lambda g, s=s, dg=dg, src=src: g.dma_start(
                         out=xbf[s][:, dg * 4 * CHT:(dg + 1) * 4 * CHT]
                         .rearrange("p (a t) -> p a t", a=4), in_=src),
                     incs=xbp.inc(slot), waits=w)
            st["xb_ld"][dg] = xbp.at(slot)
            return
        j = dt % 4
        plan.add("vector",
                 lambda e, s=s, dt=dt, j=j: e.tensor_tensor(
                     out=xsq[j][:], in0=xbf[s][:, dt * CHT:(dt + 1) * CHT],
                     in1=xbf[s][:, dt * CHT:(dt + 1) * CHT], op=ALU.mult),
                 incs=[(ve, 1)], waits=[st["xb_ld"][dt // 4], (te, st["xsq_free"][j])])
        sq_ve = ve.n
        w = [(ve, sq_ve)]
        if dt == 0:
            w += [(cst, const_ld), (cstm, memset_done), (ve, st["stat_rows_ve"])]
        plan.add("tensor",
                 lambda e, s=s, dt=dt: e.matmul(
                     ps_ssum[0:1, :], lhsT=ones_col[:],
                     rhs=xbf[s][:, dt * CHT:(dt + 1) * CHT],
                     start=(dt == 0), stop=(dt == NDT - 1)),
                 waits=w)
        mark = (dt % 2 == 1) or (dt == NDT - 1)
        plan.add("tensor",
                 lambda e, j=j, dt=dt: e.matmul(
                     ps_ssq[0:1, :], lhsT=ones_col[:], rhs=xsq[j][:],
                     start=(dt == 0), stop=(dt == NDT - 1)),
                 incs=[(te, 1)] if mark else [])
        if mark:
            st["xsq_free"][(j - 1) % 4] = te.n
            st["xsq_free"][j] = te.n
        if dt == NDT - 1:
            st["statmm_te"] = te.n

    def stats_math(W):
        inv = 1.0 / DM
        plan.add("vector", lambda e, W=W: e.tensor_scalar(out=r_sum[0:1, 0:W], in0=r_sum[0:1, 0:W], scalar1=inv, scalar2=None, op0=ALU.mult))
        plan.add("vector", lambda e, W=W: e.tensor_scalar(out=r_sq[0:1, 0:W], in0=r_sq[0:1, 0:W], scalar1=inv, scalar2=None, op0=ALU.mult))
        plan.add("vector", lambda e, W=W: e.tensor_tensor(out=r_ln[0:1, 0:W], in0=r_sum[0:1, 0:W], in1=r_sum[0:1, 0:W], op=ALU.mult))
        plan.add("vector", lambda e, W=W: e.tensor_tensor(out=r_sq[0:1, 0:W], in0=r_sq[0:1, 0:W], in1=r_ln[0:1, 0:W], op=ALU.subtract),
                 incs=[(ve, 1)])
        var_ve = ve.n
        plan.add("scalar", lambda e, W=W: e.activation(out=r_ln[0:1, 0:W], in_=r_sq[0:1, 0:W], func=AF.Ln, bias=eps1[:], scale=1.0),
                 incs=[(ac, 1)], waits=[(ve, var_ve), (cstm, memset_done)])
        plan.add("scalar", lambda e, W=W: e.activation(out=rstd_bf[0:1, 0:W], in_=r_ln[0:1, 0:W], func=AF.Exp, scale=-0.5),
                 incs=[(ac, 1)], waits=[(ac, ac.n)])
        rstd_ac = ac.n
        plan.add("vector", lambda e, W=W: e.tensor_tensor(out=r_mr[0:1, 0:W], in0=r_sum[0:1, 0:W], in1=rstd_bf[0:1, 0:W], op=ALU.mult),
                 waits=[(ac, rstd_ac)])
        plan.add("vector", lambda e, W=W: e.tensor_copy(mr_bf[0:1, 0:W], r_mr[0:1, 0:W]), incs=[(ve, 1)])

    pending = list(unit_tasks(0))
    while pending:
        do_task(pending.pop(0))

    for ch in range(NCH):
        s = ch % 2
        plan.add("vector", lambda e: e.tensor_copy(r_sum[0:1, 0:512], ps_ssum[0:1, :]),
                 waits=[(te, st["statmm_te"])])
        plan.add("vector", lambda e: e.tensor_copy(r_sq[0:1, 0:512], ps_ssq[0:1, :]),
                 incs=[(ve, 1)])
        st["stat_rows_ve"] = ve.n
        stats_math(512)
        rows_ve = ve.n
        plan.add("tensor", lambda e: e.matmul(ps_bcr[:], lhsT=ones_row[:], rhs=rstd_bf[0:1, 0:512], start=True, stop=True),
                 waits=[(ve, rows_ve), (ve, st["bc_evac_ve"])])
        plan.add("tensor", lambda e: e.matmul(ps_bcm[:], lhsT=ones_row[:], rhs=mr_bf[0:1, 0:512], start=True, stop=True),
                 incs=[(te, 1)])
        bc_te = te.n
        plan.add("vector", lambda e: e.tensor_copy(rrep[:], ps_bcr[:]), waits=[(te, bc_te)])
        plan.add("vector", lambda e: e.tensor_copy(mrrep[:], ps_bcm[:]), incs=[(ve, 1)])
        st["bc_evac_ve"] = ve.n

        xb_qkv = list(st["xb_ld"])
        if ch + 1 < NCH:
            pending = list(unit_tasks(ch + 1))

        for ct in range(NCT):
            wi = ct % 2
            if ch == 0:
                plan.add("gpsimd",
                         lambda g, ct=ct, wi=wi: g.dma_start(
                             out=wsl[wi][:], in_=wqkv_d[ct].rearrange("p t c -> p (t c)")),
                         incs=wlp.inc(wi), waits=[(te, st["wsl_free"][wi])])
                lded = wlp.at(wi)
                plan.add("gpsimd",
                         lambda g, ct=ct, wi=wi: g.dma_start(
                             out=wbf_d[ct].rearrange("p t c -> p (t c)"), in_=wsl[wi][:]),
                         incs=[(wbs, 16)], waits=[lded])
            else:
                w = [(te, st["wsl_free"][wi])]
                if ch == 1 and ct in (0, 1):
                    w.append((wbs, 16 * NCT))
                plan.add("gpsimd",
                         lambda g, ct=ct, wi=wi: g.dma_start(
                             out=wsl[wi][:], in_=wbf_d[ct].rearrange("p t c -> p (t c)")),
                         incs=wlp.inc(wi), waits=w)
            slab_ld = wlp.at(wi)
            pi = ct % 2
            for dt in range(NDT):
                w = [xb_qkv[dt // 4]] if dt % 4 == 0 else []
                if dt == 0:
                    w = [slab_ld, (ve, st["qkv_evac_ve"][pi]), xb_qkv[0]]
                plan.add("tensor",
                         lambda e, s=s, wi=wi, dt=dt, pi=pi: e.matmul(
                             ps_qkv[pi][:], lhsT=wsl[wi][:, dt * 128:(dt + 1) * 128],
                             rhs=xbf[s][:, dt * CHT:(dt + 1) * CHT],
                             start=(dt == 0), stop=(dt == NDT - 1)),
                         waits=w, incs=[(te, 1)] if dt == NDT - 1 else [])
            mm_te = te.n
            if ch == 0:
                wci = ct % 2
                for dt in range(NDT):
                    plan.add("tensor",
                             lambda e, wi=wi, dt=dt, wci=wci: e.matmul(
                                 ps_wcs[wci][:], lhsT=wsl[wi][:, dt * 128:(dt + 1) * 128],
                                 rhs=ones_col[:], start=(dt == 0), stop=(dt == NDT - 1)),
                             waits=[(ve, st["wcs_ve"][ct - 2])] if (dt == 0 and ct >= 2) else [],
                             incs=[(te, 1)] if dt == NDT - 1 else [])
                myte = te.n
                plan.add("vector",
                         lambda e, ct=ct, wci=wci: e.tensor_copy(wcs_sb[:, ct:ct + 1], ps_wcs[wci][:]),
                         incs=[(ve, 1)], waits=[(te, myte)])
                st["wcs_ve"][ct] = ve.n
            st["wsl_free"][wi] = te.n
            ti = ct % 2
            w_ev = [(te, mm_te), (ve, st["wcs_ve"][ct])]
            plan.add("vector",
                     lambda e, pi=pi, ti=ti: e.tensor_tensor(
                         out=tmpe[ti][:], in0=ps_qkv[pi][:], in1=rrep[:], op=ALU.mult),
                     waits=w_ev)
            plan.add("vector",
                     lambda e, ct=ct, ti=ti: e.tensor_scalar(
                         out=tmpe[1 - ti][:], in0=mrrep[:], scalar1=wcs_sb[:, ct:ct + 1],
                         scalar2=None, op0=ALU.mult))
            if ct <= 7:
                qi = ct % 2
                plan.add("vector",
                         lambda e, qi=qi, ti=ti: e.tensor_tensor(
                             out=qstage[qi][:], in0=tmpe[ti][:], in1=tmpe[1 - ti][:], op=ALU.subtract),
                         incs=[(ve, 1)], waits=[qsp.at(qi)])
                st["qkv_evac_ve"][pi] = ve.n
                myve = ve.n
                plan.add("sync",
                         lambda e, ct=ct, ch=ch, qi=qi: e.dma_start(
                             out=qscr_d[ct, :, ch * CHT:(ch + 1) * CHT], in_=qstage[qi][:]),
                         incs=qsp.inc(qi), waits=[(ve, myve)])
            else:
                dsttn = kt_sb if ct <= 9 else vt_sb
                off = (ct - 8 if ct <= 9 else ct - 10) * SEQ + ch * CHT
                plan.add("vector",
                         lambda e, dsttn=dsttn, off=off, ti=ti: e.tensor_tensor(
                             out=dsttn[:, off:off + CHT], in0=tmpe[ti][:], in1=tmpe[1 - ti][:],
                             op=ALU.subtract),
                         incs=[(ve, 1)])
                st["qkv_evac_ve"][pi] = ve.n
            for _ in range(6):
                if pending:
                    do_task(pending.pop(0))
        st["slab_free"][s] = te.n
        while pending:
            do_task(pending.pop(0))

    teA, veA, acA = te.n, ve.n, ac.n
    qscr_done = qsp.totals()
    esA.close()
    psA.close()

    if debug == "A":
        plan.add("sync", lambda e: e.dma_start(out=dbg["dkv"][:, 0:2 * SEQ], in_=kt_sb[:]),
                 incs=[(dbs, 16)], waits=[(ve, veA)] + qscr_done)
        plan.add("sync", lambda e: e.dma_start(out=dbg["dkv"][:, 2 * SEQ:], in_=vt_sb[:]), incs=[(dbs, 16)])
        for k, row in enumerate([r_sum, r_sq, r_ln, r_sum]):
            plan.add("sync", lambda e, k=k, row=row: e.dma_start(out=dbg["drow"][k:k + 1, :], in_=row[0:1, :]),
                     incs=[(dbs, 16)])
        plan.add("sync", lambda e, v=dbs.n: e.wait_ge(dbs.h, v))
        return _finish(nc, plan, [es, esAB])

    esB = ExitStack()
    expT = [esB.enter_context(nc.sbuf_tensor(f"expT{i}", [128, NDT * 512], BF16)) for i in range(2)]
    vp_sb = esB.enter_context(nc.sbuf_tensor("vp_sb", [128, 2 * 32 * 129], BF16))
    qtb = [esB.enter_context(nc.sbuf_tensor(f"qtb{i}", [128, 512], BF16)) for i in range(2)]
    out_qc = [esB.enter_context(nc.sbuf_tensor(f"outqc{i}", [128, 4096], BF16)) for i in range(4)]
    masks_sb = esB.enter_context(nc.sbuf_tensor("masks_sb", [128, 4 * 512], BF16))
    rec_sb = esB.enter_context(nc.sbuf_tensor("rec_sb", [128, 4], F32))

    plan.add("gpsimd", lambda g: g.memset(vp_sb[:], 1.0), incs=[(vpm, 1)], waits=[(ve, veA)])
    plan.add("gpsimd", lambda g: g.dma_start(out=masks_sb[:].rearrange("p (j q) -> p j q", j=4), in_=masks_d),
             incs=[(mks, 16)])
    mks_ld = mks.n

    psT = ExitStack()
    ps_tr = [psT.enter_context(nc.psum_tensor(f"ps_tr{i}", [128, 128], BF16)) for i in range(2)]
    tr_free = [0, 0]
    for g2 in range(2):
        for kt in range(32):
            i = (g2 * 32 + kt) % 2
            w = [(ve, tr_free[i])]
            if g2 == 0 and kt == 0:
                w += [(te, teA), (ve, veA), (cst, const_ld)]
            plan.add("tensor",
                     lambda e, g2=g2, kt=kt, i=i: e.transpose(
                         ps_tr[i][:], in_=vt_sb[:, g2 * SEQ + kt * 128: g2 * SEQ + (kt + 1) * 128],
                         identity=identb[:]),
                     incs=[(te, 1)], waits=w)
            myte = te.n
            plan.add("vector",
                     lambda e, g2=g2, kt=kt, i=i: e.tensor_copy(
                         vp_sb[:, (g2 * 32 + kt) * 129:(g2 * 32 + kt) * 129 + 128], ps_tr[i][:]),
                     incs=[(ve, 1)], waits=[(te, myte), (mks, mks_ld), (vpm, 1)])
            tr_free[i] = ve.n
    psT.close()
    vbuild_ve = ve.n
    vbuild_te = te.n

    for dtp in range(2):
        plan.add("gpsimd",
                 lambda g, dtp=dtp: g.dma_start(
                     out=wosl[dtp][:], in_=wo_d[dtp].rearrange("p t c -> p (t c)")),
                 incs=wop.inc(dtp), waits=[(ve, veA)])

    psB = ExitStack()
    ps_sc = [psB.enter_context(nc.psum_tensor(f"ps_sc{i}", [128, 1536], F32)) for i in range(2)]
    ps_av = [psB.enter_context(nc.psum_tensor(f"ps_av{i}", [128, 258], F32)) for i in range(2)]

    bst = dict(sc_free=[0, 0], av_free=[0, 0], qtb_free=[0, 0], gidx=0)
    units = [(qc, lh) for qc in range(NCH) for lh in range(8)]
    exp_ac = {}
    mask_ve = {}
    av_done = {-2: 0, -1: 0}

    def gen_scores(ui):
        qc, lh = units[ui]
        g2 = lh // 4
        nkt = 4 * (qc + 1)
        ei = ui % 2
        qi = ui % 2
        w = [(te, bst["qtb_free"][qi])]
        if ui in (0, 1):
            w += qscr_done
        plan.add("sync",
                 lambda e, qc=qc, lh=lh, qi=qi: e.dma_start(
                     out=qtb[qi][:], in_=qscr_d[lh, :, qc * 512:(qc + 1) * 512]),
                 incs=qtp.inc(qi), waits=w)
        qt_ld = qtp.at(qi)
        exp_ac[ui] = [0] * nkt
        mask_ve[ui] = [0] * 4
        kt0 = 0
        while kt0 < nkt:
            gn = min(3, nkt - kt0)
            gi = bst["gidx"] % 2
            bst["gidx"] += 1
            for j in range(gn):
                kt = kt0 + j
                w = []
                if j == 0:
                    w = [(ac, bst["sc_free"][gi]), qt_ld]
                    if ui == 0 and kt0 == 0:
                        w.append((te, vbuild_te))
                plan.add("tensor",
                         lambda e, g2=g2, kt=kt, j=j, gi=gi, qi=qi: e.matmul(
                             ps_sc[gi][:, j * 512:(j + 1) * 512],
                             lhsT=kt_sb[:, g2 * SEQ + kt * 128: g2 * SEQ + (kt + 1) * 128],
                             rhs=qtb[qi][:], start=True, stop=True),
                         waits=w, incs=[(te, 1)] if j == gn - 1 else [])
            bst["qtb_free"][qi] = te.n
            grp_te = te.n
            plan.add("scalar",
                     lambda e, ei=ei, kt0=kt0, gn=gn, gi=gi: e.activation(
                         out=expT[ei][:, kt0 * 512:(kt0 + gn) * 512],
                         in_=ps_sc[gi][:, 0:gn * 512], func=AF.Exp,
                         bias=negc_sb[:], scale=ATT_SCALE),
                     incs=[(ac, 1)],
                     waits=[(te, max(grp_te, av_done[ui - 2])), (cstm, memset_done)])
            bst["sc_free"][gi] = ac.n
            for j in range(gn):
                exp_ac[ui][kt0 + j] = ac.n
            yield
            kt0 += gn
        for j in range(4):
            kt = 4 * qc + j
            plan.add("gpsimd",
                     lambda g, ei=ei, kt=kt, j=j: g.tensor_tensor(
                         out=expT[ei][:, kt * 512:(kt + 1) * 512],
                         in0=expT[ei][:, kt * 512:(kt + 1) * 512],
                         in1=masks_sb[:, j * 512:(j + 1) * 512], op=ALU.mult),
                     incs=[(vpm, 1)],
                     waits=[(ac, exp_ac[ui][kt]), (mks, mks_ld), (ve, vbuild_ve),
                            (te, av_done[ui - 2])])
            mask_ve[ui][j] = vpm.n
        yield

    def gen_av(ui):
        qc, lh = units[ui]
        g2 = lh // 4
        ei = ui % 2
        oi = qc % 4
        for qs in range(4):
            pi = qs // 2
            nkt_q = 4 * qc + qs + 1
            for kt in range(nkt_q):
                w = []
                if kt == 0:
                    w = [(ac, exp_ac[ui][nkt_q - 1]), (vpm, mask_ve[ui][qs]),
                         (ve, bst["av_free"][pi])]
                plan.add("tensor",
                         lambda e, ei=ei, kt=kt, qs=qs, g2=g2, pi=pi: e.matmul(
                             ps_av[pi][:, (qs % 2) * 129:(qs % 2) * 129 + 129],
                             lhsT=expT[ei][:, kt * 512 + qs * 128: kt * 512 + (qs + 1) * 128],
                             rhs=vp_sb[:, (g2 * 32 + kt) * 129:(g2 * 32 + kt + 1) * 129],
                             start=(kt == 0), stop=(kt == nkt_q - 1)),
                         waits=w, incs=[(te, 1)] if kt == nkt_q - 1 else [])
            avte = te.n
            if qs % 2 == 1:
                plan.add("vector",
                         lambda e, pi=pi: e.reciprocal(
                             out=rec_sb[:, 2 * pi:2 * pi + 2], in_=ps_av[pi][:, 128::129]),
                         incs=[(ve, 1)], waits=[(te, avte), oqp.at(oi)])
                myve = ve.n
                for sq in range(2):
                    qs2 = (qs - 1) + sq
                    plan.add("vector",
                             lambda e, pi=pi, sq=sq, qs2=qs2, lh=lh, oi=oi: e.tensor_scalar(
                                 out=out_qc[oi][:, (qs2 * 8 + lh) * 128:(qs2 * 8 + lh + 1) * 128],
                                 in0=ps_av[pi][:, sq * 129: sq * 129 + 128],
                                 scalar1=rec_sb[:, 2 * pi + sq:2 * pi + sq + 1],
                                 scalar2=None, op0=ALU.mult),
                             incs=[(ve, 1)] if sq == 1 else [],
                             waits=[(ve, myve)])
                bst["av_free"][pi] = ve.n
            yield
        av_done[ui] = te.n

    g_sc = gen_scores(0)
    for _ in g_sc:
        pass
    for ui in range(len(units)):
        g_av = gen_av(ui)
        g_next = gen_scores(ui + 1) if ui + 1 < len(units) else iter(())
        done_av = done_sc = False
        while not (done_av and done_sc):
            if not done_av:
                done_av = next(g_av, "end") == "end"
            if not done_sc:
                done_sc = next(g_next, "end") == "end"
            if not done_sc:
                done_sc = next(g_next, "end") == "end"
        qc, lh = units[ui]
        if lh == 7:
            oi = qc % 4
            myve = ve.n
            plan.add("sync",
                     lambda e, qc=qc, oi=oi: e.dma_start(
                         out=ccin_d[qc * 512:(qc + 1) * 512, :].rearrange("(s p) x -> p s x", s=4),
                         in_=out_qc[oi][:].rearrange("p (s x) -> p s x", s=4)),
                     incs=oqp.inc(oi), waits=[(ve, myve)])
            if debug != "B":
                plan.add("gpsimd",
                         lambda g, qc=qc: g.collective_compute(
                             "AllGather", ALU.bypass,
                             replica_groups=[[0, 1, 2, 3], [4, 5, 6, 7]],
                             ins=[ccin_d[qc * 512:(qc + 1) * 512, :].opt()],
                             outs=[ccout_d[qc * 2048:(qc + 1) * 2048, :].opt()]),
                         incs=[(ccs, 1)], waits=[oqp.at(oi)])

    teB, veB, acB = te.n, ve.n, ac.n
    psB.close()
    esB.close()
    esAB.close()

    if debug == "B":
        plan.add("sync", lambda e, w=tuple(oqp.totals()): [e.wait_ge(s.h, v) for s, v in w] and None)
        return _finish(nc, plan, [es])

    esC = ExitStack()
    ot_sb = esC.enter_context(nc.sbuf_tensor("ot_sb", [128, 32 * 1024], BF16))
    y_sb = esC.enter_context(nc.sbuf_tensor("y_sb", [128, 32 * 1024], BF16))
    xfc = esC.enter_context(nc.sbuf_tensor("xfc0", [128, 1024], F32))
    ysq0 = esC.enter_context(nc.sbuf_tensor("ysq0", [128, 1024], BF16))
    outf = [xfc, esC.enter_context(nc.sbuf_tensor("outf1", [128, 1024], F32))]
    stage = [esC.enter_context(nc.sbuf_tensor(f"stage{i}", [128, 4 * 1024], BF16)) for i in range(2)]
    awr_sb = esC.enter_context(nc.sbuf_tensor("awr_sb", [128, NDT * 68], BF16))
    btil_sb = esC.enter_context(nc.sbuf_tensor("btil_sb", [64, DM], BF16))
    midlog = esC.enter_context(nc.sbuf_tensor("midlog", [68, 1024], F32))
    r2rep = esC.enter_context(nc.sbuf_tensor("r2rep", [68, 1024], BF16))
    mr2rep = esC.enter_context(nc.sbuf_tensor("mr2rep", [68, 1024], BF16))
    acs_sb = esC.enter_context(nc.sbuf_tensor("acs_sb", [68, 1], F32))
    grep_sb = stage[1]
    midg = grep_sb
    gatest = stage[0]
    lgn = [esC.enter_context(nc.sbuf_tensor(f"lgn{i}", [128, 4], F32)) for i in range(2)]
    g4bf = esC.enter_context(nc.sbuf_tensor("g4bf", [128, 4], BF16))
    gwork = esC.enter_context(nc.sbuf_tensor("gwork", [128, 8], F32))
    gtmp0 = esC.enter_context(nc.sbuf_tensor("gtmp0", [128, 8], F32))
    gtmp = [gtmp0[:, 0:4], gtmp0[:, 4:8]]

    plan.add("gpsimd", lambda g: g.dma_start(out=awr_sb[:].rearrange("p (t j) -> p t j", t=NDT), in_=awr_d),
             incs=[(cdw, 16)], waits=[(te, teB), (ve, veB)])
    plan.add("gpsimd", lambda g: g.dma_start(out=btil_sb[:], in_=btil_d), incs=[(cdw, 16)])
    cd_w = cdw.n

    psC = ExitStack()
    ps_tr2 = [psC.enter_context(nc.psum_tensor(f"ps_tr2{i}", [128, 512], F32)) for i in range(2)]
    ps_y2 = [psC.enter_context(nc.psum_tensor(f"ps_y2{i}", [128, 512], F32)) for i in range(2)]
    ps_cs = psC.enter_context(nc.psum_tensor("ps_cs", [1, 1024], F32))
    ps_cq = psC.enter_context(nc.psum_tensor("ps_cq", [1, 1024], F32))

    cst_ = dict(wo_free=[0, 0], xfc_free=0, ysq_free=0, y_evac_ve=[0, 0], sq_ve=0)
    tr_free = [0, 0]
    tr_te_grp = {-2: 0, -1: 0}
    tr_rank_ve = [0] * 4

    def emit_stats(dt, sq_ve):
        for h in range(2):
            plan.add("tensor",
                     lambda e, dt=dt, h=h: e.matmul(
                         ps_cs[0:1, h * 512:(h + 1) * 512], lhsT=ones_col[:],
                         rhs=y_sb[:, dt * 1024 + h * 512: dt * 1024 + h * 512 + 512],
                         start=(dt == 0), stop=(dt == NDT - 1)),
                     waits=[(ve, sq_ve)] if h == 0 else [])
            plan.add("tensor",
                     lambda e, dt=dt, h=h: e.matmul(
                         ps_cq[0:1, h * 512:(h + 1) * 512], lhsT=ones_col[:],
                         rhs=ysq0[:, h * 512:(h + 1) * 512],
                         start=(dt == 0), stop=(dt == NDT - 1)),
                     incs=[(te, 1)] if h == 1 else [])
        cst_["ysq_free"] = te.n

    plan.add("sync", lambda e: e.dma_start(out=xfc[:], in_=xtc_d[0:128, :]),
             incs=xcp.inc(0), waits=[])
    xfc_ld0 = xcp.at(0)

    for r in range(4):
        for th in range(2):
            grp = r * 2 + th
            si = grp % 2
            for s4 in range(4):
                blk = r * 8 + th * 4 + s4
                w = [(ccs, 8), (te, tr_te_grp[grp - 2])]
                plan.add("gpsimd",
                         lambda g, blk=blk, s4=s4, si=si: g.indirect_dma_start(
                             out=stage[si][:, s4 * 1024:(s4 + 1) * 1024], out_offset=None,
                             in_=ccout_d,
                             in_offset=bass_rust.IndirectOffsetOnAxis(ap=gidx_sb[:, blk:blk + 1], axis=0)),
                         incs=stp.inc(blk % 2), waits=w)
            g_ld = [stp.at(0), stp.at(1)]
            for loc in range(8):
                pi = loc % 2
                for j in range(4):
                    w = []
                    if j == 0:
                        w = g_ld + [(ve, tr_free[pi])]
                        if grp == 0 and loc == 0:
                            w += [(te, teB), (ve, veB), (ac, acB)]
                    plan.add("tensor",
                             lambda e, si=si, loc=loc, j=j, pi=pi: e.matmul(
                                 ps_tr2[pi][:, j * 128:(j + 1) * 128],
                                 lhsT=stage[si][:, j * 1024 + loc * 128: j * 1024 + loc * 128 + 128],
                                 rhs=identb[:], start=True, stop=True),
                             waits=w, incs=[(te, 1)] if j == 3 else [])
                my_te = te.n
                plan.add("vector",
                         lambda e, pi=pi, r=r, loc=loc, th=th: e.tensor_copy(
                             ot_sb[:, (r * 8 + loc) * 1024 + th * 512:
                                   (r * 8 + loc) * 1024 + th * 512 + 512],
                             ps_tr2[pi][:]),
                         incs=[(ve, 1)], waits=[(te, my_te)])
                tr_free[pi] = ve.n
            tr_te_grp[grp] = te.n
        tr_rank_ve[r] = ve.n
        if debug == "G":
            continue
        for h in range(2):
            for loc in range(8):
                ot = r * 8 + loc
                w = []
                if loc == 0:
                    w = [(ve, tr_rank_ve[r])]
                    if r == 0:
                        w += [wop.at(0)]
                plan.add("tensor",
                         lambda e, ot=ot, h=h: e.matmul(
                             ps_y2[h][:],
                             lhsT=wosl[0][:, ot * 128:(ot + 1) * 128],
                             rhs=ot_sb[:, ot * 1024 + h * 512: ot * 1024 + h * 512 + 512],
                             start=(ot == 0), stop=(ot == 31)),
                         waits=w, incs=[(te, 1)] if ot == 31 else [])
            if r == 3:
                cst_["ymm_h" + str(h)] = te.n
    ot_done_ve = tr_rank_ve[3]

    if debug == "G":
        plan.add("sync", lambda e: e.dma_start(out=dbg["dy"], in_=ot_sb[:]), incs=[(dbs, 16)],
                 waits=[(ve, ot_done_ve)])
        plan.add("sync", lambda e, v=dbs.n: e.wait_ge(dbs.h, v))
        psC.close()
        return _finish(nc, plan, [es, esC])

    for dt in range(NDT):
        i = dt % 2
        if dt >= 2:
            plan.add("gpsimd",
                     lambda g, dt=dt, i=i: g.dma_start(
                         out=wosl[i][:], in_=wo_d[dt].rearrange("p t c -> p (t c)")),
                     incs=wop.inc(i), waits=[(te, cst_["wo_free"][i])])
        wo_ld = wop.at(i)
        if dt > 0:
            plan.add("sync",
                     lambda e, dt=dt: e.dma_start(
                         out=xfc[:], in_=xtc_d[dt * 128:(dt + 1) * 128, :]),
                     incs=xcp.inc(0), waits=[(ve, cst_["xfc_free"])])
        xfc_ld = xcp.at(0)
        if dt > 0:
            for h in range(2):
                for ot in range(32):
                    w = []
                    if ot == 0:
                        w = [wo_ld, (ve, cst_["y_evac_ve"][h])]
                    plan.add("tensor",
                             lambda e, i=i, ot=ot, h=h: e.matmul(
                                 ps_y2[h][:],
                                 lhsT=wosl[i][:, ot * 128:(ot + 1) * 128],
                                 rhs=ot_sb[:, ot * 1024 + h * 512: ot * 1024 + h * 512 + 512],
                                 start=(ot == 0), stop=(ot == 31)),
                             waits=w, incs=[(te, 1)] if ot == 31 else [])
                cst_["ymm_h" + str(h)] = te.n
        cst_["wo_free"][i] = te.n
        for h in range(2):
            plan.add("vector",
                     lambda e, dt=dt, h=h: e.tensor_tensor(
                         out=y_sb[:, dt * 1024 + h * 512: dt * 1024 + h * 512 + 512],
                         in0=ps_y2[h][:], in1=xfc[:, h * 512:(h + 1) * 512], op=ALU.add),
                     incs=[(ve, 1)],
                     waits=[(te, cst_["ymm_h" + str(h)]), xfc_ld if dt > 0 else xfc_ld0])
            cst_["y_evac_ve"][h] = ve.n
        cst_["xfc_free"] = ve.n
        if dt > 0:
            emit_stats(dt - 1, cst_["sq_ve"])
        plan.add("vector",
                 lambda e, dt=dt: e.tensor_tensor(
                     out=ysq0[:], in0=y_sb[:, dt * 1024:(dt + 1) * 1024],
                     in1=y_sb[:, dt * 1024:(dt + 1) * 1024], op=ALU.mult),
                 incs=[(ve, 1)], waits=[(te, cst_["ysq_free"])])
        cst_["sq_ve"] = ve.n
    emit_stats(NDT - 1, cst_["sq_ve"])
    cstat_te = te.n

    plan.add("vector", lambda e: e.tensor_copy(r_sum[0:1, :], ps_cs[0:1, :]), waits=[(te, cstat_te)])
    plan.add("vector", lambda e: e.tensor_copy(r_sq[0:1, :], ps_cq[0:1, :]), incs=[(ve, 1)])
    stats_math(1024)
    rows2_ve = ve.n
    psC.close()

    if debug == "C":
        plan.add("sync", lambda e: e.dma_start(out=dbg["dy"], in_=y_sb[:]), incs=[(dbs, 16)],
                 waits=[(ve, rows2_ve)])
        for k, row in enumerate([r_sum, r_sq, r_ln, r_sum]):
            plan.add("sync", lambda e, k=k, row=row: e.dma_start(out=dbg["drow"][k:k + 1, :], in_=row[0:1, :]),
                     incs=[(dbs, 16)])
        plan.add("sync", lambda e, v=dbs.n: e.wait_ge(dbs.h, v))
        return _finish(nc, plan, [es, esC])

    psD1 = ExitStack()
    ps_mid = psD1.enter_context(nc.psum_tensor("ps_mid", [68, 1024], F32))
    ps_acs = psD1.enter_context(nc.psum_tensor("ps_acs", [68, 1], F32))
    ps_b68 = psD1.enter_context(nc.psum_tensor("ps_b68", [68, 1024], F32))

    for dt in range(NDT):
        plan.add("tensor",
                 lambda e, dt=dt: e.matmul(
                     ps_acs[:], lhsT=awr_sb[:, dt * 68:(dt + 1) * 68], rhs=ones_col[:],
                     start=(dt == 0), stop=(dt == NDT - 1)),
                 waits=[(cdw, cd_w), (te, cstat_te), (ve, rows2_ve)] if dt == 0 else [])
        for h in range(2):
            plan.add("tensor",
                     lambda e, dt=dt, h=h: e.matmul(
                         ps_mid[:, h * 512:(h + 1) * 512],
                         lhsT=awr_sb[:, dt * 68:(dt + 1) * 68],
                         rhs=y_sb[:, dt * 1024 + h * 512: dt * 1024 + h * 512 + 512],
                         start=(dt == 0), stop=(dt == NDT - 1)),
                     incs=[(te, 1)] if (dt == NDT - 1 and h == 1) else [])
    mid_te = te.n
    plan.add("vector", lambda e: e.tensor_copy(acs_sb[:], ps_acs[:]), incs=[(ve, 1)],
             waits=[(te, mid_te)])
    acs_ve = ve.n

    for h in range(2):
        plan.add("tensor",
                 lambda e, h=h: e.matmul(
                     ps_b68[:, h * 512:(h + 1) * 512], lhsT=ones_row[0:1, 0:68],
                     rhs=rstd_bf[0:1, h * 512:(h + 1) * 512], start=True, stop=True),
                 waits=[(ve, rows2_ve)] if h == 0 else [], incs=[(te, 1)] if h == 1 else [])
    plan.add("vector", lambda e: e.tensor_copy(r2rep[:], ps_b68[:]), incs=[(ve, 1)],
             waits=[(te, te.n)])
    r2_ve = ve.n
    for h in range(2):
        plan.add("tensor",
                 lambda e, h=h: e.matmul(
                     ps_b68[:, h * 512:(h + 1) * 512], lhsT=ones_row[0:1, 0:68],
                     rhs=mr_bf[0:1, h * 512:(h + 1) * 512], start=True, stop=True),
                 waits=[(ve, r2_ve)] if h == 0 else [], incs=[(te, 1)] if h == 1 else [])
    plan.add("vector", lambda e: e.tensor_copy(mr2rep[:], ps_b68[:]), incs=[(ve, 1)],
             waits=[(te, te.n)])

    plan.add("vector", lambda e: e.tensor_tensor(out=midlog[:], in0=ps_mid[:], in1=r2rep[:], op=ALU.mult),
             waits=[(ve, ve.n)])
    plan.add("vector", lambda e: e.tensor_scalar(out=ysq0[0:68, :], in0=mr2rep[:], scalar1=acs_sb[:],
                                                 scalar2=None, op0=ALU.mult),
             waits=[(ve, acs_ve)])
    plan.add("vector", lambda e: e.tensor_tensor(out=midlog[:], in0=midlog[:], in1=ysq0[0:68, :], op=ALU.subtract),
             incs=[(ve, 1)])
    midlog_ve = ve.n
    psD1.close()

    psD2 = ExitStack()
    ps_tr4 = psD2.enter_context(nc.psum_tensor("ps_tr4", [128, 4], F32))
    ps_gt = psD2.enter_context(nc.psum_tensor("ps_gt", [4, 128], BF16))
    ps_grep = psD2.enter_context(nc.psum_tensor("ps_grep", [64, 1024], F32))
    lgn_free = [0, 0]
    tr4_free = 0
    gt_copy_ve = 0
    prev_gt_te = 0
    for tc in range(8):
        i = tc % 2
        w = [(ve, lgn_free[i]), (ve, tr4_free)]
        if tc == 0:
            w.append((ve, midlog_ve))
        plan.add("tensor",
                 lambda e, tc=tc: e.transpose(
                     ps_tr4[:], in_=midlog[64:68, tc * 128:(tc + 1) * 128], identity=identf[64:68, 64:68]),
                 incs=[(te, 1)], waits=w)
        myte = te.n
        plan.add("vector", lambda e, i=i: e.tensor_copy(lgn[i][:], ps_tr4[:]), incs=[(ve, 1)],
                 waits=[(te, myte)])
        tr4_free = ve.n

        def V(fn, waits=()):
            plan.add("vector", fn, incs=[(ve, 1)], waits=list(waits) + [(ve, ve.n)])
        V(lambda e, i=i: e.tensor_reduce(out=gwork[:, 0:1], in_=lgn[i][:], axis=AX.X, op=ALU.max))
        m1_ve = ve.n
        V(lambda e: e.tensor_scalar(out=gwork[:, 1:2], in0=gwork[:, 0:1], scalar1=-1.0, scalar2=None, op0=ALU.mult))
        negm1_ve = ve.n
        plan.add("scalar",
                 lambda e, i=i: e.activation(out=gtmp[0], in_=lgn[i][:], func=AF.Exp, bias=gwork[:, 1:2], scale=1.0),
                 incs=[(ac, 1)], waits=[(ve, negm1_ve)])
        e4_ac = ac.n
        V(lambda e, i=i: e.tensor_scalar(out=gtmp[1], in0=lgn[i][:], scalar1=gwork[:, 0:1], scalar2=BIG,
                                         op0=ALU.is_equal, op1=ALU.mult))
        V(lambda e, i=i: e.tensor_tensor(out=gtmp[1], in0=lgn[i][:], in1=gtmp[1], op=ALU.subtract))
        V(lambda e: e.tensor_reduce(out=gwork[:, 2:3], in_=gtmp[1], axis=AX.X, op=ALU.max))
        V(lambda e, i=i: e.tensor_scalar(out=gtmp[1], in0=lgn[i][:], scalar1=gwork[:, 2:3], scalar2=None,
                                         op0=ALU.is_ge))
        lgn_free[i] = ve.n
        V(lambda e: e.tensor_tensor(out=gtmp[0], in0=gtmp[0], in1=gtmp[1], op=ALU.mult),
          waits=[(ac, e4_ac)])
        V(lambda e: e.tensor_reduce(out=gwork[:, 3:4], in_=gtmp[0], axis=AX.X, op=ALU.add))
        V(lambda e: e.reciprocal(out=gwork[:, 4:5], in_=gwork[:, 3:4]))
        V(lambda e: e.tensor_scalar(out=g4bf[:], in0=gtmp[0], scalar1=gwork[:, 4:5], scalar2=4.0,
                                    op0=ALU.mult, op1=ALU.mult),
          waits=[(te, prev_gt_te)])
        g4_ve = ve.n
        plan.add("tensor",
                 lambda e: e.transpose(ps_gt[:], in_=g4bf[:], identity=identb[:]),
                 incs=[(te, 1)], waits=[(ve, g4_ve), (ve, gt_copy_ve)])
        prev_gt_te = te.n
        plan.add("vector", lambda e, tc=tc: e.tensor_copy(gatest[0:4, tc * 128:(tc + 1) * 128], ps_gt[:]),
                 incs=[(ve, 1)], waits=[(te, prev_gt_te)])
        gt_copy_ve = ve.n
    gates_ve = ve.n

    for h in range(2):
        plan.add("tensor",
                 lambda e, h=h: e.matmul(
                     ps_grep[:, h * 512:(h + 1) * 512], lhsT=pmat_sb[:],
                     rhs=gatest[0:4, h * 512:(h + 1) * 512], start=True, stop=True),
                 waits=[(ve, gates_ve), (cst, const_ld)] if h == 0 else [],
                 incs=[(te, 1)] if h == 1 else [])
    plan.add("vector", lambda e: e.tensor_copy(grep_sb[0:64, 0:1024], ps_grep[:]), incs=[(ve, 1)],
             waits=[(te, te.n)])
    if debug == "D":
        plan.add("sync", lambda e: e.dma_start(out=dbg["dgw"], in_=gwork[:]), incs=[(dbs, 16)],
                 waits=[(ve, gates_ve)])
        plan.add("sync", lambda e: e.dma_start(out=dbg["dlg"], in_=lgn[1][:]), incs=[(dbs, 16)])
        plan.add("sync", lambda e: e.dma_start(out=dbg["dg0"], in_=gtmp[0]), incs=[(dbs, 16)])
        plan.add("sync", lambda e: e.dma_start(out=dbg["dg1"], in_=gtmp[1]), incs=[(dbs, 16)])
        plan.add("sync", lambda e: e.dma_start(out=dbg["dml"], in_=midlog[:]), incs=[(dbs, 16)],
                 waits=[(ve, midlog_ve)])
        plan.add("sync", lambda e: e.dma_start(out=dbg["dgt"], in_=gatest[0:4, 0:1024]), incs=[(dbs, 16)],
                 waits=[(ve, gates_ve)])
        plan.add("sync", lambda e: e.dma_start(out=dbg["dgr"], in_=grep_sb[0:64, 0:1024]), incs=[(dbs, 16)],
                 waits=[(ve, ve.n)])
        plan.add("sync", lambda e: e.dma_start(out=dbg["dac"], in_=acs_sb[:]), incs=[(dbs, 16)],
                 waits=[(ve, acs_ve)])
        plan.add("sync", lambda e: e.dma_start(out=dbg["dr2"][:, 0:1024], in_=r2rep[:]), incs=[(dbs, 16)])
        plan.add("sync", lambda e: e.dma_start(out=dbg["dr2"][:, 1024:2048], in_=mr2rep[:]), incs=[(dbs, 16)])
    plan.add("vector", lambda e: e.tensor_tensor(out=midg[0:64, 0:1024], in0=midlog[0:64, :], in1=grep_sb[0:64, 0:1024], op=ALU.mult),
             incs=[(ve, 1)])
    midg_ve = ve.n
    psD2.close()

    psD3 = ExitStack()
    ps_d = [psD3.enter_context(nc.psum_tensor(f"ps_d{i}", [128, 1024], F32)) for i in range(2)]
    d_free = [0, 0]
    for dt in range(NDT):
        i = dt % 2
        for h in range(2):
            w = []
            if h == 0:
                w = [(ve, midg_ve), (ve, d_free[i])]
            plan.add("tensor",
                     lambda e, dt=dt, i=i, h=h: e.matmul(
                         ps_d[i][:, h * 512:(h + 1) * 512],
                         lhsT=btil_sb[:, dt * 128:(dt + 1) * 128],
                         rhs=midg[0:64, h * 512:(h + 1) * 512], start=True, stop=True),
                     waits=w, incs=[(te, 1)] if h == 1 else [])
        dte = te.n
        plan.add("vector",
                 lambda e, dt=dt, i=i: e.tensor_tensor(
                     out=outf[i][:], in0=ps_d[i][:], in1=y_sb[:, dt * 1024:(dt + 1) * 1024], op=ALU.add),
                 incs=[(ve, 1)], waits=[(te, dte), ofp.at(i)])
        d_free[i] = ve.n
        myve = ve.n
        plan.add("sync",
                 lambda e, dt=dt, i=i: e.dma_start(out=yt_d[dt * 128:(dt + 1) * 128, :], in_=outf[i][:]),
                 incs=ofp.inc(i), waits=[(ve, myve)])

    fin = tuple(ofp.totals())
    plan.add("sync", lambda e, w=fin: [e.wait_ge(s.h, v) for s, v in w] and None)
    psD3.close()
    return _finish(nc, plan, [es, esC])


def _finish(nc, plan, stacks):
    block_cm = nc.Block()
    block = block_cm.__enter__()
    for eng in ENGINES:
        lst = plan.ops[eng]
        if not lst:
            continue
        dec = getattr(block, eng)

        def section(e, lst=lst):
            hwm = {}
            for waits, fn, incs in lst:
                for s, v in waits:
                    if v > hwm.get(s.name, 0):
                        e.wait_ge(s.h, v)
                        hwm[s.name] = v
                inst = fn(e) if fn is not None else None
                for s, k in incs:
                    inst.then_inc(s.h, k)
        dec(section)
    block_cm.__exit__(None, None, None)
    for stk in stacks[::-1]:
        try:
            stk.close()
        except Exception:
            pass
    return nc


_CACHE = {}


def _host_prep(L0_attn_x, L0_attn_wq, L0_attn_wk, L0_attn_wv, L0_attn_wo,
               L0_ffn_A, L0_ffn_B, L0_ffn_Wr):
    x = np.asarray(L0_attn_x, np.float32)
    wq = np.asarray(L0_attn_wq, np.float32)
    wk = np.asarray(L0_attn_wk, np.float32)
    wv = np.asarray(L0_attn_wv, np.float32)
    wo = np.asarray(L0_attn_wo, np.float32)
    A = np.asarray(L0_ffn_A, np.float32)
    B = np.asarray(L0_ffn_B, np.float32)
    Wr = np.asarray(L0_ffn_Wr, np.float32)

    xts = [np.ascontiguousarray(x[b].T) for b in range(2)]
    wo_arr = np.ascontiguousarray(wo.reshape(32, 128, 32, 128).transpose(2, 1, 0, 3))
    awr = np.concatenate([A.transpose(2, 0, 1).reshape(DM, 64), Wr.T], axis=1)
    awr_arr = np.ascontiguousarray(awr.reshape(NDT, 128, 68).transpose(1, 0, 2))
    btil_arr = np.ascontiguousarray(B.transpose(0, 2, 1).reshape(64, DM))
    masks = (np.arange(512)[None, None, :] >=
             (128 * np.arange(4)[None, :, None] + np.arange(128)[:, None, None])).astype(bfnp)
    identb = np.eye(128, dtype=bfnp)
    identf = np.eye(128, dtype=np.float32)
    pmat = (np.arange(64)[None, :] // 16 == np.arange(4)[:, None]).astype(bfnp)

    in_maps = []
    for core in range(8):
        b, g = core // 4, core % 4
        wcat = np.concatenate([wq[:, g * 1024:(g + 1) * 1024],
                               wk[:, g * 256:(g + 1) * 256],
                               wv[:, g * 256:(g + 1) * 256]], axis=1)
        wqkv_arr = np.ascontiguousarray(wcat.reshape(NDT, 128, NCT, 128).transpose(2, 1, 0, 3))
        xtc = np.ascontiguousarray(xts[b][:, g * TOK:(g + 1) * TOK])
        p = np.arange(128)[:, None]
        blk = np.arange(32)[None, :]
        s = blk % 8
        r = blk // 8
        gidx = ((2 * g + s // 4) * 2048 + r * 512 + (s % 4) * 128 + p).astype(np.int32)
        in_maps.append({
            "xt": xts[b], "xtc": xtc, "wqkv": wqkv_arr, "wot": wo_arr,
            "awr": awr_arr, "btil": btil_arr, "masks": masks,
            "identb": identb, "identf": identf, "pmat": pmat, "gidx": gidx,
        })
    return in_maps


def kernel(**inputs):
    in_maps = _host_prep(**inputs)
    if "nc" not in _CACHE:
        _CACHE["nc"] = build_program(DEBUG_PHASE)
    nc = _CACHE["nc"]
    res = run_bass_kernel_spmd(nc, in_maps, core_ids=list(range(8)))
    out = np.empty((2, SEQ, DM), np.float32)
    for core in range(8):
        b, g = core // 4, core % 4
        out[b, g * TOK:(g + 1) * TOK, :] = np.asarray(res.results[core]["yt"]).T
    return out
